# version 01
# baseline (speedup 1.0000x reference)
import numpy as np
import concourse.bass as bass
import concourse.mybir as mybir
from concourse.bass_utils import run_bass_kernel_spmd

F32 = mybir.dt.float32
BF16 = mybir.dt.bfloat16
AF = mybir.ActivationFunctionType
ALU = mybir.AluOpType
AX = mybir.AxisListType

NCORE = 8
B = 4
S = 8192
DM = 1024
COLS = 4096
NCH = DM // 128
CT = COLS // 512
EPS = 1e-5
ATT_SCALE = float(512 ** -0.5)

TAP = None
TAPD = None
DBG_DUMP = False
STRICT = False


def core_blocks(c):
    return (c, 15 - c)


ENGINES = ("tensor", "vector", "scalar", "sync", "gpsimd")
NLANE = 8
SEMS = {
    "T": ("tensor", 1), "V": ("vector", 1), "SC": ("scalar", 1),
    "G": ("gpsimd", 1), "CC": ("gpsimd", 1),
}
DMA_PREFIX = {"sync": "DY", "gpsimd": "DG", "scalar": "DA"}
for _q, _p in DMA_PREFIX.items():
    for _i in range(NLANE):
        SEMS[f"{_p}{_i}"] = (_q, 16)
ENG_COMPUTE_SEM = {"tensor": "T", "vector": "V", "scalar": "SC", "gpsimd": "G"}


class R:
    __slots__ = ("w", "r", "sbuf")

    def __init__(self, sbuf=False):
        self.w = {}
        self.r = {}
        self.sbuf = sbuf

    def snap(self):
        return (dict(self.w), dict(self.r))

    def restore(self, s):
        self.w, self.r = dict(s[0]), dict(s[1])


class TV:
    __slots__ = ("ap", "res")

    def __init__(self, ap, res):
        self.ap = ap
        self.res = res

    def __getitem__(self, sl):
        return TV(self.ap[sl], self.res)

    def rearrange(self, pat, **kw):
        return TV(self.ap.rearrange(pat, **kw), self.res)


class Ctx:
    def __init__(self):
        self.dry = True
        self.cur = None
        self.eng = None
        self.counts = {k: 0 for k in SEMS}
        self.handles = {}
        self.pid = None
        self.wm = {e: {s: 0 for s in SEMS} for e in ENGINES}
        self.registry = []
        self.labels = {}
        self.dma_idx = {q: 0 for q in DMA_PREFIX}

    def newres(self, sbuf=False):
        r = R(sbuf)
        self.registry.append(r)
        return r

    def _wait(self, engine, sem, val):
        if val <= 0:
            return
        if not STRICT and engine == "tensor" and sem == "T":
            return
        if self.wm[engine][sem] >= val:
            return
        self.wm[engine][sem] = val
        if not self.dry and self.cur == engine:
            self.eng.wait_ge(self.handles[sem], val)

    def _wait_map(self, engine, m):
        for sem, val in m.items():
            self._wait(engine, sem, val)

    def _pre(self, engine, reads, writes, skip_pfx=None):
        def wm_of(res, m):
            if skip_pfx is not None and res.sbuf:
                return {k: v for k, v in m.items() if not k.startswith(skip_pfx)}
            return m
        for tvs in reads:
            if tvs is not None:
                self._wait_map(engine, wm_of(tvs.res, tvs.res.w))
        for tvs in writes:
            if tvs is not None:
                self._wait_map(engine, wm_of(tvs.res, tvs.res.w))
                self._wait_map(engine, wm_of(tvs.res, tvs.res.r))

    def _post(self, sem, reads, writes, label=None):
        inc = SEMS[sem][1]
        self.counts[sem] += inc
        v = self.counts[sem]
        if label is not None:
            if self.dry:
                if label in self.labels and self.labels[label] != (sem, v):
                    raise RuntimeError(f"label {label} inconsistent")
                self.labels[label] = (sem, v)
        for tvs in reads:
            if tvs is not None:
                if tvs.res.r.get(sem, 0) < v:
                    tvs.res.r[sem] = v
        for tvs in writes:
            if tvs is not None:
                if tvs.res.w.get(sem, 0) < v:
                    tvs.res.w[sem] = v
        return v

    def op(self, engine, fn, outs, ins, *a, label=None, dma_sem=None, **kw):
        sem = dma_sem if dma_sem else ENG_COMPUTE_SEM[engine]
        self._pre(engine, ins, outs,
                  skip_pfx=(DMA_PREFIX[engine] if dma_sem else None))
        if not self.dry and self.cur == engine:
            args = [x.ap if isinstance(x, TV) else x for x in a]
            inst = getattr(self.eng, fn)(*args, **{
                k: (v.ap if isinstance(v, TV) else v) for k, v in kw.items()})
            inst.then_inc(self.handles[sem], SEMS[sem][1])
        self._post(sem, ins, outs, label)

    def wait_label(self, engine, name):
        if self.dry:
            assert name in self.labels, f"undefined label {name}"
            return
        sem, val = self.labels[name]
        self._wait(engine, sem, val)

    def mm_group(self, out, ops, label=None, already_open=False):
        n = len(ops)
        self._pre("tensor", [x for p in ops for x in p], [out])
        for i, (l_, r_) in enumerate(ops):
            if not self.dry and self.cur == "tensor":
                inst = self.eng.matmul(out.ap, l_.ap, r_.ap,
                                       start=(i == 0 and not already_open),
                                       stop=(i == n - 1))
                if i == n - 1:
                    inst.then_inc(self.handles["T"], 1)
        self._post("T", [x for p in ops for x in p], [out], label)

    def mm_acc(self, out, lhsT, rhs, start, stop, label=None):
        self._pre("tensor", [lhsT, rhs], [out])
        if not self.dry and self.cur == "tensor":
            inst = self.eng.matmul(out.ap, lhsT.ap, rhs.ap, start=start, stop=stop)
            inst.then_inc(self.handles["T"], 1)
        self._post("T", [lhsT, rhs], [out], label)

    def tr(self, out, in_, ident, label=None):
        self.op("tensor", "transpose", [out], [in_, ident], out, in_, ident,
                label=label)

    def act(self, out, in_, func, bias=0.0, scale=1.0, label=None):
        br = [bias] if isinstance(bias, TV) else []
        sr = [scale] if isinstance(scale, TV) else []
        self.op("scalar", "activation", [out], [in_] + br + sr, out, in_, func,
                bias=bias, scale=scale, label=label)

    def s_copy(self, out, in_, label=None):
        self.op("scalar", "copy", [out], [in_], out, in_, label=label)

    def vcopy(self, out, in_, label=None):
        self.op("vector", "tensor_copy", [out], [in_], out, in_, label=label)

    def vtt(self, fn, out, a_, b_, label=None):
        self.op("vector", fn, [out], [a_, b_], out, a_, b_, label=label)

    def vts(self, fn, out, a_, s_, label=None):
        ins = [a_] + ([s_] if isinstance(s_, TV) else [])
        self.op("vector", fn, [out], ins, out, a_, s_, label=label)

    def vstt(self, out, in0, scal, in1, op0, op1, label=None):
        ins = [in0, in1] + ([scal] if isinstance(scal, TV) else [])
        self.op("vector", "scalar_tensor_tensor", [out], ins,
                out, in0, scal, in1, op0, op1, label=label)

    def vreduce(self, fn, out, in_, label=None):
        self.op("vector", fn, [out], [in_], out, in_, axis=AX.X, label=label)

    def vrecip(self, out, in_, label=None):
        self.op("vector", "reciprocal", [out], [in_], out, in_, label=label)

    def vmemset(self, out, val, label=None):
        self.op("vector", "memset", [out], [], out, val, label=label)

    def gmemset(self, out, val, label=None):
        self.op("gpsimd", "memset", [out], [], out, val, label=label)

    def gaffsel(self, out, in_, pattern, cmp, fill, base, label=None, chm=1):
        self.op("gpsimd", "affine_select", [out], [in_], label=label,
                out=out, in_=in_, pattern=pattern, compare_op=cmp, fill=fill,
                base=base, channel_multiplier=chm)

    def dma(self, q, out, in_, label=None, accum=False, slow=False):
        kw = {}
        if accum:
            kw["accum_op"] = ALU.add
        if slow:
            kw["allow_slow_non_contiguous"] = True
        qi = self.dma_idx[q]
        self.dma_idx[q] = qi + 1
        lane = f"{DMA_PREFIX[q]}{qi % NLANE}"
        if qi >= NLANE:
            self._wait(q, lane, 16 * (qi // NLANE))
        outs = [out]
        ins = [in_] + ([out] if accum else [])
        self.op(q, "dma_start", outs, ins, label=label, dma_sem=lane,
                out=out, in_=in_, **kw)

    def cc(self, kind, in_tv, out_tv, label=None):
        self._pre("gpsimd", [in_tv], [out_tv])
        if not self.dry and self.cur == "gpsimd":
            inst = self.eng.collective_compute(
                kind, ALU.bypass, replica_groups=[list(range(NCORE))],
                ins=[in_tv.ap.opt()], outs=[out_tv.ap.opt()])
            inst.then_inc(self.handles["CC"], 1)
        self._post("CC", [in_tv], [out_tv], label)

    def fence(self):
        for e in ENGINES:
            for s_ in SEMS:
                self._wait(e, s_, self.counts[s_])

    def snapshot(self):
        return (dict(self.counts),
                [r.snap() for r in self.registry],
                {e: dict(m) for e, m in self.wm.items()},
                dict(self.dma_idx))

    def restore(self, s):
        self.counts = dict(s[0])
        for r, rs in zip(self.registry, s[1]):
            r.restore(rs)
        self.wm = {e: dict(m) for e, m in s[2].items()}
        self.dma_idx = dict(s[3])

    def core_switch(self, tag, body_fn):
        nreg = len(self.registry)
        entry = self.snapshot()
        if self.dry:
            exit_s = None
            d0 = None
            for i in range(NCORE):
                self.restore(entry)
                body_fn(i)
                if len(self.registry) != nreg:
                    raise RuntimeError(f"switch {tag}: body allocated resources")
                d = {k: self.counts[k] - entry[0][k] for k in SEMS}
                if d0 is None:
                    d0, exit_s = d, self.snapshot()
                elif d != d0:
                    diff = {k: (d0[k], d[k]) for k in SEMS if d[k] != d0[k]}
                    raise RuntimeError(f"switch {tag} body {i}: delta {diff}")
            self.restore(exit_s)
        else:
            if self.pid is None:
                self.pid = self.eng.partition_id()
            exit_s = None
            first = True
            for i in self.eng.Switch(self.pid, n=NCORE):
                self.restore(entry)
                body_fn(i)
                if first:
                    exit_s = self.snapshot()
                    first = False
            self.restore(exit_s)
        final = dict(self.counts)
        for r_, es in zip(self.registry, entry[1]):
            if r_.snap() != es:
                r_.w = {s_: final[s_] for s_ in SEMS if final[s_] > 0}
                r_.r = {}


def build_program():
    nc = bass.Bass("TRN2", target_bir_lowering=False, debug=False,
                   detect_race_conditions=STRICT)
    ctx = Ctx()

    d = {}

    def ein(name, shape):
        d[name] = TV(nc.dram_tensor(name, list(shape), F32,
                                    kind="ExternalInput").ap(), ctx.newres())

    def din(name, shape, dt=F32):
        d[name] = TV(nc.dram_tensor(name, list(shape), dt).ap(), ctx.newres())

    ein("x0", (DM, COLS))
    ein("wqkvT", (DM, 1536))
    ein("woutT", (512, DM))
    for n in ("b_out", "bfv", "biv", "bcv", "bov", "incv"):
        ein(n, (DM,))
    for n in ("wqT", "wkT", "wvT", "woT",
              "wfT", "ufT", "wiT", "uiT", "wcT", "ucT", "wobT", "uoT"):
        ein(n, (DM, DM))
    ein("normw", (128,))
    d["y"] = TV(nc.dram_tensor("y", [DM, COLS], F32, kind="ExternalOutput").ap(),
                ctx.newres())

    din("x1", (DM, COLS))
    din("x2", (DM, COLS))
    din("x3", (DM, COLS))
    din("ha", (DM, COLS), BF16)
    din("hb", (DM, COLS), BF16)
    din("ka_pub", (513, COLS), BF16)
    din("va_pub", (COLS, 512), BF16)
    din("ka_gat", (NCORE * 513, COLS), BF16)
    din("va_gat", (NCORE * COLS, 512), BF16)
    din("qb", (DM, COLS), BF16)
    din("kb", (DM, COLS), BF16)
    din("kbt", (COLS, DM), BF16)
    din("vbt", (COLS, DM), BF16)
    din("bs_pub", (2 * B * 8 * 128, 129))
    din("bs_gat", (NCORE * 2 * B * 8 * 128, 129))
    din("wstage", (12 * 8 * 128, DM), BF16)

    mgrs = []
    t = {}

    def sb(name, shape, dt=F32, grp=None):
        g = nc.sbuf_tensor(name, list(shape), dt)
        h = g.__enter__()
        if grp is None:
            mgrs.append(g)
        else:
            grp.append(g)
        t[name] = TV(h[:], ctx.newres(sbuf=True))
        return t[name]

    def free_grp(grp):
        for g in reversed(grp):
            g.__exit__(None, None, None)

    sb("ident", (128, 128), BF16)
    sb("tmask", (128, 896), BF16)
    sb("tmask32", (128, 128))
    sb("ones_b", (128, 1), BF16)
    sb("ones_r32", (1, 128))
    sb("bout", (128, NCH))
    sb("bfs", (128, NCH))
    sb("bis", (128, NCH))
    sb("bcs", (128, NCH))
    sb("bos", (128, NCH))
    sb("incs", (128, NCH))
    sb("normw_s", (128, 1))
    sb("rows", (1, 2048))
    sb("xold", (128, 2 * COLS))
    sb("xnew", (128, COLS))
    sb("tscr", (128, 512))
    sb("hprod", (128, 2 * 512), BF16)
    sb("sq", (128, 2 * 512), BF16)
    sb("xbf", (128, 2 * 512), BF16)
    sb("htile", (128, 2 * COLS), BF16)
    sb("wring", (128, 10240), BF16)
    ga = []
    sb("q_a", (128, 4 * COLS), BF16, ga)
    sb("kblk", (128, 2 * 2048), BF16, ga)
    sb("vtile", (128, 3 * 512), BF16, ga)
    sb("etile", (128, 2 * 512), BF16, ga)
    sb("kn", (1, 2 * 1024), BF16, ga)
    sb("qmaxg", (1, CT), F32, ga)
    sb("kmaxb", (1, B * 16), F32, ga)
    sb("negb", (128, 2), F32, ga)
    sb("rscol", (128, 4), F32, ga)
    sb("ao_tm", (128, 4 * 512), BF16, ga)
    sb("ao_fm", (128, 2 * 4 * 512), BF16, ga)
    free_grp(ga)
    gf = []
    sb("c8", (128, NCH * 1024), F32, gf)
    sb("cbf", (128, NCH * 1024), BF16, gf)
    sb("htf", (128, 2 * NCH * 1024), BF16, gf)
    sb("gates", (128, 6 * 512), F32, gf)
    free_grp(gf)
    gb = []
    sb("sT", (128, 8 * 8 * 128), BF16, gb)
    sb("kcs", (128, 256), F32, gb)
    sb("kpfx", (128, 64), F32, gb)
    sb("bsring", (128, 5 * 129), F32, gb)
    sb("pfxm", (128, 2 * 129), F32, gb)
    sb("zcol", (128, 4), BF16, gb)
    sb("zrun", (128, 1), F32, gb)
    sb("recd", (1, 8 * 1024), BF16, gb)
    sb("stq", (128, 2 * 1024), BF16, gb)
    sb("stk", (128, 2 * 1024), BF16, gb)
    sb("stkt", (128, 2 * 1024), BF16, gb)
    sb("stvt", (128, 2 * 1024), BF16, gb)
    sb("sop", (128, 2 * 128), BF16, gb)
    sb("stmf", (128, 2 * 128), F32, gb)
    sb("osb", (128, 2 * 1024), F32, gb)
    sb("o2", (128, 2 * 128), BF16, gb)
    sb("onorm", (128, 8 * 1024), BF16, gb)
    sb("arow", (1, 2048), F32, gb)
    gp = []
    sb("ptile", (128, 8 * 1024), BF16, gp)
    free_grp(gp)
    gs = []
    sb("spfx", (128, 64 * 128), BF16, gs)
    free_grp(gs)
    free_grp(gb)

    ps = []
    for i in range(7):
        g = nc.psum_tensor(f"ps{i}", [128, 512], F32)
        h = g.__enter__()
        mgrs.append(g)
        ps.append(TV(h[:], ctx.newres(sbuf=True)))
    g = nc.psum_tensor("psbf", [128, 1024], BF16)
    h = g.__enter__()
    mgrs.append(g)
    psbf = TV(h[:], ctx.newres(sbuf=True))

    for s_ in SEMS:
        g = nc.semaphore(s_)
        ctx.handles[s_] = g.__enter__()
        mgrs.append(g)

    def seg(name, a, b_):
        return t[name][:, a:b_]

    def ldtile(q, dst, src2d, c=8):
        k = src2d.ap.shape[1]
        ctx.dma(q, dst.rearrange("p (c k) -> p c k", c=c),
                src2d.rearrange("(c p) k -> p c k", p=128))

    def sttile(q, src, dst2d, c=8):
        ctx.dma(q, dst2d.rearrange("(c p) k -> p c k", p=128),
                src.rearrange("p (c k) -> p c k", c=c))

    def wslot(off, width):
        return t["wring"][:, off:off + width]

    def load_wcast(q, wt, ocol0, width, off, label=None):
        src = wt[:, ocol0:ocol0 + width].rearrange("(c p) o -> p c o", p=128)
        dst = wslot(off, NCH * width).rearrange("p (c o) -> p c o", c=NCH)
        ctx.dma(q, dst, src, label=label)

    def wst_fm(m, oc):
        return d["wstage"][(m * 8 + oc) * 128:(m * 8 + oc + 1) * 128, :]

    def ln_stats_chunk(xn_tv, dc, uid):
        xb = seg("xbf", (dc % 2) * 512, (dc % 2) * 512 + 512)
        sqs = seg("sq", (dc % 2) * 512, (dc % 2) * 512 + 512)
        ctx.vcopy(xb, xn_tv)
        ctx.act(sqs, xb, AF.Square)
        ctx.mm_acc(ps[2][0:1, :], t["ones_b"], xb, start=(dc == 0),
                   stop=(dc == NCH - 1))
        ctx.mm_acc(ps[3][0:1, :], t["ones_b"], sqs, start=(dc == 0),
                   stop=(dc == NCH - 1))

    def ln_finalize():
        mu = t["rows"][0:1, 0:512]
        m2 = t["rows"][0:1, 512:1024]
        rstd = t["rows"][0:1, 1024:1536]
        ctx.vts("tensor_scalar_mul", mu, ps[2][0:1, 0:512], 1.0 / DM)
        ctx.vts("tensor_scalar_mul", m2, ps[3][0:1, 0:512], 1.0 / DM)
        ctx.vtt("tensor_mul", rstd, mu, mu)
        ctx.vtt("tensor_sub", rstd, m2, rstd)
        ctx.vts("tensor_scalar_add", rstd, rstd, EPS)
        ctx.act(rstd, rstd, AF.Ln)
        ctx.act(rstd, rstd, AF.Exp, scale=-0.5)
        ctx.mm_group(ps[4], [(t["ones_r32"], mu)])
        ctx.mm_group(ps[5], [(t["ones_r32"], rstd)])

    def ln_apply_chunk(x_tv, h_tv):
        ctx.vtt("tensor_sub", t["tscr"], x_tv, ps[4])
        ctx.vtt("tensor_mul", h_tv, t["tscr"], ps[5])

    epi_seq = [0]

    def emit_epilogue_tile(colbase, y_provider, x_old, x_new_dram, h_dram):
        eid = epi_seq[0]
        epi_seq[0] += 1
        xo = t["xold"][:, (eid % 2) * COLS:(eid % 2) * COLS + COLS]
        ldtile("sync", xo, x_old[:, colbase:colbase + 512])
        for dc in range(NCH):
            ysrc, bias = y_provider(dc)
            xn = t["xnew"][:, dc * 512:(dc + 1) * 512]
            if bias is not None:
                ctx.vstt(xn, ysrc, bias, xo[:, dc * 512:(dc + 1) * 512],
                         ALU.add, ALU.add)
            else:
                ctx.vtt("tensor_add", xn, ysrc, xo[:, dc * 512:(dc + 1) * 512])
            ctx.dma("scalar", x_new_dram[dc * 128:(dc + 1) * 128,
                                         colbase:colbase + 512], xn)
            if h_dram is not None:
                ln_stats_chunk(xn, dc, eid)
        if h_dram is None:
            return
        ln_finalize()
        for dc in range(NCH):
            hp = t["hprod"][:, (dc % 2) * 512:(dc % 2) * 512 + 512]
            xn = t["xnew"][:, dc * 512:(dc + 1) * 512]
            ln_apply_chunk(xn, hp)
            ctx.dma("scalar", h_dram[dc * 128:(dc + 1) * 128,
                                     colbase:colbase + 512], hp)

    def emit_consts():
        ctx.gmemset(t["ones_b"], 1.0)
        ctx.gmemset(t["ones_r32"], 1.0)
        ctx.gmemset(t["tmask"], 1.0)
        ctx.gmemset(t["tmask32"], 1.0)
        ctx.gmemset(t["ident"], 1.0)
        ctx.gaffsel(t["ident"], t["ident"], [[-1, 128]], ALU.is_equal, 0.0, 0)
        ctx.gaffsel(t["tmask"], t["tmask"], [[1, 896]], ALU.is_ge, 0.0, -384,
                    chm=-1)
        ctx.gaffsel(t["tmask32"], t["tmask32"], [[1, 128]], ALU.is_ge, 0.0, 0,
                    chm=-1)
        for name, dst in (("b_out", "bout"), ("bfv", "bfs"), ("biv", "bis"),
                          ("bcv", "bcs"), ("bov", "bos"), ("incv", "incs")):
            ctx.dma("sync", t[dst], d[name].rearrange("(c p) -> p c", p=128),
                    slow=True)
        ctx.dma("sync", t["normw_s"], d["normw"][:, None])
        for m, w in enumerate(("wfT", "ufT", "wiT", "uiT", "wcT", "ucT",
                               "wobT", "uoT", "wqT", "wkT", "wvT", "woT")):
            for oc in range(8):
                src = d[w][:, oc * 128:(oc + 1) * 128].rearrange(
                    "(dc p) o -> p dc o", p=128)
                dst = wst_fm(m, oc).rearrange("p (dc o) -> p dc o", dc=8)
                ctx.dma("gpsimd", dst, src)

    def emit_attn_a():
        for ct in range(CT):
            b = ct // 2
            colbase = ct * 512
            xo = t["xold"][:, (ct % 2) * COLS:(ct % 2) * COLS + COLS]
            ldtile("sync", xo, d["x0"][:, colbase:colbase + 512])
            for dc in range(NCH):
                ln_stats_chunk(xo[:, dc * 512:(dc + 1) * 512], dc, ct)
            ln_finalize()
            ht = t["htile"][:, (ct % 2) * COLS:(ct % 2) * COLS + COLS]
            for dc in range(NCH):
                ln_apply_chunk(xo[:, dc * 512:(dc + 1) * 512],
                               ht[:, dc * 512:(dc + 1) * 512])
            for oc in range(8):
                off = (oc % 2) * 1024
                load_wcast("gpsimd", d["wqkvT"], oc * 128, 128, off)
                bank = ps[oc % 2]
                ctx.mm_group(bank, [
                    (wslot(off + dc * 128, 128), ht[:, dc * 512:(dc + 1) * 512])
                    for dc in range(NCH)])
                if oc < 4:
                    qd = t["q_a"][:, oc * COLS + colbase: oc * COLS + colbase + 512]
                    ctx.vcopy(qd, bank)
                    sqs = seg("sq", (oc % 2) * 512, (oc % 2) * 512 + 512)
                    ctx.act(sqs, qd, AF.Square)
                    ctx.mm_acc(ps[6][0:1, :], t["ones_b"], sqs, start=(oc == 0),
                               stop=(oc == 3))
                    if oc == 3:
                        ctx.vreduce("reduce_max",
                                    t["qmaxg"][0:1, ct:ct + 1], ps[6][0:1, :])
                else:
                    ko = oc - 4
                    kp = t["hprod"][:, (oc % 2) * 512:(oc % 2) * 512 + 512]
                    ctx.vcopy(kp, bank)
                    ctx.dma("scalar", d["ka_pub"][ko * 128:(ko + 1) * 128,
                                                  colbase:colbase + 512], kp)
                    sqs = seg("sq", (oc % 2) * 512, (oc % 2) * 512 + 512)
                    ctx.act(sqs, kp, AF.Square)
                    ctx.mm_acc(ps[6][0:1, :], t["ones_b"], sqs, start=(ko == 0),
                               stop=(ko == 3))
                    if ko == 3:
                        kr = t["rows"][0:1, 1536:2048]
                        ctx.vcopy(kr, ps[6][0:1, :])
                        ctx.s_copy(t["kn"][0:1, 0:512], kr)
                        ctx.dma("scalar", d["ka_pub"][512:513, colbase:colbase + 512],
                                t["kn"][0:1, 0:512])
            load_wcast("gpsimd", d["wqkvT"], 1024, 256, 2048)
            load_wcast("gpsimd", d["wqkvT"], 1280, 256, 4096)
            for pc in range(4):
                bank = ps[pc % 2]
                for half in range(2):
                    ctx.mm_group(bank[:, half * 256:(half + 1) * 256], [
                        (ht[:, dc * 512 + pc * 128: dc * 512 + (pc + 1) * 128],
                         wslot(2048 + half * 2048 + dc * 256, 256))
                        for dc in range(NCH)])
                vp = t["hprod"][:, (pc % 2) * 512:(pc % 2) * 512 + 512]
                ctx.vcopy(vp, bank)
                ctx.dma("scalar", d["va_pub"][colbase + pc * 128:
                                              colbase + (pc + 1) * 128, :], vp)
        ctx.cc("AllGather", d["ka_pub"], d["ka_gat"], label="ag_k")
        ctx.cc("AllGather", d["va_pub"], d["va_gat"], label="ag_v")
        for b in range(B):
            for r_ in range(NCORE):
                sl = t["kn"][0:1, (r_ % 2) * 1024:(r_ % 2) * 1024 + 1024]
                ctx.dma("sync", sl,
                        d["ka_gat"][r_ * 513 + 512:r_ * 513 + 513,
                                    b * 1024:(b + 1) * 1024])
                for half in range(2):
                    j = r_ if half == 0 else 15 - r_
                    ctx.vreduce("reduce_max",
                                t["kmaxb"][0:1, b * 16 + j:b * 16 + j + 1],
                                sl[0:1, half * 512:(half + 1) * 512])
        if TAP == "dbg":
            ctx.vmemset(t["xnew"], 0.0)
        for b in range(B):
            ctx.core_switch(f"attn_a_b{b}",
                            lambda cid, b=b: attn_a_body(cid, b))
            if TAP == "dbg":
                return
            for g in range(2):
                emit_attn_a_epilogue(b, g)

    def attn_a_body(cid, b):
        for g, bl in enumerate(core_blocks(cid)):
            attn_a_group(cid, b, g, bl)

    def attn_a_group(cid, b, g, bl):
        colbase = b * 1024 + g * 512
        nkc = (bl + 1) * 4
        qmax = t["rows"][0:1, 0:1]
        kmax = t["rows"][0:1, 1:2]
        ctx.vreduce("reduce_max", kmax, t["kmaxb"][0:1, b * 16:b * 16 + bl + 1])
        ctx.vtt("tensor_mul", qmax, t["qmaxg"][0:1, 2 * b + g:2 * b + g + 1],
                kmax)
        ctx.act(qmax, qmax, AF.Ln)
        ctx.act(qmax, qmax, AF.Exp, scale=0.5)
        ctx.op("scalar", "mul", [qmax], [qmax], qmax, qmax, -ATT_SCALE)
        ctx.mm_group(ps[2][:, 0:1], [(t["ones_r32"], qmax)])
        nb = t["negb"][:, g:g + 1]
        ctx.vcopy(nb, ps[2][:, 0:1])

        def kblk_slot(j):
            return t["kblk"][:, (j % 2) * 2048:(j % 2) * 2048 + 2048]

        def kblk_load(j):
            r_ = j if j < 8 else 15 - j
            off = b * 1024 + (0 if j < 8 else 512)
            ldtile("sync", kblk_slot(j),
                   d["ka_gat"][r_ * 513:r_ * 513 + 512, off:off + 512], c=4)

        def vt_slot(i):
            return t["vtile"][:, (i % 3) * 512:(i % 3) * 512 + 512]

        def vt_load(kc):
            j = kc // 4
            r_ = j if j < 8 else 15 - j
            row = r_ * COLS + b * 1024 + (0 if j < 8 else 512) + (kc % 4) * 128
            ctx.dma("gpsimd", vt_slot(kc), d["va_gat"][row:row + 128, :])

        kblk_load(0)
        if nkc > 4:
            kblk_load(1)
        for i in range(min(3, nkc)):
            vt_load(i)

        for kc in range(nkc):
            j = kc // 4
            sbank = ps[kc % 2]
            et = t["etile"][:, (kc % 2) * 512:(kc % 2) * 512 + 512]
            ctx.mm_group(sbank, [
                (kblk_slot(j)[:, dc * 512 + (kc % 4) * 128:
                              dc * 512 + (kc % 4) * 128 + 128],
                 t["q_a"][:, dc * COLS + colbase: dc * COLS + colbase + 512])
                for dc in range(4)])
            ctx.act(et, sbank, AF.Exp, bias=nb, scale=ATT_SCALE)
            if kc >= nkc - 4:
                o_ = (kc - (nkc - 4)) * 128
                msl = t["tmask"][:, 384 - o_:384 - o_ + 512]
                ctx.vtt("tensor_mul", et, et, msl)
            if TAP == "dbg" and DBG_DUMP and b == 0 and kc == 0:
                xn = t["xnew"]
                ctx.vcopy(xn[:, g * 2048 + 0:g * 2048 + 512], sbank)
                ctx.vcopy(xn[:, g * 2048 + 512:g * 2048 + 1024], et)
                ctx.vcopy(xn[:, g * 2048 + 1024:g * 2048 + 1152],
                          kblk_slot(0)[:, 0:128])
                ctx.vcopy(xn[:, g * 2048 + 1152:g * 2048 + 1280],
                          t["q_a"][:, colbase:colbase + 128])
                ctx.vcopy(xn[:, g * 2048 + 1280:g * 2048 + 1281], nb)
                ctx.vcopy(xn[0:1, g * 2048 + 1300:g * 2048 + 1308],
                          t["qmaxg"][0:1, :])
                ctx.vcopy(xn[0:1, g * 2048 + 1308:g * 2048 + 1372],
                          t["kmaxb"][0:1, :])
                ctx.vcopy(xn[:, g * 2048 + 1408:g * 2048 + 1920],
                          vt_slot(0))
            ctx.mm_acc(ps[2][0:1, :], t["ones_b"], et, start=(kc == 0),
                       stop=(kc == nkc - 1))
            for qt in range(4):
                ctx.mm_acc(ps[3 + qt], et[:, qt * 128:(qt + 1) * 128],
                           vt_slot(kc), start=(kc == 0), stop=(kc == nkc - 1))
            if kc % 4 == 3 and j + 2 <= bl:
                kblk_load(j + 2)
            if kc + 3 < nkc:
                vt_load(kc + 3)
        se = t["rows"][0:1, 0:512]
        rs = t["rows"][0:1, 512:1024]
        ctx.vcopy(se, ps[2][0:1, :])
        ctx.vts("tensor_scalar_add", se, se, 1e-30)
        ctx.vrecip(rs, se)
        for qt in range(4):
            ctx.tr(ps[2][:, qt:qt + 1], rs[0:1, qt * 128:(qt + 1) * 128],
                   t["ones_r32"][0:1, 0:1])
            ctx.vcopy(t["rscol"][:, qt:qt + 1], ps[2][:, qt:qt + 1])
            ctx.vts("tensor_scalar_mul", t["ao_tm"][:, qt * 512:(qt + 1) * 512],
                    ps[3 + qt], t["rscol"][:, qt:qt + 1])
        for qt in range(4):
            for vc in range(4):
                pslot = psbf[:, (qt % 2) * 512 + vc * 128:
                             (qt % 2) * 512 + (vc + 1) * 128]
                ctx.tr(pslot, t["ao_tm"][:, qt * 512 + vc * 128:
                                         qt * 512 + (vc + 1) * 128], t["ident"])
                ctx.vcopy(t["ao_fm"][:, g * 2048 + vc * 512 + qt * 128:
                                     g * 2048 + vc * 512 + (qt + 1) * 128], pslot)

    def emit_attn_a_epilogue(b, g):
        colbase = b * 1024 + g * 512

        def y_provider(dc):
            off = 6144 + (dc % 2) * 512
            src = d["woutT"][:, dc * 128:(dc + 1) * 128].rearrange(
                "(c p) o -> p c o", p=128)
            ctx.dma("gpsimd", wslot(off, 512).rearrange("p (c o) -> p c o", c=4),
                    src)
            bank = ps[dc % 2]
            ctx.mm_group(bank, [
                (wslot(off + vc * 128, 128),
                 t["ao_fm"][:, g * 2048 + vc * 512:g * 2048 + (vc + 1) * 512])
                for vc in range(4)])
            return bank, t["bout"][:, dc:dc + 1]

        emit_epilogue_tile(colbase, y_provider, d["x0"], d["x1"],
                           d["ha"] if TAP != "x1" else None)

    def emit_ffn(call):
        ctx.fence()
        x_in = d["x1"] if call == 0 else d["x3"]
        x_out = d["x2"] if call == 0 else d["y"]
        h_in = d["ha"]
        h_out = d["hb"] if (call == 0 and TAP != "x2") else None
        cc8 = [t["c8"][:, i * 1024:(i + 1) * 1024] for i in range(NCH)]
        cbf = [t["cbf"][:, i * 1024:(i + 1) * 1024] for i in range(NCH)]

        def htslot(i):
            return t["htf"][:, (i % 2) * 8192:(i % 2) * 8192 + 8192]

        def ht_load(i, tstep):
            ldtile("sync", htslot(i), h_in[:, tstep * 1024:(tstep + 1) * 1024])

        for dc in range(NCH):
            ctx.vmemset(cc8[dc], 0.0)
            ctx.op("scalar", "add", [cc8[dc]], [cc8[dc], t["incs"]],
                   cc8[dc], cc8[dc], t["incs"][:, dc:dc + 1])
        ht_load(0, 0)
        GW = {0: ("bfs", AF.Sigmoid, 0), 1: ("bis", AF.Sigmoid, 2),
              2: ("bcs", AF.Tanh, 4)}
        for tstep in range(4):
            if tstep < 3:
                ht_load(tstep + 1, tstep + 1)
            ht = htslot(tstep)
            for dc in range(NCH):
                ctx.vcopy(cbf[dc], cc8[dc])
            for oc in range(NCH):
                for g3 in range(3):
                    ctx.dma("sync", wslot(g3 * 2048, 1024), wst_fm(2 * g3, oc))
                    ctx.dma("sync", wslot(g3 * 2048 + 1024, 1024),
                            wst_fm(2 * g3 + 1, oc))
                for g3 in range(3):
                    bname, afn, bk = GW[g3]
                    for ph in range(2):
                        bank = ps[bk + ph]
                        ctx.mm_group(bank, [
                            (wslot(g3 * 2048 + dc * 128, 128),
                             ht[:, dc * 1024 + ph * 512: dc * 1024 + ph * 512 + 512])
                            for dc in range(NCH)] + [
                            (wslot(g3 * 2048 + 1024 + dc * 128, 128),
                             cbf[dc][:, ph * 512:(ph + 1) * 512])
                            for dc in range(NCH)])
                        gt = t["gates"][:, (g3 * 2 + ph) * 512:
                                        (g3 * 2 + ph + 1) * 512]
                        ctx.act(gt, bank, afn, bias=t[bname][:, oc:oc + 1])
                for ph in range(2):
                    fg = t["gates"][:, ph * 512:(ph + 1) * 512]
                    ig = t["gates"][:, (2 + ph) * 512:(3 + ph) * 512]
                    gg = t["gates"][:, (4 + ph) * 512:(5 + ph) * 512]
                    cs = cc8[oc][:, ph * 512:(ph + 1) * 512]
                    ctx.vtt("tensor_mul", t["tscr"], ig, gg)
                    ctx.vtt("tensor_mul", cs, fg, cs)
                    ctx.vtt("tensor_add", cs, cs, t["tscr"])
        for dc in range(NCH):
            ctx.vcopy(cbf[dc], cc8[dc])
        for oc in range(NCH):
            ctx.dma("sync", wslot(0, 1024), wst_fm(7, oc))
            for ph in range(2):
                bank = ps[oc % 2]
                ctx.mm_group(bank, [
                    (wslot(dc * 128, 128), cbf[dc][:, ph * 512:(ph + 1) * 512])
                    for dc in range(NCH)])
                ctx.vcopy(cc8[oc][:, ph * 512:(ph + 1) * 512], bank)
        for tstep in range(4):
            ht_load(tstep, tstep)
            ht = htslot(tstep)
            for ph in range(2):
                def y_provider(dc, tstep=tstep, ph=ph, ht=ht):
                    ctx.dma("sync", wslot(2048 + (dc % 2) * 1024, 1024),
                            wst_fm(6, dc))
                    bank = ps[dc % 2]
                    ctx.mm_group(bank, [
                        (wslot(2048 + (dc % 2) * 1024 + k_ * 128, 128),
                         ht[:, k_ * 1024 + ph * 512: k_ * 1024 + ph * 512 + 512])
                        for k_ in range(NCH)])
                    ys = t["gates"][:, (dc % 2) * 512:(dc % 2) * 512 + 512]
                    ctx.vstt(ys, bank, t["bos"][:, dc:dc + 1],
                             cc8[dc][:, ph * 512:(ph + 1) * 512], ALU.add, ALU.add)
                    ctx.act(ys, ys, AF.Tanh)
                    return ys, None
                emit_epilogue_tile(tstep * 1024 + ph * 512, y_provider,
                                   x_in, x_out, h_out)

    def emit_attn_b():
        ctx.fence()
        h_in = d["hb"]
        for g in range(2):
            for b in range(B):
                colbase = b * 1024 + g * 512
                cti = g * 4 + b
                ht = t["htile"][:, (cti % 2) * COLS:(cti % 2) * COLS + COLS]
                ldtile("sync", ht, h_in[:, colbase:colbase + 512])
                for mi, (m, dstd) in enumerate(((8, d["qb"]), (9, d["kb"]))):
                    woff = mi * 1024
                    for oc in range(NCH):
                        ctx.dma("sync", wslot(woff, 1024), wst_fm(m, oc))
                        bank = ps[oc % 2]
                        ctx.mm_group(bank, [
                            (wslot(woff + dc * 128, 128),
                             ht[:, dc * 512:(dc + 1) * 512])
                            for dc in range(NCH)])
                        pp = t["hprod"][:, (oc % 2) * 512:(oc % 2) * 512 + 512]
                        ctx.vcopy(pp, bank)
                        ctx.dma("scalar", dstd[oc * 128:(oc + 1) * 128,
                                               colbase:colbase + 512], pp)
                        if m == 9:
                            for pc in range(4):
                                ctx.vreduce(
                                    "reduce_sum",
                                    t["kcs"][:, (cti * 8 + oc) * 4 + pc:
                                             (cti * 8 + oc) * 4 + pc + 1],
                                    pp[:, pc * 128:(pc + 1) * 128])
                for mi, (m, dstd) in enumerate(((9, d["kbt"]), (10, d["vbt"]))):
                    for half in range(2):
                        wsl = wslot(2048 + half * 4096, 4096)
                        rows = d["wstage"][(m * 8 + half * 4) * 128:
                                           (m * 8 + half * 4 + 4) * 128, :]
                        ctx.dma("sync",
                                wsl.rearrange("p (dc oc o) -> p dc oc o",
                                              dc=8, oc=4),
                                rows.rearrange("(oc p) (dc o) -> p dc oc o",
                                               p=128, dc=8))
                        for pc in range(4):
                            bank = ps[pc % 2]
                            ctx.mm_group(bank, [
                                (ht[:, dc * 512 + pc * 128:
                                    dc * 512 + (pc + 1) * 128],
                                 wsl[:, dc * 512:(dc + 1) * 512])
                                for dc in range(NCH)])
                            pdst = t["ptile"][:, mi * 4096 + pc * 1024 +
                                              half * 512: mi * 4096 + pc * 1024 +
                                              half * 512 + 512]
                            ctx.vcopy(pdst, bank)
                    for pc in range(4):
                        ctx.dma("scalar",
                                dstd[colbase + pc * 128:colbase + (pc + 1) * 128, :],
                                t["ptile"][:, mi * 4096 + pc * 1024:
                                           mi * 4096 + (pc + 1) * 1024])
                for h in range(8):
                    ctx.mm_group(ps[6][:, 0:128], [
                        (t["ptile"][:, pc * 1024 + h * 128:
                                    pc * 1024 + (h + 1) * 128],
                         t["ptile"][:, 4096 + pc * 1024 + h * 128:
                                    4096 + pc * 1024 + (h + 1) * 128])
                        for pc in range(4)], )
                    bst = t["tscr"]
                    ctx.vcopy(bst[:, 0:128], ps[6][:, 0:128])
                    ctx.vreduce("reduce_sum", bst[:, 128:129],
                                t["kcs"][:, (cti * 8 + h) * 4:(cti * 8 + h) * 4 + 4])
                    rowb = ((g * 4 + b) * 8 + h) * 128
                    ctx.dma("scalar", d["bs_pub"][rowb:rowb + 128, 0:129],
                            bst[:, 0:129])
        ctx.cc("AllGather", d["bs_pub"], d["bs_gat"], label="ag_bs")
        ctx.fence()
        ctx.core_switch("attn_b_pfx", attn_b_pfx_body)
        for b in range(B):
            for h in range(8):
                attn_b_B1(b, h)
            for hp in range(4):
                attn_b_B2(b, hp)
            emit_attn_b_project(b)

    def attn_b_pfx_body(cid):
        bl0, bl1 = core_blocks(cid)
        for b in range(B):
            for h in range(8):
                sl8 = ((b * 8 + h) % 2) * 129
                mst = t["pfxm"][:, sl8:sl8 + 129]
                ctx.vmemset(mst, 0.0)
                for j in range(16):
                    for gi, bl in enumerate((bl0, bl1)):
                        if j == bl:
                            idx = (b * 8 + h) * 2 + gi
                            ctx.vcopy(t["spfx"][:, idx * 128:(idx + 1) * 128],
                                      mst[:, 0:128])
                            ctx.vcopy(t["kpfx"][:, idx:idx + 1],
                                      mst[:, 128:129])
                    if j < 15:
                        r_ = j if j < 8 else 15 - j
                        gj = 0 if j < 8 else 1
                        rowb = r_ * 8192 + ((gj * 4 + b) * 8 + h) * 128
                        sl = t["bsring"][:, (j % 5) * 129:(j % 5) * 129 + 129]
                        ctx.dma("gpsimd", sl,
                                d["bs_gat"][rowb:rowb + 128, 0:129])
                        ctx.vtt("tensor_add", mst, mst, sl)

    def attn_b_B1(b, h):
        qs = t["stq"][:, (h % 2) * 1024:(h % 2) * 1024 + 1024]
        ks = t["stk"][:, (h % 2) * 1024:(h % 2) * 1024 + 1024]
        ctx.dma("sync", qs, d["qb"][h * 128:(h + 1) * 128,
                                    b * 1024:(b + 1) * 1024])
        ctx.dma("sync", ks, d["kb"][h * 128:(h + 1) * 128,
                                    b * 1024:(b + 1) * 1024])
        for cc8 in range(8):
            g = cc8 // 4
            sbank = ps[cc8 % 2]
            ctx.mm_group(sbank[:, 0:128], [
                (ks[:, cc8 * 128:(cc8 + 1) * 128], qs[:, cc8 * 128:(cc8 + 1) * 128])])
            stile = t["sT"][:, (h * 8 + cc8) * 128:(h * 8 + cc8 + 1) * 128]
            ctx.vtt("tensor_mul", stile, sbank[:, 0:128], t["tmask32"])
            if cc8 % 4 == 0:
                idx = (b * 8 + h) * 2 + g
                ctx.vcopy(t["zrun"], t["kpfx"][:, idx:idx + 1])
            else:
                cti = g * 4 + b
                ctx.vtt("tensor_add", t["zrun"], t["zrun"],
                        t["kcs"][:, (cti * 8 + h) * 4 + cc8 % 4 - 1:
                                 (cti * 8 + h) * 4 + cc8 % 4])
            ctx.vcopy(t["zcol"][:, 0:1], t["zrun"])
            dbank = ps[2] if cc8 < 4 else ps[6]
            ctx.mm_group(dbank[0:1, (cc8 % 4) * 128:(cc8 % 4 + 1) * 128], [
                (t["zcol"][:, 0:1], qs[:, cc8 * 128:(cc8 + 1) * 128]),
                (t["ones_b"], stile)])
        for half, bank in ((0, ps[2]), (1, ps[6])):
            tmp = t["rows"][0:1, 0:512]
            ctx.act(tmp, bank[0:1, 0:512], AF.Abs)
            ctx.vts("tensor_scalar_max",
                    t["recd"][0:1, h * 1024 + half * 512:
                              h * 1024 + half * 512 + 512], tmp, 1.0)

    O_BANK = {0: 5, 1: 0}
    MS_BANK = {0: 4, 1: 2}
    ST_BANK = {0: 3, 1: 6}

    def attn_b_B2(b, hp):
        hs = (hp * 2, hp * 2 + 1)
        qsl, ktl, vtl, sol, osl, o2l = {}, {}, {}, {}, {}, {}
        for s_, h in enumerate(hs):
            qsl[s_] = t["stq"][:, s_ * 1024:(s_ + 1) * 1024]
            ctx.dma("sync", qsl[s_], d["qb"][h * 128:(h + 1) * 128,
                                             b * 1024:(b + 1) * 1024])
            ktl[s_] = t["stkt"][:, s_ * 1024:(s_ + 1) * 1024]
            ctx.dma("gpsimd", ktl[s_].rearrange("p (pc e) -> p pc e", pc=8),
                    d["kbt"][b * 1024:(b + 1) * 1024,
                             h * 128:(h + 1) * 128].rearrange(
                        "(pc p) e -> p pc e", p=128))
            vtl[s_] = t["stvt"][:, s_ * 1024:(s_ + 1) * 1024]
            ctx.dma("gpsimd", vtl[s_].rearrange("p (pc e) -> p pc e", pc=8),
                    d["vbt"][b * 1024:(b + 1) * 1024,
                             h * 128:(h + 1) * 128].rearrange(
                        "(pc p) e -> p pc e", p=128))
            sol[s_] = t["sop"][:, s_ * 128:(s_ + 1) * 128]
            osl[s_] = t["osb"][:, s_ * 1024:(s_ + 1) * 1024]
            o2l[s_] = t["o2"][:, s_ * 128:(s_ + 1) * 128]
        stm = {0: t["stmf"][:, 0:128], 1: t["stmf"][:, 128:256]}
        for cc8 in range(8):
            for s_, h in enumerate(hs):
                stb = ps[ST_BANK[s_]]
                ob = ps[O_BANK[s_]]
                msb = ps[MS_BANK[s_]]
                if cc8 % 4 == 0:
                    g = cc8 // 4
                    idx = (b * 8 + h) * 2 + g
                    ctx.vcopy(stm[s_],
                              t["spfx"][:, idx * 128:(idx + 1) * 128])
                ctx.vcopy(sol[s_], stm[s_])
                ctx.mm_group(ob[:, 0:128], [
                    (sol[s_], qsl[s_][:, cc8 * 128:(cc8 + 1) * 128]),
                    (vtl[s_][:, cc8 * 128:(cc8 + 1) * 128],
                     t["sT"][:, (h * 8 + cc8) * 128:(h * 8 + cc8 + 1) * 128])])
                ctx.vcopy(osl[s_][:, cc8 * 128:(cc8 + 1) * 128], ob[:, 0:128])
                if cc8 % 4 != 3:
                    ctx.mm_group(stb[:, 0:128],
                                 [(ktl[s_][:, cc8 * 128:(cc8 + 1) * 128],
                                   vtl[s_][:, cc8 * 128:(cc8 + 1) * 128])])
                    ctx.vtt("tensor_add", stm[s_], stm[s_], stb[:, 0:128])
                ctx.act(o2l[s_], osl[s_][:, cc8 * 128:(cc8 + 1) * 128], AF.Square)
                ctx.mm_group(msb[0:1, (cc8 % 4) * 128:(cc8 % 4 + 1) * 128],
                             [(t["ones_b"], o2l[s_])])
                if cc8 % 4 == 3:
                    g = cc8 // 4
                    ctx.vcopy(t["arow"][0:1, s_ * 1024 + g * 512:
                                        s_ * 1024 + (g + 1) * 512],
                              msb[0:1, 0:512])
        for s_, h in enumerate(hs):
            dd = t["rows"][0:1, 0:1024]
            rd = t["recd"][0:1, h * 1024:(h + 1) * 1024]
            ctx.vtt("tensor_mul", dd, rd, rd)
            ctx.vts("tensor_scalar_mul", dd, dd, EPS)
            ctx.vstt(t["rows"][0:1, 1024:2048], t["arow"][0:1, s_ * 1024:
                                                          (s_ + 1) * 1024],
                     1.0 / 128.0, dd, ALU.mult, ALU.add)
            al = t["rows"][0:1, 1024:2048]
            ctx.act(al, al, AF.Ln)
            ctx.act(al, al, AF.Exp, scale=-0.5)
            for g in range(2):
                bcb = ps[1] if g == 0 else ps[O_BANK[0]]
                ctx.mm_group(bcb, [(t["ones_r32"],
                                    al[0:1, g * 512:(g + 1) * 512])])
                ctx.vstt(t["onorm"][:, h * 1024 + g * 512:
                                    h * 1024 + (g + 1) * 512],
                         osl[s_][:, g * 512:(g + 1) * 512], t["normw_s"],
                         bcb, ALU.mult, ALU.mult)

    def emit_attn_b_project(b):
        for g in range(2):
            colbase = b * 1024 + g * 512

            def y_provider(dc, g=g):
                ctx.dma("sync", wslot((dc % 2) * 1024, 1024), wst_fm(11, dc))
                bank = ps[dc % 2]
                ctx.mm_group(bank, [
                    (wslot((dc % 2) * 1024 + k_ * 128, 128),
                     t["onorm"][:, k_ * 1024 + g * 512: k_ * 1024 + g * 512 + 512])
                    for k_ in range(NCH)])
                return bank, None

            emit_epilogue_tile(colbase, y_provider, d["x2"], d["x3"],
                               d["ha"] if TAP != "x3" else None)

    def emit_tap_x(src_dram):
        for ct in range(CT):
            xo = t["xold"][:, (ct % 2) * COLS:(ct % 2) * COLS + COLS]
            ldtile("sync", xo, src_dram[:, ct * 512:(ct + 1) * 512])
            sttile("gpsimd", xo, d["y"][:, ct * 512:(ct + 1) * 512])

    def emit_final_wait():
        for sem, val in sorted(d["y"].res.w.items()):
            if not ctx.dry and ctx.cur == "gpsimd":
                ctx.eng.wait_ge(ctx.handles[sem], val)

    def emit_program():
        epi_seq[0] = 0
        emit_consts()
        emit_attn_a()
        if TAP == "dbg":
            sttile("gpsimd", t["xnew"], d["y"][:, 0:512])
            emit_final_wait()
            return
        if TAP == "x1":
            emit_tap_x(d["x1"])
            emit_final_wait()
            return
        emit_ffn(0)
        if TAP == "x2":
            emit_tap_x(d["x2"])
            emit_final_wait()
            return
        emit_attn_b()
        if TAP == "x3":
            if TAPD is not None:
                src = d[TAPD]
                rows, cols = src.ap.shape
                if (rows, cols) == (DM, COLS):
                    ydst = d["y"]
                else:
                    ydst = d["y"].rearrange("a (b c) -> (a b) c", b=4)
                ctx.dma("gpsimd", ydst, src)
                emit_final_wait()
                return
            emit_tap_x(d["x3"])
            emit_final_wait()
            return
        emit_ffn(1)
        emit_final_wait()

    ctx.dry = True
    emit_program()
    ctx.dry = False
    with nc.Block() as block:
        def run_for(name):
            def _(eng):
                ctx.cur = name
                ctx.eng = eng
                ctx.pid = None
                ctx.counts = {k: 0 for k in SEMS}
                ctx.wm = {e: {s: 0 for s in SEMS} for e in ENGINES}
                ctx.dma_idx = {q: 0 for q in DMA_PREFIX}
                for r_ in ctx.registry:
                    r_.w, r_.r = {}, {}
                emit_program()
            return _
        block.tensor(run_for("tensor"))
        block.vector(run_for("vector"))
        block.scalar(run_for("scalar"))
        block.sync(run_for("sync"))
        block.gpsimd(run_for("gpsimd"))

    for g in reversed(mgrs):
        g.__exit__(None, None, None)
    return nc


def stage_inputs(inp):
    x0 = np.asarray(inp["L0_attn_a_x"])
    xT = np.transpose(x0, (2, 0, 1))
    tr = lambda k: np.ascontiguousarray(np.asarray(inp[k]).T)
    ar = lambda k: np.ascontiguousarray(np.asarray(inp[k]))
    shared = {
        "wqkvT": tr("L0_attn_a_w_qkv"), "woutT": tr("L0_attn_a_w_out"),
        "b_out": ar("L0_attn_a_b_out"),
        "wqT": tr("L0_attn_b_Wq"), "wkT": tr("L0_attn_b_Wk"),
        "wvT": tr("L0_attn_b_Wv"), "woT": tr("L0_attn_b_Wo"),
        "normw": ar("L0_attn_b_norm_w"),
        "wfT": tr("L0_ffn_wf"), "ufT": tr("L0_ffn_uf"),
        "wiT": tr("L0_ffn_wi"), "uiT": tr("L0_ffn_ui"),
        "wcT": tr("L0_ffn_wc"), "ucT": tr("L0_ffn_uc"),
        "wobT": tr("L0_ffn_wo"), "uoT": tr("L0_ffn_uo"),
        "bfv": ar("L0_ffn_bf"), "biv": ar("L0_ffn_bi"), "bcv": ar("L0_ffn_bc"),
        "bov": ar("L0_ffn_bo"), "incv": ar("L0_ffn_in_c"),
    }
    in_maps = []
    for c in range(NCORE):
        b0, b1 = core_blocks(c)
        xc = np.concatenate([xT[:, :, b0 * 512:(b0 + 1) * 512],
                             xT[:, :, b1 * 512:(b1 + 1) * 512]], axis=2)
        m = dict(shared)
        m["x0"] = np.ascontiguousarray(xc.reshape(DM, COLS))
        in_maps.append(m)
    return in_maps


def unstage_output(results):
    out = np.empty((DM, B, S), np.float32)
    for c in range(NCORE):
        b0, b1 = core_blocks(c)
        yc = results[c]["y"].reshape(DM, B, 1024)
        out[:, :, b0 * 512:(b0 + 1) * 512] = yc[:, :, 0:512]
        out[:, :, b1 * 512:(b1 + 1) * 512] = yc[:, :, 512:1024]
    return np.ascontiguousarray(np.transpose(out, (1, 2, 0)))


def kernel(**inp):
    in_maps = stage_inputs(inp)
    nc = build_program()
    res = run_bass_kernel_spmd(nc, in_maps, core_ids=list(range(NCORE)))
    return unstage_output(res.results)


# version 02
# speedup vs baseline: 1.0299x; 1.0299x over previous
import numpy as np
import concourse.bass as bass
import concourse.mybir as mybir
from concourse.bass_utils import run_bass_kernel_spmd

F32 = mybir.dt.float32
BF16 = mybir.dt.bfloat16
AF = mybir.ActivationFunctionType
ALU = mybir.AluOpType
AX = mybir.AxisListType

NCORE = 8
B = 4
S = 8192
DM = 1024
COLS = 4096
NCH = DM // 128
CT = COLS // 512
EPS = 1e-5
ATT_SCALE = float(512 ** -0.5)

TAP = None
TAPD = None
DBG_DUMP = False
STRICT = False


def core_blocks(c):
    return (c, 15 - c)


ENGINES = ("tensor", "vector", "scalar", "sync", "gpsimd")
NLANE = 8
SEMS = {
    "T": ("tensor", 1), "V": ("vector", 1), "SC": ("scalar", 1),
    "G": ("gpsimd", 1), "CC": ("gpsimd", 1),
}
DMA_PREFIX = {"sync": "DY", "gpsimd": "DG", "scalar": "DA"}
for _q, _p in DMA_PREFIX.items():
    for _i in range(NLANE):
        SEMS[f"{_p}{_i}"] = (_q, 16)
ENG_COMPUTE_SEM = {"tensor": "T", "vector": "V", "scalar": "SC", "gpsimd": "G"}


class R:
    __slots__ = ("w", "r", "sbuf")

    def __init__(self, sbuf=False):
        self.w = {}
        self.r = {}
        self.sbuf = sbuf

    def snap(self):
        return (dict(self.w), dict(self.r))

    def restore(self, s):
        self.w, self.r = dict(s[0]), dict(s[1])


class TV:
    __slots__ = ("ap", "res")

    def __init__(self, ap, res):
        self.ap = ap
        self.res = res

    def __getitem__(self, sl):
        return TV(self.ap[sl], self.res)

    def rearrange(self, pat, **kw):
        return TV(self.ap.rearrange(pat, **kw), self.res)


class Ctx:
    def __init__(self):
        self.dry = True
        self.cur = None
        self.eng = None
        self.counts = {k: 0 for k in SEMS}
        self.handles = {}
        self.pid = None
        self.wm = {e: {s: 0 for s in SEMS} for e in ENGINES}
        self.registry = []
        self.labels = {}
        self.dma_idx = {q: 0 for q in DMA_PREFIX}

    def newres(self, sbuf=False):
        r = R(sbuf)
        self.registry.append(r)
        return r

    def _wait(self, engine, sem, val):
        if val <= 0:
            return
        if not STRICT and engine == "tensor" and sem == "T":
            return
        if self.wm[engine][sem] >= val:
            return
        self.wm[engine][sem] = val
        if not self.dry and self.cur == engine:
            self.eng.wait_ge(self.handles[sem], val)

    def _wait_map(self, engine, m):
        for sem, val in m.items():
            self._wait(engine, sem, val)

    def _pre(self, engine, reads, writes, skip_pfx=None):
        def wm_of(res, m):
            if skip_pfx is not None and res.sbuf:
                return {k: v for k, v in m.items() if not k.startswith(skip_pfx)}
            return m
        for tvs in reads:
            if tvs is not None:
                self._wait_map(engine, wm_of(tvs.res, tvs.res.w))
        for tvs in writes:
            if tvs is not None:
                self._wait_map(engine, wm_of(tvs.res, tvs.res.w))
                self._wait_map(engine, wm_of(tvs.res, tvs.res.r))

    def _post(self, sem, reads, writes, label=None):
        inc = SEMS[sem][1]
        self.counts[sem] += inc
        v = self.counts[sem]
        if label is not None:
            if self.dry:
                if label in self.labels and self.labels[label] != (sem, v):
                    raise RuntimeError(f"label {label} inconsistent")
                self.labels[label] = (sem, v)
        for tvs in reads:
            if tvs is not None:
                if tvs.res.r.get(sem, 0) < v:
                    tvs.res.r[sem] = v
        for tvs in writes:
            if tvs is not None:
                if tvs.res.w.get(sem, 0) < v:
                    tvs.res.w[sem] = v
        return v

    def op(self, engine, fn, outs, ins, *a, label=None, dma_sem=None, **kw):
        sem = dma_sem if dma_sem else ENG_COMPUTE_SEM[engine]
        self._pre(engine, ins, outs,
                  skip_pfx=(DMA_PREFIX[engine] if dma_sem else None))
        if not self.dry and self.cur == engine:
            args = [x.ap if isinstance(x, TV) else x for x in a]
            inst = getattr(self.eng, fn)(*args, **{
                k: (v.ap if isinstance(v, TV) else v) for k, v in kw.items()})
            inst.then_inc(self.handles[sem], SEMS[sem][1])
        self._post(sem, ins, outs, label)

    def wait_label(self, engine, name):
        if self.dry:
            assert name in self.labels, f"undefined label {name}"
            return
        sem, val = self.labels[name]
        self._wait(engine, sem, val)

    def mm_group(self, out, ops, label=None, already_open=False):
        n = len(ops)
        self._pre("tensor", [x for p in ops for x in p], [out])
        for i, (l_, r_) in enumerate(ops):
            if not self.dry and self.cur == "tensor":
                inst = self.eng.matmul(out.ap, l_.ap, r_.ap,
                                       start=(i == 0 and not already_open),
                                       stop=(i == n - 1))
                if i == n - 1:
                    inst.then_inc(self.handles["T"], 1)
        self._post("T", [x for p in ops for x in p], [out], label)

    def mm_acc(self, out, lhsT, rhs, start, stop, label=None):
        self._pre("tensor", [lhsT, rhs], [out])
        if not self.dry and self.cur == "tensor":
            inst = self.eng.matmul(out.ap, lhsT.ap, rhs.ap, start=start, stop=stop)
            inst.then_inc(self.handles["T"], 1)
        self._post("T", [lhsT, rhs], [out], label)

    def tr(self, out, in_, ident, label=None):
        self.op("tensor", "transpose", [out], [in_, ident], out, in_, ident,
                label=label)

    def act(self, out, in_, func, bias=0.0, scale=1.0, label=None):
        br = [bias] if isinstance(bias, TV) else []
        sr = [scale] if isinstance(scale, TV) else []
        self.op("scalar", "activation", [out], [in_] + br + sr, out, in_, func,
                bias=bias, scale=scale, label=label)

    def s_copy(self, out, in_, label=None):
        self.op("scalar", "copy", [out], [in_], out, in_, label=label)

    def vcopy(self, out, in_, label=None):
        self.op("vector", "tensor_copy", [out], [in_], out, in_, label=label)

    def vtt(self, fn, out, a_, b_, label=None):
        self.op("vector", fn, [out], [a_, b_], out, a_, b_, label=label)

    def vts(self, fn, out, a_, s_, label=None):
        ins = [a_] + ([s_] if isinstance(s_, TV) else [])
        self.op("vector", fn, [out], ins, out, a_, s_, label=label)

    def vstt(self, out, in0, scal, in1, op0, op1, label=None):
        ins = [in0, in1] + ([scal] if isinstance(scal, TV) else [])
        self.op("vector", "scalar_tensor_tensor", [out], ins,
                out, in0, scal, in1, op0, op1, label=label)

    def vreduce(self, fn, out, in_, label=None):
        self.op("vector", fn, [out], [in_], out, in_, axis=AX.X, label=label)

    def vrecip(self, out, in_, label=None):
        self.op("vector", "reciprocal", [out], [in_], out, in_, label=label)

    def vmemset(self, out, val, label=None):
        self.op("vector", "memset", [out], [], out, val, label=label)

    def gmemset(self, out, val, label=None):
        self.op("gpsimd", "memset", [out], [], out, val, label=label)

    def gaffsel(self, out, in_, pattern, cmp, fill, base, label=None, chm=1):
        self.op("gpsimd", "affine_select", [out], [in_], label=label,
                out=out, in_=in_, pattern=pattern, compare_op=cmp, fill=fill,
                base=base, channel_multiplier=chm)

    def dma(self, q, out, in_, label=None, accum=False, slow=False):
        kw = {}
        if accum:
            kw["accum_op"] = ALU.add
        if slow:
            kw["allow_slow_non_contiguous"] = True
        qi = self.dma_idx[q]
        self.dma_idx[q] = qi + 1
        lane = f"{DMA_PREFIX[q]}{qi % NLANE}"
        if qi >= NLANE:
            self._wait(q, lane, 16 * (qi // NLANE))
        outs = [out]
        ins = [in_] + ([out] if accum else [])
        self.op(q, "dma_start", outs, ins, label=label, dma_sem=lane,
                out=out, in_=in_, **kw)

    def cc(self, kind, in_tv, out_tv, label=None):
        self._pre("gpsimd", [in_tv], [out_tv])
        if not self.dry and self.cur == "gpsimd":
            inst = self.eng.collective_compute(
                kind, ALU.bypass, replica_groups=[list(range(NCORE))],
                ins=[in_tv.ap.opt()], outs=[out_tv.ap.opt()])
            inst.then_inc(self.handles["CC"], 1)
        self._post("CC", [in_tv], [out_tv], label)

    def seed_alias(self, dst_tv, src_tv):
        for sem, v in src_tv.res.w.items():
            if dst_tv.res.w.get(sem, 0) < v:
                dst_tv.res.w[sem] = v
        for sem, v in src_tv.res.r.items():
            if dst_tv.res.r.get(sem, 0) < v:
                dst_tv.res.r[sem] = v

    def fence(self):
        for e in ENGINES:
            for s_ in SEMS:
                self._wait(e, s_, self.counts[s_])

    def snapshot(self):
        return (dict(self.counts),
                [r.snap() for r in self.registry],
                {e: dict(m) for e, m in self.wm.items()},
                dict(self.dma_idx))

    def restore(self, s):
        self.counts = dict(s[0])
        for r, rs in zip(self.registry, s[1]):
            r.restore(rs)
        self.wm = {e: dict(m) for e, m in s[2].items()}
        self.dma_idx = dict(s[3])

    def core_switch(self, tag, body_fn):
        nreg = len(self.registry)
        entry = self.snapshot()
        if self.dry:
            exit_s = None
            d0 = None
            for i in range(NCORE):
                self.restore(entry)
                body_fn(i)
                if len(self.registry) != nreg:
                    raise RuntimeError(f"switch {tag}: body allocated resources")
                d = {k: self.counts[k] - entry[0][k] for k in SEMS}
                if d0 is None:
                    d0, exit_s = d, self.snapshot()
                elif d != d0:
                    diff = {k: (d0[k], d[k]) for k in SEMS if d[k] != d0[k]}
                    raise RuntimeError(f"switch {tag} body {i}: delta {diff}")
            self.restore(exit_s)
        else:
            if self.pid is None:
                self.pid = self.eng.partition_id()
            exit_s = None
            first = True
            for i in self.eng.Switch(self.pid, n=NCORE):
                self.restore(entry)
                body_fn(i)
                if first:
                    exit_s = self.snapshot()
                    first = False
            self.restore(exit_s)
        final = dict(self.counts)
        for r_, es in zip(self.registry, entry[1]):
            if r_.snap() != es:
                r_.w = {s_: final[s_] for s_ in SEMS if final[s_] > 0}
                r_.r = {}


def build_program():
    nc = bass.Bass("TRN2", target_bir_lowering=False, debug=False,
                   detect_race_conditions=STRICT)
    ctx = Ctx()

    d = {}

    def ein(name, shape):
        d[name] = TV(nc.dram_tensor(name, list(shape), F32,
                                    kind="ExternalInput").ap(), ctx.newres())

    def din(name, shape, dt=F32):
        d[name] = TV(nc.dram_tensor(name, list(shape), dt).ap(), ctx.newres())

    ein("x0", (DM, COLS))
    ein("wqkvT", (DM, 1536))
    ein("woutT", (512, DM))
    for n in ("b_out", "bfv", "biv", "bcv", "bov", "incv"):
        ein(n, (DM,))
    for n in ("wqT", "wkT", "wvT", "woT",
              "wfT", "ufT", "wiT", "uiT", "wcT", "ucT", "wobT", "uoT"):
        ein(n, (DM, DM))
    ein("normw", (128,))
    d["y"] = TV(nc.dram_tensor("y", [DM, COLS], F32, kind="ExternalOutput").ap(),
                ctx.newres())

    din("x1", (DM, COLS))
    din("x2", (DM, COLS))
    din("x3", (DM, COLS))
    din("ha", (DM, COLS), BF16)
    din("hb", (DM, COLS), BF16)
    din("ka_pub", (B * 513, 1024), BF16)
    din("va_pub", (COLS, 512), BF16)
    for _b in range(B):
        din(f"ka_gat{_b}", (NCORE * 513, 1024), BF16)
        din(f"va_gat{_b}", (NCORE * 1024, 512), BF16)
    din("qb", (DM, COLS), BF16)
    din("kb", (DM, COLS), BF16)
    din("kbt", (COLS, DM), BF16)
    din("vbt", (COLS, DM), BF16)
    din("bs_pub", (2 * B * 8 * 128, 129), BF16)
    din("bs_gat_a", (NCORE * B * 8 * 128, 129), BF16)
    din("bs_gat_b", (NCORE * B * 8 * 128, 129), BF16)
    for _m in range(12):
        din(f"wst{_m}", (8 * 128, DM), BF16)

    mgrs = []
    t = {}

    def sb(name, shape, dt=F32, grp=None):
        g = nc.sbuf_tensor(name, list(shape), dt)
        h = g.__enter__()
        if grp is None:
            mgrs.append(g)
        else:
            grp.append(g)
        t[name] = TV(h[:], ctx.newres(sbuf=True))
        return t[name]

    def free_grp(grp):
        for g in reversed(grp):
            g.__exit__(None, None, None)

    sb("ident", (128, 128), BF16)
    sb("tmask", (128, 896), BF16)
    sb("tmask32", (128, 128))
    sb("ones_b", (128, 1), BF16)
    sb("ones_r32", (1, 128))
    sb("bout", (128, NCH))
    sb("bfs", (128, NCH))
    sb("bis", (128, NCH))
    sb("bcs", (128, NCH))
    sb("bos", (128, NCH))
    sb("incs", (128, NCH))
    sb("normw_s", (128, 1))
    sb("rows", (1, 4096))
    sb("xold", (128, 2 * COLS))
    sb("xnew", (128, COLS))
    sb("tscr", (128, 512))
    sb("hprod", (128, 2 * 512), BF16)
    sb("sq", (128, 2 * 512), BF16)
    sb("xbf", (128, 2 * 512), BF16)
    sb("wring", (128, 12288), BF16)
    ga = []
    sb("htile", (128, 2 * COLS), BF16, ga)
    sb("q_a", (128, 4 * COLS), BF16, ga)
    sb("kblk", (128, 2 * 2048), BF16, ga)
    sb("vtile", (128, 3 * 512), BF16, ga)
    sb("etile", (128, 2 * 512), BF16, ga)
    sb("kn", (1, 2 * 1024), BF16, ga)
    sb("qmaxg", (1, CT), F32, ga)
    sb("kmaxb", (1, B * 16), F32, ga)
    sb("negb", (128, 2), F32, ga)
    sb("rscol", (128, 4), F32, ga)
    sb("ao_tm", (128, 4 * 512), BF16, ga)
    sb("ao_fm", (128, 2 * 4 * 512), BF16, ga)
    free_grp(ga)
    gf = []
    sb("c8", (128, NCH * 1024), F32, gf)
    sb("cbf", (128, NCH * 1024), BF16, gf)
    sb("htf", (128, 2 * NCH * 1024), BF16, gf)
    sb("gates", (128, 12 * 512), F32, gf)
    free_grp(gf)
    gb = []
    sb("sT_dummy", (1, 1), F32, gb)
    sb("kcs", (128, 256), F32, gb)
    for _b in range(B):
        sb(f"kpfx{_b}", (128, 16), F32, gb)
    sb("bsring", (128, 5 * 129), F32, gb)
    sb("pfxm", (128, 4 * 129), F32, gb)
    sb("zcol", (128, 4), BF16, gb)
    sb("zrun", (128, 1), F32, gb)
    sb("recd", (1, 8 * 1024), BF16, gb)
    sb("stq", (128, 2 * 1024), BF16, gb)
    sb("stk", (128, 2 * 1024), BF16, gb)
    sb("stkt", (128, 2 * 1024), BF16, gb)
    sb("stvt", (128, 2 * 1024), BF16, gb)
    sb("sop", (128, 2 * 128), BF16, gb)
    sb("stmf", (128, 2 * 128), F32, gb)
    sb("osb", (128, 2 * 1024), F32, gb)
    sb("o2", (128, 2 * 128), BF16, gb)
    sb("onorm", (128, 8 * 1024), BF16, gb)
    gp = []
    sb("htileb", (128, 2 * COLS), BF16, gp)
    sb("ptile", (128, 8 * 1024), BF16, gp)
    free_grp(gp)
    gs = []
    sb("sT", (128, 8 * 8 * 128), BF16, gs)
    for _b in range(B):
        sb(f"spfx{_b}", (128, 16 * 128), BF16, gs)
    free_grp(gs)
    free_grp(gb)

    ps = []
    for i in range(7):
        g = nc.psum_tensor(f"ps{i}", [128, 512], F32)
        h = g.__enter__()
        mgrs.append(g)
        ps.append(TV(h[:], ctx.newres(sbuf=True)))
    g = nc.psum_tensor("psbf", [128, 1024], BF16)
    h = g.__enter__()
    mgrs.append(g)
    psbf = TV(h[:], ctx.newres(sbuf=True))

    for s_ in SEMS:
        g = nc.semaphore(s_)
        ctx.handles[s_] = g.__enter__()
        mgrs.append(g)

    def seg(name, a, b_):
        return t[name][:, a:b_]

    def ldtile(q, dst, src2d, c=8):
        k = src2d.ap.shape[1]
        ctx.dma(q, dst.rearrange("p (c k) -> p c k", c=c),
                src2d.rearrange("(c p) k -> p c k", p=128))

    def sttile(q, src, dst2d, c=8):
        ctx.dma(q, dst2d.rearrange("(c p) k -> p c k", p=128),
                src.rearrange("p (c k) -> p c k", c=c))

    def wslot(off, width):
        return t["wring"][:, off:off + width]

    def load_wcast(q, wt, ocol0, width, off, label=None):
        src = wt[:, ocol0:ocol0 + width].rearrange("(c p) o -> p c o", p=128)
        dst = wslot(off, NCH * width).rearrange("p (c o) -> p c o", c=NCH)
        ctx.dma(q, dst, src, label=label)

    def wst_fm(m, oc):
        return d[f"wst{m}"][oc * 128:(oc + 1) * 128, :]

    def ln_stats_chunk(xn_tv, dc, uid):
        xb = seg("xbf", (dc % 2) * 512, (dc % 2) * 512 + 512)
        sqs = seg("sq", (dc % 2) * 512, (dc % 2) * 512 + 512)
        ctx.vcopy(xb, xn_tv)
        ctx.act(sqs, xb, AF.Square)
        ctx.mm_acc(ps[2][0:1, :], t["ones_b"], xb, start=(dc == 0),
                   stop=(dc == NCH - 1))
        ctx.mm_acc(ps[3][0:1, :], t["ones_b"], sqs, start=(dc == 0),
                   stop=(dc == NCH - 1))

    def ln_finalize():
        mu = t["rows"][0:1, 0:512]
        m2 = t["rows"][0:1, 512:1024]
        rstd = t["rows"][0:1, 1024:1536]
        ctx.vts("tensor_scalar_mul", mu, ps[2][0:1, 0:512], 1.0 / DM)
        ctx.vts("tensor_scalar_mul", m2, ps[3][0:1, 0:512], 1.0 / DM)
        ctx.vtt("tensor_mul", rstd, mu, mu)
        ctx.vtt("tensor_sub", rstd, m2, rstd)
        ctx.vts("tensor_scalar_add", rstd, rstd, EPS)
        ctx.act(rstd, rstd, AF.Ln)
        ctx.act(rstd, rstd, AF.Exp, scale=-0.5)
        ctx.mm_group(ps[4], [(t["ones_r32"], mu)])
        ctx.mm_group(ps[5], [(t["ones_r32"], rstd)])

    def ln_apply_chunk(x_tv, h_tv):
        ctx.vtt("tensor_sub", t["tscr"], x_tv, ps[4])
        ctx.vtt("tensor_mul", h_tv, t["tscr"], ps[5])

    epi_seq = [0]

    def emit_epilogue_tile(colbase, y_provider, x_old, x_new_dram, h_dram):
        eid = epi_seq[0]
        epi_seq[0] += 1
        xo = t["xold"][:, (eid % 2) * COLS:(eid % 2) * COLS + COLS]
        ldtile("sync", xo, x_old[:, colbase:colbase + 512])
        for dc in range(NCH):
            ysrc, bias = y_provider(dc)
            xn = t["xnew"][:, dc * 512:(dc + 1) * 512]
            if bias is not None:
                ctx.vstt(xn, ysrc, bias, xo[:, dc * 512:(dc + 1) * 512],
                         ALU.add, ALU.add)
            else:
                ctx.vtt("tensor_add", xn, ysrc, xo[:, dc * 512:(dc + 1) * 512])
            ctx.dma("scalar", x_new_dram[dc * 128:(dc + 1) * 128,
                                         colbase:colbase + 512], xn)
            if h_dram is not None:
                ln_stats_chunk(xn, dc, eid)
        if h_dram is None:
            return
        ln_finalize()
        for dc in range(NCH):
            hp = t["hprod"][:, (dc % 2) * 512:(dc % 2) * 512 + 512]
            xn = t["xnew"][:, dc * 512:(dc + 1) * 512]
            ln_apply_chunk(xn, hp)
            ctx.dma("scalar", h_dram[dc * 128:(dc + 1) * 128,
                                     colbase:colbase + 512], hp)

    def emit_consts():
        ctx.gmemset(t["ones_b"], 1.0)
        ctx.gmemset(t["ones_r32"], 1.0)
        ctx.gmemset(t["tmask"], 1.0)
        ctx.gmemset(t["tmask32"], 1.0)
        ctx.gmemset(t["ident"], 1.0)
        ctx.gaffsel(t["ident"], t["ident"], [[-1, 128]], ALU.is_equal, 0.0, 0)
        ctx.gaffsel(t["tmask"], t["tmask"], [[1, 896]], ALU.is_ge, 0.0, -384,
                    chm=-1)
        ctx.gaffsel(t["tmask32"], t["tmask32"], [[1, 128]], ALU.is_ge, 0.0, 0,
                    chm=-1)
        for name, dst in (("b_out", "bout"), ("bfv", "bfs"), ("biv", "bis"),
                          ("bcv", "bcs"), ("bov", "bos"), ("incv", "incs")):
            ctx.dma("sync", t[dst], d[name].rearrange("(c p) -> p c", p=128),
                    slow=True)
        ctx.dma("sync", t["normw_s"], d["normw"][:, None])


    def emit_wstage(ms):
        for m in ms:
            w = ("wfT", "ufT", "wiT", "uiT", "wcT", "ucT",
                 "wobT", "uoT", "wqT", "wkT", "wvT", "woT")[m]
            for oc in range(8):
                src = d[w][:, oc * 128:(oc + 1) * 128].rearrange(
                    "(dc p) o -> p dc o", p=128)
                dst = wst_fm(m, oc).rearrange("p (dc o) -> p dc o", dc=8)
                ctx.dma("gpsimd", dst, src)

    def emit_attn_a():
        for ct in range(CT):
            b, g_ = ct // 2, ct % 2
            colbase = ct * 512
            xo = t["xold"][:, (ct % 2) * COLS:(ct % 2) * COLS + COLS]
            ldtile("sync", xo, d["x0"][:, colbase:colbase + 512])
            for dc in range(NCH):
                ln_stats_chunk(xo[:, dc * 512:(dc + 1) * 512], dc, ct)
            ln_finalize()
            ht = t["htile"][:, (ct % 2) * COLS:(ct % 2) * COLS + COLS]
            for dc in range(NCH):
                ln_apply_chunk(xo[:, dc * 512:(dc + 1) * 512],
                               ht[:, dc * 512:(dc + 1) * 512])
            for oc in range(8):
                off = (oc % 2) * 1024
                load_wcast("gpsimd", d["wqkvT"], oc * 128, 128, off)
                bank = ps[oc % 2]
                ctx.mm_group(bank, [
                    (wslot(off + dc * 128, 128), ht[:, dc * 512:(dc + 1) * 512])
                    for dc in range(NCH)])
                if oc < 4:
                    qd = t["q_a"][:, oc * COLS + colbase: oc * COLS + colbase + 512]
                    ctx.vcopy(qd, bank)
                    sqs = seg("sq", (oc % 2) * 512, (oc % 2) * 512 + 512)
                    ctx.act(sqs, qd, AF.Square)
                    ctx.mm_acc(ps[6][0:1, :], t["ones_b"], sqs, start=(oc == 0),
                               stop=(oc == 3))
                    if oc == 3:
                        ctx.vreduce("reduce_max",
                                    t["qmaxg"][0:1, ct:ct + 1], ps[6][0:1, :])
                else:
                    ko = oc - 4
                    kp = t["hprod"][:, (oc % 2) * 512:(oc % 2) * 512 + 512]
                    ctx.vcopy(kp, bank)
                    ctx.dma("scalar",
                            d["ka_pub"][b * 513 + ko * 128:b * 513 + (ko + 1) * 128,
                                        g_ * 512:g_ * 512 + 512], kp)
                    sqs = seg("sq", (oc % 2) * 512, (oc % 2) * 512 + 512)
                    ctx.act(sqs, kp, AF.Square)
                    ctx.mm_acc(ps[6][0:1, :], t["ones_b"], sqs, start=(ko == 0),
                               stop=(ko == 3))
                    if ko == 3:
                        kr = t["rows"][0:1, 1536:2048]
                        ctx.vcopy(kr, ps[6][0:1, :])
                        ctx.s_copy(t["kn"][0:1, 0:512], kr)
                        ctx.dma("scalar",
                                d["ka_pub"][b * 513 + 512:b * 513 + 513,
                                            g_ * 512:g_ * 512 + 512],
                                t["kn"][0:1, 0:512])
            load_wcast("gpsimd", d["wqkvT"], 1024, 256, 2048)
            load_wcast("gpsimd", d["wqkvT"], 1280, 256, 4096)
            for pc in range(4):
                bank = ps[pc % 2]
                for half in range(2):
                    ctx.mm_group(bank[:, half * 256:(half + 1) * 256], [
                        (ht[:, dc * 512 + pc * 128: dc * 512 + (pc + 1) * 128],
                         wslot(2048 + half * 2048 + dc * 256, 256))
                        for dc in range(NCH)])
                vp = t["hprod"][:, (pc % 2) * 512:(pc % 2) * 512 + 512]
                ctx.vcopy(vp, bank)
                ctx.dma("scalar", d["va_pub"][colbase + pc * 128:
                                              colbase + (pc + 1) * 128, :], vp)
            if g_ == 1:
                ctx.cc("AllGather", d["ka_pub"][b * 513:(b + 1) * 513, :],
                       d[f"ka_gat{b}"])
                ctx.cc("AllGather", d["va_pub"][b * 1024:(b + 1) * 1024, :],
                       d[f"va_gat{b}"])
        emit_wstage(range(8))
        for b in range(B):
            for r_ in range(NCORE):
                sl = t["kn"][0:1, (r_ % 2) * 1024:(r_ % 2) * 1024 + 1024]
                ctx.dma("sync", sl,
                        d[f"ka_gat{b}"][r_ * 513 + 512:r_ * 513 + 513, :])
                for half in range(2):
                    j = r_ if half == 0 else 15 - r_
                    ctx.vreduce("reduce_max",
                                t["kmaxb"][0:1, b * 16 + j:b * 16 + j + 1],
                                sl[0:1, half * 512:(half + 1) * 512])
        if TAP == "dbg":
            ctx.vmemset(t["xnew"], 0.0)
        for b in range(B):
            ctx.core_switch(f"attn_a_b{b}",
                            lambda cid, b=b: attn_a_body(cid, b))
            if TAP == "dbg":
                return
            for g in range(2):
                emit_attn_a_epilogue(b, g)

    def attn_a_body(cid, b):
        for g, bl in enumerate(core_blocks(cid)):
            attn_a_group(cid, b, g, bl)

    def attn_a_group(cid, b, g, bl):
        colbase = b * 1024 + g * 512
        nkc = (bl + 1) * 4
        qmax = t["rows"][0:1, 0:1]
        kmax = t["rows"][0:1, 1:2]
        ctx.vreduce("reduce_max", kmax, t["kmaxb"][0:1, b * 16:b * 16 + bl + 1])
        ctx.vtt("tensor_mul", qmax, t["qmaxg"][0:1, 2 * b + g:2 * b + g + 1],
                kmax)
        ctx.act(qmax, qmax, AF.Ln)
        ctx.act(qmax, qmax, AF.Exp, scale=0.5)
        ctx.op("scalar", "mul", [qmax], [qmax], qmax, qmax, -ATT_SCALE)
        ctx.mm_group(ps[2][:, 0:1], [(t["ones_r32"], qmax)])
        nb = t["negb"][:, g:g + 1]
        ctx.vcopy(nb, ps[2][:, 0:1])

        def kblk_slot(j):
            return t["kblk"][:, (j % 2) * 2048:(j % 2) * 2048 + 2048]

        def kblk_load(j):
            r_ = j if j < 8 else 15 - j
            off = 0 if j < 8 else 512
            ldtile("sync", kblk_slot(j),
                   d[f"ka_gat{b}"][r_ * 513:r_ * 513 + 512, off:off + 512], c=4)

        def vt_slot(i):
            return t["vtile"][:, (i % 3) * 512:(i % 3) * 512 + 512]

        def vt_load(kc):
            j = kc // 4
            r_ = j if j < 8 else 15 - j
            row = r_ * 1024 + (0 if j < 8 else 512) + (kc % 4) * 128
            ctx.dma("gpsimd", vt_slot(kc), d[f"va_gat{b}"][row:row + 128, :])

        kblk_load(0)
        if nkc > 4:
            kblk_load(1)
        for i in range(min(3, nkc)):
            vt_load(i)

        for kc in range(nkc):
            j = kc // 4
            sbank = ps[kc % 2]
            et = t["etile"][:, (kc % 2) * 512:(kc % 2) * 512 + 512]
            ctx.mm_group(sbank, [
                (kblk_slot(j)[:, dc * 512 + (kc % 4) * 128:
                              dc * 512 + (kc % 4) * 128 + 128],
                 t["q_a"][:, dc * COLS + colbase: dc * COLS + colbase + 512])
                for dc in range(4)])
            ctx.act(et, sbank, AF.Exp, bias=nb, scale=ATT_SCALE)
            if kc >= nkc - 4:
                o_ = (kc - (nkc - 4)) * 128
                msl = t["tmask"][:, 384 - o_:384 - o_ + 512]
                ctx.vtt("tensor_mul", et, et, msl)
            if TAP == "dbg" and DBG_DUMP and b == 0 and kc == 0:
                xn = t["xnew"]
                ctx.vcopy(xn[:, g * 2048 + 0:g * 2048 + 512], sbank)
                ctx.vcopy(xn[:, g * 2048 + 512:g * 2048 + 1024], et)
                ctx.vcopy(xn[:, g * 2048 + 1024:g * 2048 + 1152],
                          kblk_slot(0)[:, 0:128])
                ctx.vcopy(xn[:, g * 2048 + 1152:g * 2048 + 1280],
                          t["q_a"][:, colbase:colbase + 128])
                ctx.vcopy(xn[:, g * 2048 + 1280:g * 2048 + 1281], nb)
                ctx.vcopy(xn[0:1, g * 2048 + 1300:g * 2048 + 1308],
                          t["qmaxg"][0:1, :])
                ctx.vcopy(xn[0:1, g * 2048 + 1308:g * 2048 + 1372],
                          t["kmaxb"][0:1, :])
                ctx.vcopy(xn[:, g * 2048 + 1408:g * 2048 + 1920],
                          vt_slot(0))
            ctx.mm_acc(ps[2][0:1, :], t["ones_b"], et, start=(kc == 0),
                       stop=(kc == nkc - 1))
            for qt in range(4):
                ctx.mm_acc(ps[3 + qt], et[:, qt * 128:(qt + 1) * 128],
                           vt_slot(kc), start=(kc == 0), stop=(kc == nkc - 1))
            if kc % 4 == 3 and j + 2 <= bl:
                kblk_load(j + 2)
            if kc + 3 < nkc:
                vt_load(kc + 3)
        se = t["rows"][0:1, 0:512]
        rs = t["rows"][0:1, 512:1024]
        ctx.vcopy(se, ps[2][0:1, :])
        ctx.vts("tensor_scalar_add", se, se, 1e-30)
        ctx.vrecip(rs, se)
        for qt in range(4):
            ctx.tr(ps[2][:, qt:qt + 1], rs[0:1, qt * 128:(qt + 1) * 128],
                   t["ones_r32"][0:1, 0:1])
            ctx.vcopy(t["rscol"][:, qt:qt + 1], ps[2][:, qt:qt + 1])
            ctx.vts("tensor_scalar_mul", t["ao_tm"][:, qt * 512:(qt + 1) * 512],
                    ps[3 + qt], t["rscol"][:, qt:qt + 1])
        for qt in range(4):
            for vc in range(4):
                pslot = psbf[:, (qt % 2) * 512 + vc * 128:
                             (qt % 2) * 512 + (vc + 1) * 128]
                ctx.tr(pslot, t["ao_tm"][:, qt * 512 + vc * 128:
                                         qt * 512 + (vc + 1) * 128], t["ident"])
                ctx.vcopy(t["ao_fm"][:, g * 2048 + vc * 512 + qt * 128:
                                     g * 2048 + vc * 512 + (qt + 1) * 128], pslot)

    def emit_attn_a_epilogue(b, g):
        colbase = b * 1024 + g * 512

        def y_provider(dc):
            off = 6144 + (dc % 2) * 512
            src = d["woutT"][:, dc * 128:(dc + 1) * 128].rearrange(
                "(c p) o -> p c o", p=128)
            ctx.dma("gpsimd", wslot(off, 512).rearrange("p (c o) -> p c o", c=4),
                    src)
            bank = ps[dc % 2]
            ctx.mm_group(bank, [
                (wslot(off + vc * 128, 128),
                 t["ao_fm"][:, g * 2048 + vc * 512:g * 2048 + (vc + 1) * 512])
                for vc in range(4)])
            return bank, t["bout"][:, dc:dc + 1]

        emit_epilogue_tile(colbase, y_provider, d["x0"], d["x1"],
                           d["ha"] if TAP != "x1" else None)

    def emit_ffn(call):
        ctx.fence()
        if call == 0:
            emit_wstage((8, 9, 10, 11))
        x_in = d["x1"] if call == 0 else d["x3"]
        x_out = d["x2"] if call == 0 else d["y"]
        h_in = d["ha"]
        h_out = d["hb"] if (call == 0 and TAP != "x2") else None
        cc8 = [t["c8"][:, i * 1024:(i + 1) * 1024] for i in range(NCH)]
        cbf = [t["cbf"][:, i * 1024:(i + 1) * 1024] for i in range(NCH)]

        def htslot(i):
            return t["htf"][:, (i % 2) * 8192:(i % 2) * 8192 + 8192]

        def ht_load(i, tstep):
            ldtile("sync", htslot(i), h_in[:, tstep * 1024:(tstep + 1) * 1024])

        for dc in range(NCH):
            ctx.vmemset(cc8[dc], 0.0)
            ctx.op("scalar", "add", [cc8[dc]], [cc8[dc], t["incs"]],
                   cc8[dc], cc8[dc], t["incs"][:, dc:dc + 1])
        ht_load(0, 0)
        GW = {0: ("bfs", AF.Sigmoid, 0), 1: ("bis", AF.Sigmoid, 2),
              2: ("bcs", AF.Tanh, 4)}
        for tstep in range(4):
            if tstep < 3:
                ht_load(tstep + 1, tstep + 1)
            ht = htslot(tstep)
            for dc in range(NCH):
                ctx.vcopy(cbf[dc], cc8[dc])
            for oc in range(NCH):
                wb = (oc % 2) * 6144
                gb_ = (oc % 2) * 3072
                for g3 in range(3):
                    ctx.dma("sync", wslot(wb + g3 * 2048, 1024),
                            wst_fm(2 * g3, oc))
                    ctx.dma("sync", wslot(wb + g3 * 2048 + 1024, 1024),
                            wst_fm(2 * g3 + 1, oc))
                for g3 in range(3):
                    bname, afn, bk = GW[g3]
                    for ph in range(2):
                        bank = ps[bk + ph]
                        ctx.mm_group(bank, [
                            (wslot(wb + g3 * 2048 + dc * 128, 128),
                             ht[:, dc * 1024 + ph * 512: dc * 1024 + ph * 512 + 512])
                            for dc in range(NCH)] + [
                            (wslot(wb + g3 * 2048 + 1024 + dc * 128, 128),
                             cbf[dc][:, ph * 512:(ph + 1) * 512])
                            for dc in range(NCH)])
                        gt = t["gates"][:, gb_ + (g3 * 2 + ph) * 512:
                                        gb_ + (g3 * 2 + ph + 1) * 512]
                        ctx.act(gt, bank, afn, bias=t[bname][:, oc:oc + 1])
                for ph in range(2):
                    fg = t["gates"][:, gb_ + ph * 512:gb_ + (ph + 1) * 512]
                    ig = t["gates"][:, gb_ + (2 + ph) * 512:gb_ + (3 + ph) * 512]
                    gg = t["gates"][:, gb_ + (4 + ph) * 512:gb_ + (5 + ph) * 512]
                    cs = cc8[oc][:, ph * 512:(ph + 1) * 512]
                    ctx.vtt("tensor_mul", t["tscr"], ig, gg)
                    ctx.vtt("tensor_mul", cs, fg, cs)
                    ctx.vtt("tensor_add", cs, cs, t["tscr"])
        for dc in range(NCH):
            ctx.vcopy(cbf[dc], cc8[dc])
        for oc in range(NCH):
            ub = (oc % 2) * 1024
            ctx.dma("sync", wslot(ub, 1024), wst_fm(7, oc))
            for ph in range(2):
                bank = ps[oc % 2]
                ctx.mm_group(bank, [
                    (wslot(ub + dc * 128, 128),
                     cbf[dc][:, ph * 512:(ph + 1) * 512])
                    for dc in range(NCH)])
                ctx.vcopy(cc8[oc][:, ph * 512:(ph + 1) * 512], bank)
        for tstep in range(4):
            ht_load(tstep, tstep)
            ht = htslot(tstep)
            for ph in range(2):
                def y_provider(dc, tstep=tstep, ph=ph, ht=ht):
                    ctx.dma("sync", wslot(2048 + (dc % 2) * 1024, 1024),
                            wst_fm(6, dc))
                    bank = ps[dc % 2]
                    ctx.mm_group(bank, [
                        (wslot(2048 + (dc % 2) * 1024 + k_ * 128, 128),
                         ht[:, k_ * 1024 + ph * 512: k_ * 1024 + ph * 512 + 512])
                        for k_ in range(NCH)])
                    ys = t["gates"][:, (dc % 2) * 512:(dc % 2) * 512 + 512]
                    ctx.vstt(ys, bank, t["bos"][:, dc:dc + 1],
                             cc8[dc][:, ph * 512:(ph + 1) * 512], ALU.add, ALU.add)
                    ctx.act(ys, ys, AF.Tanh)
                    return ys, None
                emit_epilogue_tile(tstep * 1024 + ph * 512, y_provider,
                                   x_in, x_out, h_out)

    def emit_attn_b():
        ctx.fence()
        h_in = d["hb"]
        for g in range(2):
            if g == 1:
                ctx.cc("AllGather", d["bs_pub"][0:4096, :], d["bs_gat_a"])
            for b in range(B):
                colbase = b * 1024 + g * 512
                cti = g * 4 + b
                ht = t["htileb"][:, (cti % 2) * COLS:(cti % 2) * COLS + COLS]
                ldtile("sync", ht, h_in[:, colbase:colbase + 512])
                for mi, (m, dstd) in enumerate(((8, d["qb"]), (9, d["kb"]))):
                    for oc in range(NCH):
                        woff = mi * 2048 + (oc % 2) * 1024
                        ctx.dma("sync", wslot(woff, 1024), wst_fm(m, oc))
                        bank = ps[oc % 2]
                        ctx.mm_group(bank, [
                            (wslot(woff + dc * 128, 128),
                             ht[:, dc * 512:(dc + 1) * 512])
                            for dc in range(NCH)])
                        pp = t["hprod"][:, (oc % 2) * 512:(oc % 2) * 512 + 512]
                        ctx.vcopy(pp, bank)
                        ctx.dma("scalar", dstd[oc * 128:(oc + 1) * 128,
                                               colbase:colbase + 512], pp)
                        if m == 9:
                            for pc in range(4):
                                ctx.vreduce(
                                    "reduce_sum",
                                    t["kcs"][:, (cti * 8 + oc) * 4 + pc:
                                             (cti * 8 + oc) * 4 + pc + 1],
                                    pp[:, pc * 128:(pc + 1) * 128])
                for mi, (m, dstd) in enumerate(((9, d["kbt"]), (10, d["vbt"]))):
                    for half in range(2):
                        wsl = wslot(4096 + half * 4096, 4096)
                        rows = d[f"wst{m}"][(half * 4) * 128:
                                            (half * 4 + 4) * 128, :]
                        ctx.dma("sync",
                                wsl.rearrange("p (dc oc o) -> p dc oc o",
                                              dc=8, oc=4),
                                rows.rearrange("(oc p) (dc o) -> p dc oc o",
                                               p=128, dc=8))
                        for pc in range(4):
                            bank = ps[pc % 2]
                            ctx.mm_group(bank, [
                                (ht[:, dc * 512 + pc * 128:
                                    dc * 512 + (pc + 1) * 128],
                                 wsl[:, dc * 512:(dc + 1) * 512])
                                for dc in range(NCH)])
                            pdst = t["ptile"][:, mi * 4096 + pc * 1024 +
                                              half * 512: mi * 4096 + pc * 1024 +
                                              half * 512 + 512]
                            ctx.vcopy(pdst, bank)
                    for pc in range(4):
                        ctx.dma("scalar",
                                dstd[colbase + pc * 128:colbase + (pc + 1) * 128, :],
                                t["ptile"][:, mi * 4096 + pc * 1024:
                                           mi * 4096 + (pc + 1) * 1024])
                for h in range(8):
                    ctx.mm_group(ps[6][:, 0:128], [
                        (t["ptile"][:, pc * 1024 + h * 128:
                                    pc * 1024 + (h + 1) * 128],
                         t["ptile"][:, 4096 + pc * 1024 + h * 128:
                                    4096 + pc * 1024 + (h + 1) * 128])
                        for pc in range(4)], )
                    bst = t["tscr"]
                    ctx.vcopy(bst[:, 0:128], ps[6][:, 0:128])
                    ctx.vreduce("reduce_sum", bst[:, 128:129],
                                t["kcs"][:, (cti * 8 + h) * 4:(cti * 8 + h) * 4 + 4])
                    rowb = ((g * 4 + b) * 8 + h) * 128
                    ctx.dma("gpsimd", d["bs_pub"][rowb:rowb + 128, 0:129],
                            bst[:, 0:129])
        ctx.cc("AllGather", d["bs_pub"][4096:8192, :], d["bs_gat_b"])
        for _tv in [t["sT"]] + [t[f"spfx{_b}"] for _b in range(B)]:
            ctx.seed_alias(_tv, t["ptile"])
            ctx.seed_alias(_tv, t["htileb"])
        for b in range(B):
            ctx.core_switch(f"attn_b_pfx{b}",
                            lambda cid, b=b: attn_b_pfx_body(cid, b))
            for h in range(8):
                attn_b_B1(b, h)
            for hp in range(4):
                attn_b_B2(b, hp)
            emit_attn_b_project(b)

    def attn_b_pfx_body(cid, b):
        bl0, bl1 = core_blocks(cid)
        for h in range(8):
            sl8 = (h % 4) * 129
            mst = t["pfxm"][:, sl8:sl8 + 129]
            ctx.vmemset(mst, 0.0)
            for j in range(16):
                for gi, bl in enumerate((bl0, bl1)):
                    if j == bl:
                        idx = h * 2 + gi
                        ctx.vcopy(t[f"spfx{b}"][:, idx * 128:(idx + 1) * 128],
                                  mst[:, 0:128])
                        ctx.vcopy(t[f"kpfx{b}"][:, idx:idx + 1],
                                  mst[:, 128:129])
                if j < 15:
                    r_ = j if j < 8 else 15 - j
                    gat = d["bs_gat_a"] if j < 8 else d["bs_gat_b"]
                    rowb = r_ * 4096 + (b * 8 + h) * 128
                    sl = t["bsring"][:, (j % 5) * 129:(j % 5) * 129 + 129]
                    ctx.dma("gpsimd", sl, gat[rowb:rowb + 128, 0:129])
                    ctx.vtt("tensor_add", mst, mst, sl)

    def attn_b_B1(b, h):
        qs = t["stq"][:, (h % 2) * 1024:(h % 2) * 1024 + 1024]
        ks = t["stk"][:, (h % 2) * 1024:(h % 2) * 1024 + 1024]
        ctx.dma("sync", qs, d["qb"][h * 128:(h + 1) * 128,
                                    b * 1024:(b + 1) * 1024])
        ctx.dma("sync", ks, d["kb"][h * 128:(h + 1) * 128,
                                    b * 1024:(b + 1) * 1024])
        for cc8 in range(8):
            g = cc8 // 4
            sbank = ps[cc8 % 2]
            ctx.mm_group(sbank[:, 0:128], [
                (ks[:, cc8 * 128:(cc8 + 1) * 128], qs[:, cc8 * 128:(cc8 + 1) * 128])])
            stile = t["sT"][:, (h * 8 + cc8) * 128:(h * 8 + cc8 + 1) * 128]
            ctx.vtt("tensor_mul", stile, sbank[:, 0:128], t["tmask32"])
            if cc8 % 4 == 0:
                idx = h * 2 + g
                ctx.vcopy(t["zrun"], t[f"kpfx{b}"][:, idx:idx + 1])
            else:
                cti = g * 4 + b
                ctx.vtt("tensor_add", t["zrun"], t["zrun"],
                        t["kcs"][:, (cti * 8 + h) * 4 + cc8 % 4 - 1:
                                 (cti * 8 + h) * 4 + cc8 % 4])
            ctx.vcopy(t["zcol"][:, 0:1], t["zrun"])
            dbank = ps[2] if cc8 < 4 else ps[6]
            ctx.mm_group(dbank[0:1, (cc8 % 4) * 128:(cc8 % 4 + 1) * 128], [
                (t["zcol"][:, 0:1], qs[:, cc8 * 128:(cc8 + 1) * 128]),
                (t["ones_b"], stile)])
        for half, bank in ((0, ps[2]), (1, ps[6])):
            tmp = t["rows"][0:1, 0:512]
            ctx.act(tmp, bank[0:1, 0:512], AF.Abs)
            ctx.vts("tensor_scalar_max",
                    t["recd"][0:1, h * 1024 + half * 512:
                              h * 1024 + half * 512 + 512], tmp, 1.0)

    O_BANK = {0: 5, 1: 0}
    MS_BANK = {0: 4, 1: 2}
    ST_BANK = {0: 3, 1: 6}

    def attn_b_B2(b, hp):
        hs = (hp * 2, hp * 2 + 1)
        qsl, ktl, vtl, sol, osl, o2l = {}, {}, {}, {}, {}, {}
        for s_, h in enumerate(hs):
            qsl[s_] = t["stq"][:, s_ * 1024:(s_ + 1) * 1024]
            ctx.dma("sync", qsl[s_], d["qb"][h * 128:(h + 1) * 128,
                                             b * 1024:(b + 1) * 1024])
            ktl[s_] = t["stkt"][:, s_ * 1024:(s_ + 1) * 1024]
            ctx.dma("gpsimd", ktl[s_].rearrange("p (pc e) -> p pc e", pc=8),
                    d["kbt"][b * 1024:(b + 1) * 1024,
                             h * 128:(h + 1) * 128].rearrange(
                        "(pc p) e -> p pc e", p=128))
            vtl[s_] = t["stvt"][:, s_ * 1024:(s_ + 1) * 1024]
            ctx.dma("gpsimd", vtl[s_].rearrange("p (pc e) -> p pc e", pc=8),
                    d["vbt"][b * 1024:(b + 1) * 1024,
                             h * 128:(h + 1) * 128].rearrange(
                        "(pc p) e -> p pc e", p=128))
            sol[s_] = t["sop"][:, s_ * 128:(s_ + 1) * 128]
            osl[s_] = t["osb"][:, s_ * 1024:(s_ + 1) * 1024]
            o2l[s_] = t["o2"][:, s_ * 128:(s_ + 1) * 128]
        stm = {0: t["stmf"][:, 0:128], 1: t["stmf"][:, 128:256]}
        for cc8 in range(8):
            for s_, h in enumerate(hs):
                stb = ps[ST_BANK[s_]]
                ob = ps[O_BANK[s_]]
                msb = ps[MS_BANK[s_]]
                if cc8 % 4 == 0:
                    g = cc8 // 4
                    idx = h * 2 + g
                    ctx.vcopy(stm[s_],
                              t[f"spfx{b}"][:, idx * 128:(idx + 1) * 128])
                ctx.vcopy(sol[s_], stm[s_])
                ctx.mm_group(ob[:, 0:128], [
                    (sol[s_], qsl[s_][:, cc8 * 128:(cc8 + 1) * 128]),
                    (vtl[s_][:, cc8 * 128:(cc8 + 1) * 128],
                     t["sT"][:, (h * 8 + cc8) * 128:(h * 8 + cc8 + 1) * 128])])
                ctx.vcopy(osl[s_][:, cc8 * 128:(cc8 + 1) * 128], ob[:, 0:128])
                if cc8 % 4 != 3:
                    ctx.mm_group(stb[:, 0:128],
                                 [(ktl[s_][:, cc8 * 128:(cc8 + 1) * 128],
                                   vtl[s_][:, cc8 * 128:(cc8 + 1) * 128])])
                    ctx.vtt("tensor_add", stm[s_], stm[s_], stb[:, 0:128])
                ctx.act(o2l[s_], osl[s_][:, cc8 * 128:(cc8 + 1) * 128], AF.Square)
                ctx.mm_group(msb[0:1, (cc8 % 4) * 128:(cc8 % 4 + 1) * 128],
                             [(t["ones_b"], o2l[s_])])
                if cc8 % 4 == 3:
                    g = cc8 // 4
                    ctx.vcopy(t["rows"][0:1, 2048 + s_ * 1024 + g * 512:
                                        2048 + s_ * 1024 + (g + 1) * 512],
                              msb[0:1, 0:512])
        for s_, h in enumerate(hs):
            dd = t["rows"][0:1, 0:1024]
            rd = t["recd"][0:1, h * 1024:(h + 1) * 1024]
            ctx.vtt("tensor_mul", dd, rd, rd)
            ctx.vts("tensor_scalar_mul", dd, dd, EPS)
            ctx.vstt(t["rows"][0:1, 1024:2048],
                     t["rows"][0:1, 2048 + s_ * 1024:2048 + (s_ + 1) * 1024],
                     1.0 / 128.0, dd, ALU.mult, ALU.add)
            al = t["rows"][0:1, 1024:2048]
            ctx.act(al, al, AF.Ln)
            ctx.act(al, al, AF.Exp, scale=-0.5)
            for g in range(2):
                bcb = ps[1] if g == 0 else ps[O_BANK[0]]
                ctx.mm_group(bcb, [(t["ones_r32"],
                                    al[0:1, g * 512:(g + 1) * 512])])
                ctx.vstt(t["onorm"][:, h * 1024 + g * 512:
                                    h * 1024 + (g + 1) * 512],
                         osl[s_][:, g * 512:(g + 1) * 512], t["normw_s"],
                         bcb, ALU.mult, ALU.mult)

    def emit_attn_b_project(b):
        for g in range(2):
            colbase = b * 1024 + g * 512

            def y_provider(dc, g=g):
                ctx.dma("sync", wslot((dc % 2) * 1024, 1024), wst_fm(11, dc))
                bank = ps[dc % 2]
                ctx.mm_group(bank, [
                    (wslot((dc % 2) * 1024 + k_ * 128, 128),
                     t["onorm"][:, k_ * 1024 + g * 512: k_ * 1024 + g * 512 + 512])
                    for k_ in range(NCH)])
                return bank, None

            emit_epilogue_tile(colbase, y_provider, d["x2"], d["x3"],
                               d["ha"] if TAP != "x3" else None)

    def emit_tap_x(src_dram):
        for ct in range(CT):
            xo = t["xold"][:, (ct % 2) * COLS:(ct % 2) * COLS + COLS]
            ldtile("sync", xo, src_dram[:, ct * 512:(ct + 1) * 512])
            sttile("gpsimd", xo, d["y"][:, ct * 512:(ct + 1) * 512])

    def emit_final_wait():
        for sem, val in sorted(d["y"].res.w.items()):
            if not ctx.dry and ctx.cur == "gpsimd":
                ctx.eng.wait_ge(ctx.handles[sem], val)

    def emit_program():
        epi_seq[0] = 0
        emit_consts()
        emit_attn_a()
        if TAP == "dbg":
            sttile("gpsimd", t["xnew"], d["y"][:, 0:512])
            emit_final_wait()
            return
        if TAP == "x1":
            emit_tap_x(d["x1"])
            emit_final_wait()
            return
        emit_ffn(0)
        if TAP == "x2":
            emit_tap_x(d["x2"])
            emit_final_wait()
            return
        emit_attn_b()
        if TAP == "x3":
            if TAPD is not None:
                src = d[TAPD]
                rows, cols = src.ap.shape
                if (rows, cols) == (DM, COLS):
                    ydst = d["y"]
                else:
                    ydst = d["y"].rearrange("a (b c) -> (a b) c", b=4)
                ctx.dma("gpsimd", ydst, src)
                emit_final_wait()
                return
            emit_tap_x(d["x3"])
            emit_final_wait()
            return
        emit_ffn(1)
        emit_final_wait()

    ctx.dry = True
    emit_program()
    ctx.dry = False
    with nc.Block() as block:
        def run_for(name):
            def _(eng):
                ctx.cur = name
                ctx.eng = eng
                ctx.pid = None
                ctx.counts = {k: 0 for k in SEMS}
                ctx.wm = {e: {s: 0 for s in SEMS} for e in ENGINES}
                ctx.dma_idx = {q: 0 for q in DMA_PREFIX}
                for r_ in ctx.registry:
                    r_.w, r_.r = {}, {}
                emit_program()
            return _
        block.tensor(run_for("tensor"))
        block.vector(run_for("vector"))
        block.scalar(run_for("scalar"))
        block.sync(run_for("sync"))
        block.gpsimd(run_for("gpsimd"))

    for g in reversed(mgrs):
        g.__exit__(None, None, None)
    return nc


def stage_inputs(inp):
    x0 = np.asarray(inp["L0_attn_a_x"])
    xT = np.transpose(x0, (2, 0, 1))
    tr = lambda k: np.ascontiguousarray(np.asarray(inp[k]).T)
    ar = lambda k: np.ascontiguousarray(np.asarray(inp[k]))
    shared = {
        "wqkvT": tr("L0_attn_a_w_qkv"), "woutT": tr("L0_attn_a_w_out"),
        "b_out": ar("L0_attn_a_b_out"),
        "wqT": tr("L0_attn_b_Wq"), "wkT": tr("L0_attn_b_Wk"),
        "wvT": tr("L0_attn_b_Wv"), "woT": tr("L0_attn_b_Wo"),
        "normw": ar("L0_attn_b_norm_w"),
        "wfT": tr("L0_ffn_wf"), "ufT": tr("L0_ffn_uf"),
        "wiT": tr("L0_ffn_wi"), "uiT": tr("L0_ffn_ui"),
        "wcT": tr("L0_ffn_wc"), "ucT": tr("L0_ffn_uc"),
        "wobT": tr("L0_ffn_wo"), "uoT": tr("L0_ffn_uo"),
        "bfv": ar("L0_ffn_bf"), "biv": ar("L0_ffn_bi"), "bcv": ar("L0_ffn_bc"),
        "bov": ar("L0_ffn_bo"), "incv": ar("L0_ffn_in_c"),
    }
    in_maps = []
    for c in range(NCORE):
        b0, b1 = core_blocks(c)
        xc = np.concatenate([xT[:, :, b0 * 512:(b0 + 1) * 512],
                             xT[:, :, b1 * 512:(b1 + 1) * 512]], axis=2)
        m = dict(shared)
        m["x0"] = np.ascontiguousarray(xc.reshape(DM, COLS))
        in_maps.append(m)
    return in_maps


def unstage_output(results):
    out = np.empty((DM, B, S), np.float32)
    for c in range(NCORE):
        b0, b1 = core_blocks(c)
        yc = results[c]["y"].reshape(DM, B, 1024)
        out[:, :, b0 * 512:(b0 + 1) * 512] = yc[:, :, 0:512]
        out[:, :, b1 * 512:(b1 + 1) * 512] = yc[:, :, 512:1024]
    return np.ascontiguousarray(np.transpose(out, (1, 2, 0)))


def kernel(**inp):
    in_maps = stage_inputs(inp)
    nc = build_program()
    res = run_bass_kernel_spmd(nc, in_maps, core_ids=list(range(NCORE)))
    return unstage_output(res.results)


# version 03
# speedup vs baseline: 1.0819x; 1.0505x over previous
import numpy as np
import concourse.bass as bass
import concourse.mybir as mybir
from concourse.bass_utils import run_bass_kernel_spmd

F32 = mybir.dt.float32
BF16 = mybir.dt.bfloat16
AF = mybir.ActivationFunctionType
ALU = mybir.AluOpType
AX = mybir.AxisListType

NCORE = 8
B = 4
S = 8192
DM = 1024
COLS = 4096
NCH = DM // 128
CT = COLS // 512
EPS = 1e-5
ATT_SCALE = float(512 ** -0.5)

TAP = None
TAPD = None
DBG_DUMP = False
STRICT = False


def core_blocks(c):
    return (c, 15 - c)


ENGINES = ("tensor", "vector", "scalar", "sync", "gpsimd")
NLANE = 8
SEMS = {
    "T": ("tensor", 1), "V": ("vector", 1), "SC": ("scalar", 1),
    "G": ("gpsimd", 1), "CC": ("gpsimd", 1),
}
DMA_PREFIX = {"sync": "DY", "gpsimd": "DG", "scalar": "DA"}
for _q, _p in DMA_PREFIX.items():
    for _i in range(NLANE):
        SEMS[f"{_p}{_i}"] = (_q, 16)
ENG_COMPUTE_SEM = {"tensor": "T", "vector": "V", "scalar": "SC", "gpsimd": "G"}


class R:
    __slots__ = ("w", "r", "sbuf")

    def __init__(self, sbuf=False):
        self.w = {}
        self.r = {}
        self.sbuf = sbuf

    def snap(self):
        return (dict(self.w), dict(self.r))

    def restore(self, s):
        self.w, self.r = dict(s[0]), dict(s[1])


class TV:
    __slots__ = ("ap", "res")

    def __init__(self, ap, res):
        self.ap = ap
        self.res = res

    def __getitem__(self, sl):
        return TV(self.ap[sl], self.res)

    def rearrange(self, pat, **kw):
        return TV(self.ap.rearrange(pat, **kw), self.res)


class Ctx:
    def __init__(self):
        self.dry = True
        self.cur = None
        self.eng = None
        self.counts = {k: 0 for k in SEMS}
        self.handles = {}
        self.pid = None
        self.wm = {e: {s: 0 for s in SEMS} for e in ENGINES}
        self.registry = []
        self.labels = {}
        self.dma_idx = {q: 0 for q in DMA_PREFIX}

    def newres(self, sbuf=False):
        r = R(sbuf)
        self.registry.append(r)
        return r

    def _wait(self, engine, sem, val):
        if val <= 0:
            return
        if not STRICT and engine == "tensor" and sem == "T":
            return
        if self.wm[engine][sem] >= val:
            return
        self.wm[engine][sem] = val
        if not self.dry and self.cur == engine:
            self.eng.wait_ge(self.handles[sem], val)

    def _wait_map(self, engine, m):
        for sem, val in m.items():
            self._wait(engine, sem, val)

    def _pre(self, engine, reads, writes, skip_pfx=None):
        def wm_of(res, m):
            if skip_pfx is not None and res.sbuf:
                return {k: v for k, v in m.items() if not k.startswith(skip_pfx)}
            return m
        for tvs in reads:
            if tvs is not None:
                self._wait_map(engine, wm_of(tvs.res, tvs.res.w))
        for tvs in writes:
            if tvs is not None:
                self._wait_map(engine, wm_of(tvs.res, tvs.res.w))
                self._wait_map(engine, wm_of(tvs.res, tvs.res.r))

    def _post(self, sem, reads, writes, label=None):
        inc = SEMS[sem][1]
        self.counts[sem] += inc
        v = self.counts[sem]
        if label is not None:
            if self.dry:
                if label in self.labels and self.labels[label] != (sem, v):
                    raise RuntimeError(f"label {label} inconsistent")
                self.labels[label] = (sem, v)
        for tvs in reads:
            if tvs is not None:
                if tvs.res.r.get(sem, 0) < v:
                    tvs.res.r[sem] = v
        for tvs in writes:
            if tvs is not None:
                if tvs.res.w.get(sem, 0) < v:
                    tvs.res.w[sem] = v
        return v

    def op(self, engine, fn, outs, ins, *a, label=None, dma_sem=None, **kw):
        sem = dma_sem if dma_sem else ENG_COMPUTE_SEM[engine]
        self._pre(engine, ins, outs,
                  skip_pfx=(DMA_PREFIX[engine] if dma_sem else None))
        if not self.dry and self.cur == engine:
            args = [x.ap if isinstance(x, TV) else x for x in a]
            inst = getattr(self.eng, fn)(*args, **{
                k: (v.ap if isinstance(v, TV) else v) for k, v in kw.items()})
            inst.then_inc(self.handles[sem], SEMS[sem][1])
        self._post(sem, ins, outs, label)

    def wait_label(self, engine, name):
        if self.dry:
            assert name in self.labels, f"undefined label {name}"
            return
        sem, val = self.labels[name]
        self._wait(engine, sem, val)

    def mm_group(self, out, ops, label=None, already_open=False):
        n = len(ops)
        self._pre("tensor", [x for p in ops for x in p], [out])
        for i, (l_, r_) in enumerate(ops):
            if not self.dry and self.cur == "tensor":
                inst = self.eng.matmul(out.ap, l_.ap, r_.ap,
                                       start=(i == 0 and not already_open),
                                       stop=(i == n - 1))
                if i == n - 1:
                    inst.then_inc(self.handles["T"], 1)
        self._post("T", [x for p in ops for x in p], [out], label)

    def mm_acc(self, out, lhsT, rhs, start, stop, label=None):
        self._pre("tensor", [lhsT, rhs], [out])
        if not self.dry and self.cur == "tensor":
            inst = self.eng.matmul(out.ap, lhsT.ap, rhs.ap, start=start, stop=stop)
            inst.then_inc(self.handles["T"], 1)
        self._post("T", [lhsT, rhs], [out], label)

    def tr(self, out, in_, ident, label=None):
        self.op("tensor", "transpose", [out], [in_, ident], out, in_, ident,
                label=label)

    def act(self, out, in_, func, bias=0.0, scale=1.0, label=None):
        br = [bias] if isinstance(bias, TV) else []
        sr = [scale] if isinstance(scale, TV) else []
        self.op("scalar", "activation", [out], [in_] + br + sr, out, in_, func,
                bias=bias, scale=scale, label=label)

    def s_copy(self, out, in_, label=None):
        self.op("scalar", "copy", [out], [in_], out, in_, label=label)

    def vcopy(self, out, in_, label=None):
        self.op("vector", "tensor_copy", [out], [in_], out, in_, label=label)

    def vtt(self, fn, out, a_, b_, label=None):
        self.op("vector", fn, [out], [a_, b_], out, a_, b_, label=label)

    def vts(self, fn, out, a_, s_, label=None):
        ins = [a_] + ([s_] if isinstance(s_, TV) else [])
        self.op("vector", fn, [out], ins, out, a_, s_, label=label)

    def vstt(self, out, in0, scal, in1, op0, op1, label=None):
        ins = [in0, in1] + ([scal] if isinstance(scal, TV) else [])
        self.op("vector", "scalar_tensor_tensor", [out], ins,
                out, in0, scal, in1, op0, op1, label=label)

    def vreduce(self, fn, out, in_, label=None):
        self.op("vector", fn, [out], [in_], out, in_, axis=AX.X, label=label)

    def vrecip(self, out, in_, label=None):
        self.op("vector", "reciprocal", [out], [in_], out, in_, label=label)

    def vmemset(self, out, val, label=None):
        self.op("vector", "memset", [out], [], out, val, label=label)

    def gmemset(self, out, val, label=None):
        self.op("gpsimd", "memset", [out], [], out, val, label=label)

    def gaffsel(self, out, in_, pattern, cmp, fill, base, label=None, chm=1):
        self.op("gpsimd", "affine_select", [out], [in_], label=label,
                out=out, in_=in_, pattern=pattern, compare_op=cmp, fill=fill,
                base=base, channel_multiplier=chm)

    def dma(self, q, out, in_, label=None, accum=False, slow=False):
        kw = {}
        if accum:
            kw["accum_op"] = ALU.add
        if slow:
            kw["allow_slow_non_contiguous"] = True
        qi = self.dma_idx[q]
        self.dma_idx[q] = qi + 1
        lane = f"{DMA_PREFIX[q]}{qi % NLANE}"
        if qi >= NLANE:
            self._wait(q, lane, 16 * (qi // NLANE))
        outs = [out]
        ins = [in_] + ([out] if accum else [])
        self.op(q, "dma_start", outs, ins, label=label, dma_sem=lane,
                out=out, in_=in_, **kw)

    def cc(self, kind, in_tv, out_tv, label=None):
        self._pre("gpsimd", [in_tv], [out_tv])
        if not self.dry and self.cur == "gpsimd":
            inst = self.eng.collective_compute(
                kind, ALU.bypass, replica_groups=[list(range(NCORE))],
                ins=[in_tv.ap.opt()], outs=[out_tv.ap.opt()])
            inst.then_inc(self.handles["CC"], 1)
        self._post("CC", [in_tv], [out_tv], label)

    def seed_alias(self, dst_tv, src_tv):
        for sem, v in src_tv.res.w.items():
            if dst_tv.res.w.get(sem, 0) < v:
                dst_tv.res.w[sem] = v
        for sem, v in src_tv.res.r.items():
            if dst_tv.res.r.get(sem, 0) < v:
                dst_tv.res.r[sem] = v

    def fence(self):
        for e in ENGINES:
            for s_ in SEMS:
                self._wait(e, s_, self.counts[s_])

    def snapshot(self):
        return (dict(self.counts),
                [r.snap() for r in self.registry],
                {e: dict(m) for e, m in self.wm.items()},
                dict(self.dma_idx))

    def restore(self, s):
        self.counts = dict(s[0])
        for r, rs in zip(self.registry, s[1]):
            r.restore(rs)
        self.wm = {e: dict(m) for e, m in s[2].items()}
        self.dma_idx = dict(s[3])

    def core_switch(self, tag, body_fn):
        nreg = len(self.registry)
        entry = self.snapshot()
        if self.dry:
            exit_s = None
            d0 = None
            for i in range(NCORE):
                self.restore(entry)
                body_fn(i)
                if len(self.registry) != nreg:
                    raise RuntimeError(f"switch {tag}: body allocated resources")
                d = {k: self.counts[k] - entry[0][k] for k in SEMS}
                if d0 is None:
                    d0, exit_s = d, self.snapshot()
                elif d != d0:
                    diff = {k: (d0[k], d[k]) for k in SEMS if d[k] != d0[k]}
                    raise RuntimeError(f"switch {tag} body {i}: delta {diff}")
            self.restore(exit_s)
        else:
            if self.pid is None:
                self.pid = self.eng.partition_id()
            exit_s = None
            first = True
            for i in self.eng.Switch(self.pid, n=NCORE):
                self.restore(entry)
                body_fn(i)
                if first:
                    exit_s = self.snapshot()
                    first = False
            self.restore(exit_s)
        final = dict(self.counts)
        for r_, es in zip(self.registry, entry[1]):
            if r_.snap() != es:
                r_.w = {s_: final[s_] for s_ in SEMS if final[s_] > 0}
                r_.r = {}


def build_program():
    nc = bass.Bass("TRN2", target_bir_lowering=False, debug=False,
                   detect_race_conditions=STRICT)
    ctx = Ctx()

    d = {}

    def ein(name, shape):
        d[name] = TV(nc.dram_tensor(name, list(shape), F32,
                                    kind="ExternalInput").ap(), ctx.newres())

    def din(name, shape, dt=F32):
        d[name] = TV(nc.dram_tensor(name, list(shape), dt).ap(), ctx.newres())

    ein("x0", (DM, COLS))
    ein("wqkvT", (DM, 1536))
    ein("woutT", (512, DM))
    for n in ("b_out", "bfv", "biv", "bcv", "bov", "incv"):
        ein(n, (DM,))
    for n in ("wqT", "wkT", "wvT", "woT",
              "wfT", "ufT", "wiT", "uiT", "wcT", "ucT", "wobT", "uoT"):
        ein(n, (DM, DM))
    ein("normw", (128,))
    d["y"] = TV(nc.dram_tensor("y", [DM, COLS], F32, kind="ExternalOutput").ap(),
                ctx.newres())

    din("x1", (DM, COLS))
    din("x2", (DM, COLS))
    din("x3", (DM, COLS))
    din("ha", (DM, COLS), BF16)
    din("hb", (DM, COLS), BF16)
    din("ka_pub", (B * 513, 1024), BF16)
    din("va_pub", (COLS, 512), BF16)
    for _b in range(B):
        din(f"ka_gat{_b}", (NCORE * 513, 1024), BF16)
        din(f"va_gat{_b}", (NCORE * 1024, 512), BF16)
    din("qb", (DM, COLS), BF16)
    din("kb", (DM, COLS), BF16)
    din("kbt", (COLS, DM), BF16)
    din("vbt", (COLS, DM), BF16)
    din("bs_pub", (2 * B * 8 * 128, 129), BF16)
    din("bs_gat_a", (NCORE * B * 8 * 128, 129), BF16)
    din("bs_gat_b", (NCORE * B * 8 * 128, 129), BF16)
    for _m in range(12):
        din(f"wst{_m}", (8 * 128, DM), BF16)

    mgrs = []
    t = {}

    def sb(name, shape, dt=F32, grp=None):
        g = nc.sbuf_tensor(name, list(shape), dt)
        h = g.__enter__()
        if grp is None:
            mgrs.append(g)
        else:
            grp.append(g)
        t[name] = TV(h[:], ctx.newres(sbuf=True))
        return t[name]

    def free_grp(grp):
        for g in reversed(grp):
            g.__exit__(None, None, None)

    sb("ident", (128, 128), BF16)
    sb("tmask", (128, 896), BF16)
    sb("tmask32", (128, 128))
    sb("ones_b", (128, 1), BF16)
    sb("ones_r32", (1, 128))
    sb("bout", (128, NCH))
    sb("bfs", (128, NCH))
    sb("bis", (128, NCH))
    sb("bcs", (128, NCH))
    sb("bos", (128, NCH))
    sb("incs", (128, NCH))
    sb("normw_s", (128, 1))
    sb("rows", (1, 4096))
    sb("xold", (128, 2 * COLS))
    sb("xnew", (128, COLS))
    sb("tscr", (128, 2 * 512))
    sb("hprod", (128, 2 * 512), BF16)
    sb("sq", (128, 2 * 512), BF16)
    sb("xbf", (128, 2 * 512), BF16)
    sb("wring", (128, 12288), BF16)
    ga = []
    sb("htile", (128, 2 * COLS), BF16, ga)
    sb("q_a", (128, 4 * COLS), BF16, ga)
    sb("kblk", (128, 2 * 2048), BF16, ga)
    sb("vtile", (128, 3 * 512), BF16, ga)
    sb("etile", (128, 2 * 512), BF16, ga)
    sb("kn", (1, 2 * 1024), BF16, ga)
    sb("qmaxg", (1, CT), F32, ga)
    sb("kmaxb", (1, B * 16), F32, ga)
    sb("negb", (128, 2), F32, ga)
    sb("rscol", (128, 4), F32, ga)
    sb("ao_tm", (128, 4 * 512), BF16, ga)
    sb("ao_fm", (128, 2 * 4 * 512), BF16, ga)
    free_grp(ga)
    gf = []
    sb("c8", (128, NCH * 1024), F32, gf)
    sb("cbf", (128, NCH * 1024), BF16, gf)
    sb("htf", (128, 2 * NCH * 1024), BF16, gf)
    sb("gates", (128, 12 * 512), F32, gf)
    free_grp(gf)
    gb = []
    sb("sT_dummy", (1, 1), F32, gb)
    sb("kcs", (128, 256), F32, gb)
    sb("zcx", (128, 257), BF16, gb)
    sb("kpfb", (128, 64), BF16, gb)
    for _b in range(B):
        sb(f"kpfx{_b}", (128, 16), F32, gb)
    sb("bsring", (128, 5 * 129), F32, gb)
    sb("pfxm", (128, 4 * 129), F32, gb)
    sb("zcol", (128, 4), BF16, gb)
    sb("zrun", (128, 1), F32, gb)
    sb("recd", (1, 8 * 1024), BF16, gb)
    sb("stq", (128, 2 * 1024), BF16, gb)
    sb("stk", (128, 2 * 1024), BF16, gb)
    sb("stkt", (128, 2 * 1024), BF16, gb)
    sb("stvt", (128, 2 * 1024), BF16, gb)
    sb("sop", (128, 2 * 128), BF16, gb)
    sb("stmf", (128, 2 * 128), F32, gb)
    sb("osb", (128, 2 * 1024), F32, gb)
    sb("onorm", (128, 8 * 1024), BF16, gb)
    gp = []
    sb("htileb", (128, 2 * COLS), BF16, gp)
    sb("ptile", (128, 8 * 1024), BF16, gp)
    free_grp(gp)
    gs = []
    sb("sT", (128, 8 * 8 * 128), BF16, gs)
    for _b in range(B):
        sb(f"spfx{_b}", (128, 16 * 128), BF16, gs)
    free_grp(gs)
    free_grp(gb)

    ps = []
    for i in range(7):
        g = nc.psum_tensor(f"ps{i}", [128, 512], F32)
        h = g.__enter__()
        mgrs.append(g)
        ps.append(TV(h[:], ctx.newres(sbuf=True)))
    g = nc.psum_tensor("psbf", [128, 1024], BF16)
    h = g.__enter__()
    mgrs.append(g)
    psbf = TV(h[:], ctx.newres(sbuf=True))

    for s_ in SEMS:
        g = nc.semaphore(s_)
        ctx.handles[s_] = g.__enter__()
        mgrs.append(g)

    def seg(name, a, b_):
        return t[name][:, a:b_]

    def ldtile(q, dst, src2d, c=8):
        k = src2d.ap.shape[1]
        ctx.dma(q, dst.rearrange("p (c k) -> p c k", c=c),
                src2d.rearrange("(c p) k -> p c k", p=128))

    def sttile(q, src, dst2d, c=8):
        ctx.dma(q, dst2d.rearrange("(c p) k -> p c k", p=128),
                src.rearrange("p (c k) -> p c k", c=c))

    def wslot(off, width):
        return t["wring"][:, off:off + width]

    def load_wcast(q, wt, ocol0, width, off, label=None):
        src = wt[:, ocol0:ocol0 + width].rearrange("(c p) o -> p c o", p=128)
        dst = wslot(off, NCH * width).rearrange("p (c o) -> p c o", c=NCH)
        ctx.dma(q, dst, src, label=label)

    def wst_fm(m, oc):
        return d[f"wst{m}"][oc * 128:(oc + 1) * 128, :]

    def ln_stats_chunk(xn_tv, dc, uid):
        xb = seg("xbf", (dc % 2) * 512, (dc % 2) * 512 + 512)
        sqs = seg("sq", (dc % 2) * 512, (dc % 2) * 512 + 512)
        ctx.vcopy(xb, xn_tv)
        ctx.act(sqs, xb, AF.Square)
        ctx.mm_acc(ps[2][0:1, :], t["ones_b"], xb, start=(dc == 0),
                   stop=(dc == NCH - 1))
        ctx.mm_acc(ps[3][0:1, :], t["ones_b"], sqs, start=(dc == 0),
                   stop=(dc == NCH - 1))

    def ln_finalize():
        mu = t["rows"][0:1, 0:512]
        m2 = t["rows"][0:1, 512:1024]
        rstd = t["rows"][0:1, 1024:1536]
        ctx.vts("tensor_scalar_mul", mu, ps[2][0:1, 0:512], 1.0 / DM)
        ctx.vts("tensor_scalar_mul", m2, ps[3][0:1, 0:512], 1.0 / DM)
        ctx.vtt("tensor_mul", rstd, mu, mu)
        ctx.vtt("tensor_sub", rstd, m2, rstd)
        ctx.vts("tensor_scalar_add", rstd, rstd, EPS)
        ctx.act(rstd, rstd, AF.Ln)
        ctx.act(rstd, rstd, AF.Exp, scale=-0.5)
        ctx.mm_group(ps[4], [(t["ones_r32"], mu)])
        ctx.mm_group(ps[5], [(t["ones_r32"], rstd)])

    def ln_apply_chunk(x_tv, h_tv, par=0):
        sl = t["tscr"][:, (par % 2) * 512:(par % 2 + 1) * 512]
        ctx.vtt("tensor_sub", sl, x_tv, ps[4])
        ctx.vtt("tensor_mul", h_tv, sl, ps[5])

    epi_seq = [0]

    def emit_epilogue_tile(colbase, y_provider, x_old, x_new_dram, h_dram):
        eid = epi_seq[0]
        epi_seq[0] += 1
        xo = t["xold"][:, (eid % 2) * COLS:(eid % 2) * COLS + COLS]
        ldtile("sync", xo, x_old[:, colbase:colbase + 512])
        def stats_emit(dc):
            xn = t["xnew"][:, dc * 512:(dc + 1) * 512]
            ln_stats_chunk(xn, dc, eid)
        for dc in range(NCH):
            ysrc, bias = y_provider(dc)
            xn = t["xnew"][:, dc * 512:(dc + 1) * 512]
            if bias is not None:
                ctx.vstt(xn, ysrc, bias, xo[:, dc * 512:(dc + 1) * 512],
                         ALU.add, ALU.add)
            else:
                ctx.vtt("tensor_add", xn, ysrc, xo[:, dc * 512:(dc + 1) * 512])
            ctx.dma("scalar", x_new_dram[dc * 128:(dc + 1) * 128,
                                         colbase:colbase + 512], xn)
            if h_dram is not None and dc >= 1:
                stats_emit(dc - 1)
        if h_dram is not None:
            stats_emit(NCH - 1)
        if h_dram is None:
            return
        ln_finalize()
        for dc in range(NCH):
            hp = t["hprod"][:, (dc % 2) * 512:(dc % 2) * 512 + 512]
            xn = t["xnew"][:, dc * 512:(dc + 1) * 512]
            ln_apply_chunk(xn, hp, par=dc)
            ctx.dma("scalar", h_dram[dc * 128:(dc + 1) * 128,
                                     colbase:colbase + 512], hp)

    def emit_consts():
        ctx.gmemset(t["ones_b"], 1.0)
        ctx.gmemset(t["ones_r32"], 1.0)
        ctx.gmemset(t["tmask"], 1.0)
        ctx.gmemset(t["tmask32"], 1.0)
        ctx.gmemset(t["ident"], 1.0)
        ctx.gaffsel(t["ident"], t["ident"], [[-1, 128]], ALU.is_equal, 0.0, 0)
        ctx.gaffsel(t["tmask"], t["tmask"], [[1, 896]], ALU.is_ge, 0.0, -384,
                    chm=-1)
        ctx.gaffsel(t["tmask32"], t["tmask32"], [[1, 128]], ALU.is_ge, 0.0, 0,
                    chm=-1)
        for name, dst in (("b_out", "bout"), ("bfv", "bfs"), ("biv", "bis"),
                          ("bcv", "bcs"), ("bov", "bos"), ("incv", "incs")):
            ctx.dma("sync", t[dst], d[name].rearrange("(c p) -> p c", p=128),
                    slow=True)
        ctx.dma("sync", t["normw_s"], d["normw"][:, None])


    def emit_wstage(ms):
        for m in ms:
            w = ("wfT", "ufT", "wiT", "uiT", "wcT", "ucT",
                 "wobT", "uoT", "wqT", "wkT", "wvT", "woT")[m]
            for oc in range(8):
                src = d[w][:, oc * 128:(oc + 1) * 128].rearrange(
                    "(dc p) o -> p dc o", p=128)
                dst = wst_fm(m, oc).rearrange("p (dc o) -> p dc o", dc=8)
                ctx.dma("gpsimd", dst, src)

    def emit_attn_a():
        for ct in range(CT):
            b, g_ = ct // 2, ct % 2
            colbase = ct * 512
            xo = t["xold"][:, (ct % 2) * COLS:(ct % 2) * COLS + COLS]
            ldtile("sync", xo, d["x0"][:, colbase:colbase + 512])
            for dc in range(NCH):
                ln_stats_chunk(xo[:, dc * 512:(dc + 1) * 512], dc, ct)
            ln_finalize()
            ht = t["htile"][:, (ct % 2) * COLS:(ct % 2) * COLS + COLS]
            for dc in range(NCH):
                ln_apply_chunk(xo[:, dc * 512:(dc + 1) * 512],
                               ht[:, dc * 512:(dc + 1) * 512], par=dc)
            for oc in range(8):
                off = (oc % 2) * 1024
                load_wcast("gpsimd", d["wqkvT"], oc * 128, 128, off)
                bank = ps[oc % 2]
                ctx.mm_group(bank, [
                    (wslot(off + dc * 128, 128), ht[:, dc * 512:(dc + 1) * 512])
                    for dc in range(NCH)])
                if oc < 4:
                    qd = t["q_a"][:, oc * COLS + colbase: oc * COLS + colbase + 512]
                    ctx.vcopy(qd, bank)
                    sqs = seg("sq", (oc % 2) * 512, (oc % 2) * 512 + 512)
                    ctx.act(sqs, qd, AF.Square)
                    ctx.mm_acc(ps[6][0:1, :], t["ones_b"], sqs, start=(oc == 0),
                               stop=(oc == 3))
                    if oc == 3:
                        ctx.vreduce("reduce_max",
                                    t["qmaxg"][0:1, ct:ct + 1], ps[6][0:1, :])
                else:
                    ko = oc - 4
                    kp = t["hprod"][:, (oc % 2) * 512:(oc % 2) * 512 + 512]
                    ctx.vcopy(kp, bank)
                    ctx.dma("scalar",
                            d["ka_pub"][b * 513 + ko * 128:b * 513 + (ko + 1) * 128,
                                        g_ * 512:g_ * 512 + 512], kp)
                    sqs = seg("sq", (oc % 2) * 512, (oc % 2) * 512 + 512)
                    ctx.act(sqs, kp, AF.Square)
                    ctx.mm_acc(ps[6][0:1, :], t["ones_b"], sqs, start=(ko == 0),
                               stop=(ko == 3))
                    if ko == 3:
                        kr = t["rows"][0:1, 1536:2048]
                        ctx.vcopy(kr, ps[6][0:1, :])
                        ctx.s_copy(t["kn"][0:1, 0:512], kr)
                        ctx.dma("scalar",
                                d["ka_pub"][b * 513 + 512:b * 513 + 513,
                                            g_ * 512:g_ * 512 + 512],
                                t["kn"][0:1, 0:512])
            load_wcast("gpsimd", d["wqkvT"], 1024, 256, 2048)
            load_wcast("gpsimd", d["wqkvT"], 1280, 256, 4096)
            for pc in range(4):
                bank = ps[pc % 2]
                for half in range(2):
                    ctx.mm_group(bank[:, half * 256:(half + 1) * 256], [
                        (ht[:, dc * 512 + pc * 128: dc * 512 + (pc + 1) * 128],
                         wslot(2048 + half * 2048 + dc * 256, 256))
                        for dc in range(NCH)])
                vp = t["hprod"][:, (pc % 2) * 512:(pc % 2) * 512 + 512]
                ctx.vcopy(vp, bank)
                ctx.dma("scalar", d["va_pub"][colbase + pc * 128:
                                              colbase + (pc + 1) * 128, :], vp)
            if g_ == 1:
                ctx.cc("AllGather", d["ka_pub"][b * 513:(b + 1) * 513, :],
                       d[f"ka_gat{b}"])
                ctx.cc("AllGather", d["va_pub"][b * 1024:(b + 1) * 1024, :],
                       d[f"va_gat{b}"])
        emit_wstage(range(8))
        for b in range(B):
            for r_ in range(NCORE):
                sl = t["kn"][0:1, (r_ % 2) * 1024:(r_ % 2) * 1024 + 1024]
                ctx.dma("sync", sl,
                        d[f"ka_gat{b}"][r_ * 513 + 512:r_ * 513 + 513, :])
                for half in range(2):
                    j = r_ if half == 0 else 15 - r_
                    ctx.vreduce("reduce_max",
                                t["kmaxb"][0:1, b * 16 + j:b * 16 + j + 1],
                                sl[0:1, half * 512:(half + 1) * 512])
        if TAP == "dbg":
            ctx.vmemset(t["xnew"], 0.0)
        for b in range(B):
            ctx.core_switch(f"attn_a_b{b}",
                            lambda cid, b=b: attn_a_body(cid, b))
            if TAP == "dbg":
                return
            for g in range(2):
                emit_attn_a_epilogue(b, g)

    def attn_a_body(cid, b):
        for g, bl in enumerate(core_blocks(cid)):
            attn_a_group(cid, b, g, bl)

    def attn_a_group(cid, b, g, bl):
        colbase = b * 1024 + g * 512
        nkc = (bl + 1) * 4
        qmax = t["rows"][0:1, 0:1]
        kmax = t["rows"][0:1, 1:2]
        ctx.vreduce("reduce_max", kmax, t["kmaxb"][0:1, b * 16:b * 16 + bl + 1])
        ctx.vtt("tensor_mul", qmax, t["qmaxg"][0:1, 2 * b + g:2 * b + g + 1],
                kmax)
        ctx.act(qmax, qmax, AF.Ln)
        ctx.act(qmax, qmax, AF.Exp, scale=0.5)
        ctx.op("scalar", "mul", [qmax], [qmax], qmax, qmax, -ATT_SCALE)
        ctx.mm_group(ps[2][:, 0:1], [(t["ones_r32"], qmax)])
        nb = t["negb"][:, g:g + 1]
        ctx.vcopy(nb, ps[2][:, 0:1])

        def kblk_slot(j):
            return t["kblk"][:, (j % 2) * 2048:(j % 2) * 2048 + 2048]

        def kblk_load(j):
            r_ = j if j < 8 else 15 - j
            off = 0 if j < 8 else 512
            ldtile("sync", kblk_slot(j),
                   d[f"ka_gat{b}"][r_ * 513:r_ * 513 + 512, off:off + 512], c=4)

        def vt_slot(i):
            return t["vtile"][:, (i % 3) * 512:(i % 3) * 512 + 512]

        def vt_load(kc):
            j = kc // 4
            r_ = j if j < 8 else 15 - j
            row = r_ * 1024 + (0 if j < 8 else 512) + (kc % 4) * 128
            ctx.dma("gpsimd", vt_slot(kc), d[f"va_gat{b}"][row:row + 128, :])

        kblk_load(0)
        if nkc > 4:
            kblk_load(1)
        for i in range(min(3, nkc)):
            vt_load(i)

        for kc in range(nkc):
            j = kc // 4
            sbank = ps[kc % 2]
            et = t["etile"][:, (kc % 2) * 512:(kc % 2) * 512 + 512]
            ctx.mm_group(sbank, [
                (kblk_slot(j)[:, dc * 512 + (kc % 4) * 128:
                              dc * 512 + (kc % 4) * 128 + 128],
                 t["q_a"][:, dc * COLS + colbase: dc * COLS + colbase + 512])
                for dc in range(4)])
            ctx.act(et, sbank, AF.Exp, bias=nb, scale=ATT_SCALE)
            if kc >= nkc - 4:
                o_ = (kc - (nkc - 4)) * 128
                msl = t["tmask"][:, 384 - o_:384 - o_ + 512]
                ctx.vtt("tensor_mul", et, et, msl)
            if TAP == "dbg" and DBG_DUMP and b == 0 and kc == 0:
                xn = t["xnew"]
                ctx.vcopy(xn[:, g * 2048 + 0:g * 2048 + 512], sbank)
                ctx.vcopy(xn[:, g * 2048 + 512:g * 2048 + 1024], et)
                ctx.vcopy(xn[:, g * 2048 + 1024:g * 2048 + 1152],
                          kblk_slot(0)[:, 0:128])
                ctx.vcopy(xn[:, g * 2048 + 1152:g * 2048 + 1280],
                          t["q_a"][:, colbase:colbase + 128])
                ctx.vcopy(xn[:, g * 2048 + 1280:g * 2048 + 1281], nb)
                ctx.vcopy(xn[0:1, g * 2048 + 1300:g * 2048 + 1308],
                          t["qmaxg"][0:1, :])
                ctx.vcopy(xn[0:1, g * 2048 + 1308:g * 2048 + 1372],
                          t["kmaxb"][0:1, :])
                ctx.vcopy(xn[:, g * 2048 + 1408:g * 2048 + 1920],
                          vt_slot(0))
            def pv_emit(kcp):
                etp = t["etile"][:, (kcp % 2) * 512:(kcp % 2) * 512 + 512]
                ctx.mm_acc(ps[2][0:1, :], t["ones_b"], etp, start=(kcp == 0),
                           stop=(kcp == nkc - 1))
                for qt in range(4):
                    ctx.mm_acc(ps[3 + qt], etp[:, qt * 128:(qt + 1) * 128],
                               vt_slot(kcp), start=(kcp == 0),
                               stop=(kcp == nkc - 1))
            if kc >= 1:
                pv_emit(kc - 1)
                if kc + 2 < nkc:
                    vt_load(kc + 2)
            if kc == nkc - 1:
                pv_emit(kc)
            if kc % 4 == 3 and j + 2 <= bl:
                kblk_load(j + 2)
        se = t["rows"][0:1, 0:512]
        rs = t["rows"][0:1, 512:1024]
        ctx.vcopy(se, ps[2][0:1, :])
        ctx.vts("tensor_scalar_add", se, se, 1e-30)
        ctx.vrecip(rs, se)
        for qt in range(4):
            ctx.tr(ps[2][:, qt:qt + 1], rs[0:1, qt * 128:(qt + 1) * 128],
                   t["ones_r32"][0:1, 0:1])
            ctx.vcopy(t["rscol"][:, qt:qt + 1], ps[2][:, qt:qt + 1])
            ctx.vts("tensor_scalar_mul", t["ao_tm"][:, qt * 512:(qt + 1) * 512],
                    ps[3 + qt], t["rscol"][:, qt:qt + 1])
        for qt in range(4):
            for vc in range(4):
                pslot = psbf[:, (qt % 2) * 512 + vc * 128:
                             (qt % 2) * 512 + (vc + 1) * 128]
                ctx.tr(pslot, t["ao_tm"][:, qt * 512 + vc * 128:
                                         qt * 512 + (vc + 1) * 128], t["ident"])
                ctx.vcopy(t["ao_fm"][:, g * 2048 + vc * 512 + qt * 128:
                                     g * 2048 + vc * 512 + (qt + 1) * 128], pslot)

    def emit_attn_a_epilogue(b, g):
        colbase = b * 1024 + g * 512

        def y_provider(dc):
            off = 6144 + (dc % 2) * 512
            src = d["woutT"][:, dc * 128:(dc + 1) * 128].rearrange(
                "(c p) o -> p c o", p=128)
            ctx.dma("gpsimd", wslot(off, 512).rearrange("p (c o) -> p c o", c=4),
                    src)
            bank = ps[dc % 2]
            ctx.mm_group(bank, [
                (wslot(off + vc * 128, 128),
                 t["ao_fm"][:, g * 2048 + vc * 512:g * 2048 + (vc + 1) * 512])
                for vc in range(4)])
            return bank, t["bout"][:, dc:dc + 1]

        emit_epilogue_tile(colbase, y_provider, d["x0"], d["x1"],
                           d["ha"] if TAP != "x1" else None)

    def emit_ffn(call):
        ctx.fence()
        if call == 0:
            emit_wstage((8, 9, 10, 11))
        x_in = d["x1"] if call == 0 else d["x3"]
        x_out = d["x2"] if call == 0 else d["y"]
        h_in = d["ha"]
        h_out = d["hb"] if (call == 0 and TAP != "x2") else None
        cc8 = [t["c8"][:, i * 1024:(i + 1) * 1024] for i in range(NCH)]
        cbf = [t["cbf"][:, i * 1024:(i + 1) * 1024] for i in range(NCH)]

        def htslot(i):
            return t["htf"][:, (i % 2) * 8192:(i % 2) * 8192 + 8192]

        def ht_load(i, tstep):
            ldtile("sync", htslot(i), h_in[:, tstep * 1024:(tstep + 1) * 1024])

        for dc in range(NCH):
            ctx.vmemset(cc8[dc], 0.0)
            ctx.op("scalar", "add", [cc8[dc]], [cc8[dc], t["incs"]],
                   cc8[dc], cc8[dc], t["incs"][:, dc:dc + 1])
        ht_load(0, 0)
        GW = {0: ("bfs", AF.Sigmoid, 0), 1: ("bis", AF.Sigmoid, 2),
              2: ("bcs", AF.Tanh, 4)}
        for tstep in range(4):
            if tstep < 3:
                ht_load(tstep + 1, tstep + 1)
            ht = htslot(tstep)
            for dc in range(NCH):
                ctx.vcopy(cbf[dc], cc8[dc])
            for oc in range(NCH):
                wb = (oc % 2) * 6144
                gb_ = (oc % 2) * 3072
                if tstep == 0 or True:
                    ld_oc = oc + 1
                    if oc == 0:
                        for g3 in range(3):
                            ctx.dma("sync", wslot(wb + g3 * 2048, 1024),
                                    wst_fm(2 * g3, 0))
                            ctx.dma("sync", wslot(wb + g3 * 2048 + 1024, 1024),
                                    wst_fm(2 * g3 + 1, 0))
                    if ld_oc < NCH:
                        wbn = (ld_oc % 2) * 6144
                        for g3 in range(3):
                            ctx.dma("sync", wslot(wbn + g3 * 2048, 1024),
                                    wst_fm(2 * g3, ld_oc))
                            ctx.dma("sync", wslot(wbn + g3 * 2048 + 1024, 1024),
                                    wst_fm(2 * g3 + 1, ld_oc))
                for g3 in range(3):
                    bname, afn, bk = GW[g3]
                    for ph in range(2):
                        bank = ps[bk + ph]
                        ctx.mm_group(bank, [
                            (wslot(wb + g3 * 2048 + dc * 128, 128),
                             ht[:, dc * 1024 + ph * 512: dc * 1024 + ph * 512 + 512])
                            for dc in range(NCH)] + [
                            (wslot(wb + g3 * 2048 + 1024 + dc * 128, 128),
                             cbf[dc][:, ph * 512:(ph + 1) * 512])
                            for dc in range(NCH)])
                        gt = t["gates"][:, gb_ + (g3 * 2 + ph) * 512:
                                        gb_ + (g3 * 2 + ph + 1) * 512]
                        ctx.act(gt, bank, afn, bias=t[bname][:, oc:oc + 1])
                for ph in range(2):
                    fg = t["gates"][:, gb_ + ph * 512:gb_ + (ph + 1) * 512]
                    ig = t["gates"][:, gb_ + (2 + ph) * 512:gb_ + (3 + ph) * 512]
                    gg = t["gates"][:, gb_ + (4 + ph) * 512:gb_ + (5 + ph) * 512]
                    cs = cc8[oc][:, ph * 512:(ph + 1) * 512]
                    scr = t["tscr"][:, ph * 512:(ph + 1) * 512]
                    ctx.vtt("tensor_mul", scr, ig, gg)
                    ctx.vtt("tensor_mul", cs, fg, cs)
                    ctx.vtt("tensor_add", cs, cs, scr)
        for dc in range(NCH):
            ctx.vcopy(cbf[dc], cc8[dc])
        for oc in range(NCH):
            ub = (oc % 2) * 1024
            ctx.dma("sync", wslot(ub, 1024), wst_fm(7, oc))
            for ph in range(2):
                bank = ps[oc % 2]
                ctx.mm_group(bank, [
                    (wslot(ub + dc * 128, 128),
                     cbf[dc][:, ph * 512:(ph + 1) * 512])
                    for dc in range(NCH)])
                ctx.vcopy(cc8[oc][:, ph * 512:(ph + 1) * 512], bank)
        for tstep in range(4):
            ht_load(tstep, tstep)
            ht = htslot(tstep)
            for ph in range(2):
                def y_provider(dc, tstep=tstep, ph=ph, ht=ht):
                    ctx.dma("sync", wslot(2048 + (dc % 2) * 1024, 1024),
                            wst_fm(6, dc))
                    bank = ps[dc % 2]
                    ctx.mm_group(bank, [
                        (wslot(2048 + (dc % 2) * 1024 + k_ * 128, 128),
                         ht[:, k_ * 1024 + ph * 512: k_ * 1024 + ph * 512 + 512])
                        for k_ in range(NCH)])
                    ys = t["gates"][:, (dc % 2) * 512:(dc % 2) * 512 + 512]
                    ctx.vstt(ys, bank, t["bos"][:, dc:dc + 1],
                             cc8[dc][:, ph * 512:(ph + 1) * 512], ALU.add, ALU.add)
                    ctx.act(ys, ys, AF.Tanh)
                    return ys, None
                emit_epilogue_tile(tstep * 1024 + ph * 512, y_provider,
                                   x_in, x_out, h_out)

    def emit_attn_b():
        ctx.fence()
        ctx.vmemset(t["zcx"][:, 256:257], 0.0)
        h_in = d["hb"]
        for g in range(2):
            if g == 1:
                ctx.cc("AllGather", d["bs_pub"][0:4096, :], d["bs_gat_a"])
            for b in range(B):
                colbase = b * 1024 + g * 512
                cti = g * 4 + b
                ht = t["htileb"][:, (cti % 2) * COLS:(cti % 2) * COLS + COLS]
                ldtile("sync", ht, h_in[:, colbase:colbase + 512])
                for mi, (m, dstd) in enumerate(((8, d["qb"]), (9, d["kb"]))):
                    for oc in range(NCH):
                        woff = mi * 2048 + (oc % 2) * 1024
                        ctx.dma("sync", wslot(woff, 1024), wst_fm(m, oc))
                        bank = ps[oc % 2]
                        ctx.mm_group(bank, [
                            (wslot(woff + dc * 128, 128),
                             ht[:, dc * 512:(dc + 1) * 512])
                            for dc in range(NCH)])
                        pp = t["hprod"][:, (oc % 2) * 512:(oc % 2) * 512 + 512]
                        ctx.vcopy(pp, bank)
                        ctx.dma("scalar", dstd[oc * 128:(oc + 1) * 128,
                                               colbase:colbase + 512], pp)
                        if m == 9:
                            base = (cti * 8 + oc) * 4
                            for pc in range(4):
                                ctx.vreduce(
                                    "reduce_sum",
                                    t["kcs"][:, base + pc:base + pc + 1],
                                    pp[:, pc * 128:(pc + 1) * 128])
                            ctx.vcopy(t["zcx"][:, base + 1:base + 2],
                                      t["kcs"][:, base:base + 1])
                            ctx.vtt("tensor_add", t["zrun"],
                                    t["kcs"][:, base:base + 1],
                                    t["kcs"][:, base + 1:base + 2])
                            ctx.vcopy(t["zcx"][:, base + 2:base + 3], t["zrun"])
                            ctx.vtt("tensor_add", t["zrun"], t["zrun"],
                                    t["kcs"][:, base + 2:base + 3])
                            ctx.vcopy(t["zcx"][:, base + 3:base + 4], t["zrun"])
                for mi, (m, dstd) in enumerate(((9, d["kbt"]), (10, d["vbt"]))):
                    for half in range(2):
                        wsl = wslot(4096 + half * 4096, 4096)
                        rows = d[f"wst{m}"][(half * 4) * 128:
                                            (half * 4 + 4) * 128, :]
                        ctx.dma("sync",
                                wsl.rearrange("p (dc oc o) -> p dc oc o",
                                              dc=8, oc=4),
                                rows.rearrange("(oc p) (dc o) -> p dc oc o",
                                               p=128, dc=8))
                        for pc in range(4):
                            bank = ps[pc % 2]
                            ctx.mm_group(bank, [
                                (ht[:, dc * 512 + pc * 128:
                                    dc * 512 + (pc + 1) * 128],
                                 wsl[:, dc * 512:(dc + 1) * 512])
                                for dc in range(NCH)])
                            pdst = t["ptile"][:, mi * 4096 + pc * 1024 +
                                              half * 512: mi * 4096 + pc * 1024 +
                                              half * 512 + 512]
                            ctx.vcopy(pdst, bank)
                    for pc in range(4):
                        ctx.dma("scalar",
                                dstd[colbase + pc * 128:colbase + (pc + 1) * 128, :],
                                t["ptile"][:, mi * 4096 + pc * 1024:
                                           mi * 4096 + (pc + 1) * 1024])
                for h in range(8):
                    ctx.mm_group(ps[6][:, 0:128], [
                        (t["ptile"][:, pc * 1024 + h * 128:
                                    pc * 1024 + (h + 1) * 128],
                         t["ptile"][:, 4096 + pc * 1024 + h * 128:
                                    4096 + pc * 1024 + (h + 1) * 128])
                        for pc in range(4)], )
                    bst = t["tscr"]
                    ctx.vcopy(bst[:, 0:128], ps[6][:, 0:128])
                    ctx.vreduce("reduce_sum", bst[:, 128:129],
                                t["kcs"][:, (cti * 8 + h) * 4:(cti * 8 + h) * 4 + 4])
                    rowb = ((g * 4 + b) * 8 + h) * 128
                    ctx.dma("gpsimd", d["bs_pub"][rowb:rowb + 128, 0:129],
                            bst[:, 0:129])
        ctx.cc("AllGather", d["bs_pub"][4096:8192, :], d["bs_gat_b"])
        for _tv in [t["sT"]] + [t[f"spfx{_b}"] for _b in range(B)]:
            ctx.seed_alias(_tv, t["ptile"])
            ctx.seed_alias(_tv, t["htileb"])

        def pfx(b):
            ctx.core_switch(f"attn_b_pfx{b}",
                            lambda cid, b=b: attn_b_pfx_body(cid, b))
            ctx.op("gpsimd", "tensor_copy",
                   [t["kpfb"][:, b * 16:(b + 1) * 16]], [t[f"kpfx{b}"]],
                   t["kpfb"][:, b * 16:(b + 1) * 16], t[f"kpfx{b}"])
        pfx(0)
        pfx(1)
        for b in range(B):
            for h in range(8):
                attn_b_B1(b, h)
            if b + 2 < B:
                pfx(b + 2)
            for hp in range(4):
                attn_b_B2(b, hp)
            emit_attn_b_project(b)

    def attn_b_pfx_body(cid, b):
        bl0, bl1 = core_blocks(cid)
        for h in range(8):
            sl8 = (h % 4) * 129
            mst = t["pfxm"][:, sl8:sl8 + 129]
            ctx.gmemset(mst, 0.0)
            for j in range(16):
                for gi, bl in enumerate((bl0, bl1)):
                    if j == bl:
                        idx = h * 2 + gi
                        ctx.op("gpsimd", "tensor_copy",
                               [t[f"spfx{b}"][:, idx * 128:(idx + 1) * 128]],
                               [mst],
                               t[f"spfx{b}"][:, idx * 128:(idx + 1) * 128],
                               mst[:, 0:128])
                        ctx.op("gpsimd", "tensor_copy",
                               [t[f"kpfx{b}"][:, idx:idx + 1]], [mst],
                               t[f"kpfx{b}"][:, idx:idx + 1], mst[:, 128:129])
                if j < 15:
                    r_ = j if j < 8 else 15 - j
                    gat = d["bs_gat_a"] if j < 8 else d["bs_gat_b"]
                    rowb = r_ * 4096 + (b * 8 + h) * 128
                    sl = t["bsring"][:, (j % 5) * 129:(j % 5) * 129 + 129]
                    ctx.dma("gpsimd", sl, gat[rowb:rowb + 128, 0:129])
                    ctx.op("gpsimd", "tensor_add", [mst], [mst, sl],
                           mst, mst, sl)

    def attn_b_B1(b, h):
        qs = t["stq"][:, (h % 2) * 1024:(h % 2) * 1024 + 1024]
        ks = t["stk"][:, (h % 2) * 1024:(h % 2) * 1024 + 1024]
        ctx.dma("sync", qs, d["qb"][h * 128:(h + 1) * 128,
                                    b * 1024:(b + 1) * 1024])
        ctx.dma("sync", ks, d["kb"][h * 128:(h + 1) * 128,
                                    b * 1024:(b + 1) * 1024])
        SB_ = (ps[0], ps[1], ps[3], ps[4])
        for cc8 in range(8):
            sbank = SB_[cc8 % 4]
            ctx.mm_group(sbank[:, 0:128], [
                (ks[:, cc8 * 128:(cc8 + 1) * 128],
                 qs[:, cc8 * 128:(cc8 + 1) * 128])])
            stile = t["sT"][:, (h * 8 + cc8) * 128:(h * 8 + cc8 + 1) * 128]
            ctx.vtt("tensor_mul", stile, sbank[:, 0:128], t["tmask32"])
        for cc8 in range(8):
            g = cc8 // 4
            cti = g * 4 + b
            stile = t["sT"][:, (h * 8 + cc8) * 128:(h * 8 + cc8 + 1) * 128]
            kcol = t["kpfb"][:, b * 16 + h * 2 + g:b * 16 + h * 2 + g + 1]
            pc = cc8 % 4
            zcol = (t["zcx"][:, 256:257] if pc == 0 else
                    t["zcx"][:, (cti * 8 + h) * 4 + pc:
                             (cti * 8 + h) * 4 + pc + 1])
            dbank = ps[2] if cc8 < 4 else ps[6]
            ctx.mm_group(dbank[0:1, (cc8 % 4) * 128:(cc8 % 4 + 1) * 128], [
                (kcol, qs[:, cc8 * 128:(cc8 + 1) * 128]),
                (zcol, qs[:, cc8 * 128:(cc8 + 1) * 128]),
                (t["ones_b"], stile)])
        for half, bank in ((0, ps[2]), (1, ps[6])):
            tmp = t["rows"][0:1, 0:512]
            ctx.act(tmp, bank[0:1, 0:512], AF.Abs)
            ctx.vts("tensor_scalar_max",
                    t["recd"][0:1, h * 1024 + half * 512:
                              h * 1024 + half * 512 + 512], tmp, 1.0)

    O_BANK = {0: 5, 1: 0}
    MS_BANK = {0: 4, 1: 2}
    ST_BANK = {0: 3, 1: 6}

    def attn_b_B2(b, hp):
        hs = (hp * 2, hp * 2 + 1)
        qsl, ktl, vtl, sol, osl = {}, {}, {}, {}, {}
        for s_, h in enumerate(hs):
            qsl[s_] = t["stq"][:, s_ * 1024:(s_ + 1) * 1024]
            ctx.dma("sync", qsl[s_], d["qb"][h * 128:(h + 1) * 128,
                                             b * 1024:(b + 1) * 1024])
            ktl[s_] = t["stkt"][:, s_ * 1024:(s_ + 1) * 1024]
            ctx.dma("gpsimd", ktl[s_].rearrange("p (pc e) -> p pc e", pc=8),
                    d["kbt"][b * 1024:(b + 1) * 1024,
                             h * 128:(h + 1) * 128].rearrange(
                        "(pc p) e -> p pc e", p=128))
            vtl[s_] = t["stvt"][:, s_ * 1024:(s_ + 1) * 1024]
            ctx.dma("gpsimd", vtl[s_].rearrange("p (pc e) -> p pc e", pc=8),
                    d["vbt"][b * 1024:(b + 1) * 1024,
                             h * 128:(h + 1) * 128].rearrange(
                        "(pc p) e -> p pc e", p=128))
            sol[s_] = t["sop"][:, s_ * 128:(s_ + 1) * 128]
            osl[s_] = t["osb"][:, s_ * 1024:(s_ + 1) * 1024]
        stm = {0: t["stmf"][:, 0:128], 1: t["stmf"][:, 128:256]}
        for cc8 in range(8):
            for s_, h in enumerate(hs):
                stb = ps[ST_BANK[s_]]
                ob = ps[O_BANK[s_]]
                msb = ps[MS_BANK[s_]]
                if cc8 % 4 == 0:
                    g = cc8 // 4
                    idx = h * 2 + g
                    ctx.vcopy(stm[s_],
                              t[f"spfx{b}"][:, idx * 128:(idx + 1) * 128])
                ctx.vcopy(sol[s_], stm[s_])
                ctx.mm_group(ob[:, 0:128], [
                    (sol[s_], qsl[s_][:, cc8 * 128:(cc8 + 1) * 128]),
                    (vtl[s_][:, cc8 * 128:(cc8 + 1) * 128],
                     t["sT"][:, (h * 8 + cc8) * 128:(h * 8 + cc8 + 1) * 128])])
                ctx.vcopy(osl[s_][:, cc8 * 128:(cc8 + 1) * 128], ob[:, 0:128])
                if cc8 % 4 != 3:
                    ctx.mm_group(stb[:, 0:128],
                                 [(ktl[s_][:, cc8 * 128:(cc8 + 1) * 128],
                                   vtl[s_][:, cc8 * 128:(cc8 + 1) * 128])])
                    ctx.vtt("tensor_add", stm[s_], stm[s_], stb[:, 0:128])

        for s_, h in enumerate(hs):
            msb = ps[MS_BANK[s_]]
            for g in range(2):
                sqs = t["sq"][:, ((s_ * 2 + g) % 2) * 512:
                              ((s_ * 2 + g) % 2) * 512 + 512]
                ctx.act(sqs, osl[s_][:, g * 512:(g + 1) * 512], AF.Square)
                ctx.mm_group(msb[0:1, 0:512], [(t["ones_b"], sqs)])
                ctx.vcopy(t["rows"][0:1, 2048 + s_ * 1024 + g * 512:
                                    2048 + s_ * 1024 + (g + 1) * 512],
                          msb[0:1, 0:512])
        for s_, h in enumerate(hs):
            dd = t["rows"][0:1, 0:1024]
            rd = t["recd"][0:1, h * 1024:(h + 1) * 1024]
            ctx.vtt("tensor_mul", dd, rd, rd)
            ctx.vts("tensor_scalar_mul", dd, dd, EPS)
            ctx.vstt(t["rows"][0:1, 1024:2048],
                     t["rows"][0:1, 2048 + s_ * 1024:2048 + (s_ + 1) * 1024],
                     1.0 / 128.0, dd, ALU.mult, ALU.add)
            al = t["rows"][0:1, 1024:2048]
            ctx.act(al, al, AF.Ln)
            ctx.act(al, al, AF.Exp, scale=-0.5)
            for g in range(2):
                bcb = ps[1] if g == 0 else ps[O_BANK[0]]
                ctx.mm_group(bcb, [(t["ones_r32"],
                                    al[0:1, g * 512:(g + 1) * 512])])
                ctx.vstt(t["onorm"][:, h * 1024 + g * 512:
                                    h * 1024 + (g + 1) * 512],
                         osl[s_][:, g * 512:(g + 1) * 512], t["normw_s"],
                         bcb, ALU.mult, ALU.mult)

    def emit_attn_b_project(b):
        for g in range(2):
            colbase = b * 1024 + g * 512

            def y_provider(dc, g=g):
                ctx.dma("sync", wslot((dc % 2) * 1024, 1024), wst_fm(11, dc))
                bank = ps[dc % 2]
                ctx.mm_group(bank, [
                    (wslot((dc % 2) * 1024 + k_ * 128, 128),
                     t["onorm"][:, k_ * 1024 + g * 512: k_ * 1024 + g * 512 + 512])
                    for k_ in range(NCH)])
                return bank, None

            emit_epilogue_tile(colbase, y_provider, d["x2"], d["x3"],
                               d["ha"] if TAP != "x3" else None)

    def emit_tap_x(src_dram):
        for ct in range(CT):
            xo = t["xold"][:, (ct % 2) * COLS:(ct % 2) * COLS + COLS]
            ldtile("sync", xo, src_dram[:, ct * 512:(ct + 1) * 512])
            sttile("gpsimd", xo, d["y"][:, ct * 512:(ct + 1) * 512])

    def emit_final_wait():
        for sem, val in sorted(d["y"].res.w.items()):
            if not ctx.dry and ctx.cur == "gpsimd":
                ctx.eng.wait_ge(ctx.handles[sem], val)

    def emit_program():
        epi_seq[0] = 0
        emit_consts()
        emit_attn_a()
        if TAP == "dbg":
            sttile("gpsimd", t["xnew"], d["y"][:, 0:512])
            emit_final_wait()
            return
        if TAP == "x1":
            emit_tap_x(d["x1"])
            emit_final_wait()
            return
        emit_ffn(0)
        if TAP == "x2":
            emit_tap_x(d["x2"])
            emit_final_wait()
            return
        emit_attn_b()
        if TAP == "x3":
            if TAPD is not None:
                src = d[TAPD]
                rows, cols = src.ap.shape
                if (rows, cols) == (DM, COLS):
                    ydst = d["y"]
                else:
                    ydst = d["y"].rearrange("a (b c) -> (a b) c", b=4)
                ctx.dma("gpsimd", ydst, src)
                emit_final_wait()
                return
            emit_tap_x(d["x3"])
            emit_final_wait()
            return
        emit_ffn(1)
        emit_final_wait()

    ctx.dry = True
    emit_program()
    ctx.dry = False
    with nc.Block() as block:
        def run_for(name):
            def _(eng):
                ctx.cur = name
                ctx.eng = eng
                ctx.pid = None
                ctx.counts = {k: 0 for k in SEMS}
                ctx.wm = {e: {s: 0 for s in SEMS} for e in ENGINES}
                ctx.dma_idx = {q: 0 for q in DMA_PREFIX}
                for r_ in ctx.registry:
                    r_.w, r_.r = {}, {}
                emit_program()
            return _
        block.tensor(run_for("tensor"))
        block.vector(run_for("vector"))
        block.scalar(run_for("scalar"))
        block.sync(run_for("sync"))
        block.gpsimd(run_for("gpsimd"))

    for g in reversed(mgrs):
        g.__exit__(None, None, None)
    return nc


def stage_inputs(inp):
    x0 = np.asarray(inp["L0_attn_a_x"])
    xT = np.transpose(x0, (2, 0, 1))
    tr = lambda k: np.ascontiguousarray(np.asarray(inp[k]).T)
    ar = lambda k: np.ascontiguousarray(np.asarray(inp[k]))
    shared = {
        "wqkvT": tr("L0_attn_a_w_qkv"), "woutT": tr("L0_attn_a_w_out"),
        "b_out": ar("L0_attn_a_b_out"),
        "wqT": tr("L0_attn_b_Wq"), "wkT": tr("L0_attn_b_Wk"),
        "wvT": tr("L0_attn_b_Wv"), "woT": tr("L0_attn_b_Wo"),
        "normw": ar("L0_attn_b_norm_w"),
        "wfT": tr("L0_ffn_wf"), "ufT": tr("L0_ffn_uf"),
        "wiT": tr("L0_ffn_wi"), "uiT": tr("L0_ffn_ui"),
        "wcT": tr("L0_ffn_wc"), "ucT": tr("L0_ffn_uc"),
        "wobT": tr("L0_ffn_wo"), "uoT": tr("L0_ffn_uo"),
        "bfv": ar("L0_ffn_bf"), "biv": ar("L0_ffn_bi"), "bcv": ar("L0_ffn_bc"),
        "bov": ar("L0_ffn_bo"), "incv": ar("L0_ffn_in_c"),
    }
    in_maps = []
    for c in range(NCORE):
        b0, b1 = core_blocks(c)
        xc = np.concatenate([xT[:, :, b0 * 512:(b0 + 1) * 512],
                             xT[:, :, b1 * 512:(b1 + 1) * 512]], axis=2)
        m = dict(shared)
        m["x0"] = np.ascontiguousarray(xc.reshape(DM, COLS))
        in_maps.append(m)
    return in_maps


def unstage_output(results):
    out = np.empty((DM, B, S), np.float32)
    for c in range(NCORE):
        b0, b1 = core_blocks(c)
        yc = results[c]["y"].reshape(DM, B, 1024)
        out[:, :, b0 * 512:(b0 + 1) * 512] = yc[:, :, 0:512]
        out[:, :, b1 * 512:(b1 + 1) * 512] = yc[:, :, 512:1024]
    return np.ascontiguousarray(np.transpose(out, (1, 2, 0)))


def kernel(**inp):
    in_maps = stage_inputs(inp)
    nc = build_program()
    res = run_bass_kernel_spmd(nc, in_maps, core_ids=list(range(NCORE)))
    return unstage_output(res.results)


# version 04 (final)
# speedup vs baseline: 1.1902x; 1.1001x over previous
import numpy as np
import concourse.bass as bass
import concourse.mybir as mybir
from concourse.bass_utils import run_bass_kernel_spmd

F32 = mybir.dt.float32
BF16 = mybir.dt.bfloat16
AF = mybir.ActivationFunctionType
ALU = mybir.AluOpType
AX = mybir.AxisListType

NCORE = 8
B = 4
S = 8192
DM = 1024
COLS = 4096
NCH = DM // 128
CT = COLS // 512
EPS = 1e-5
ATT_SCALE = float(512 ** -0.5)

TAP = None
TAPD = None
DBG_DUMP = False
STRICT = False


def core_blocks(c):
    return (c, 15 - c)


ENGINES = ("tensor", "vector", "scalar", "sync", "gpsimd")
NLANE = 8
SEMS = {
    "T": ("tensor", 1), "V": ("vector", 1), "SC": ("scalar", 1),
    "G": ("gpsimd", 1), "CC": ("gpsimd", 1),
}
DMA_PREFIX = {"sync": "DY", "gpsimd": "DG", "scalar": "DA"}
for _q, _p in DMA_PREFIX.items():
    for _i in range(NLANE):
        SEMS[f"{_p}{_i}"] = (_q, 16)
ENG_COMPUTE_SEM = {"tensor": "T", "vector": "V", "scalar": "SC", "gpsimd": "G"}


class R:
    __slots__ = ("w", "r", "sbuf")

    def __init__(self, sbuf=False):
        self.w = {}
        self.r = {}
        self.sbuf = sbuf

    def snap(self):
        return (dict(self.w), dict(self.r))

    def restore(self, s):
        self.w, self.r = dict(s[0]), dict(s[1])


class TV:
    __slots__ = ("ap", "res")

    def __init__(self, ap, res):
        self.ap = ap
        self.res = res

    def __getitem__(self, sl):
        return TV(self.ap[sl], self.res)

    def rearrange(self, pat, **kw):
        return TV(self.ap.rearrange(pat, **kw), self.res)


class Ctx:
    def __init__(self):
        self.dry = True
        self.cur = None
        self.eng = None
        self.counts = {k: 0 for k in SEMS}
        self.handles = {}
        self.pid = None
        self.wm = {e: {s: 0 for s in SEMS} for e in ENGINES}
        self.registry = []
        self.labels = {}
        self.dma_idx = {q: 0 for q in DMA_PREFIX}

    def newres(self, sbuf=False):
        r = R(sbuf)
        self.registry.append(r)
        return r

    def _wait(self, engine, sem, val):
        if val <= 0:
            return
        if not STRICT and engine == "tensor" and sem == "T":
            return
        if self.wm[engine][sem] >= val:
            return
        self.wm[engine][sem] = val
        if not self.dry and self.cur == engine:
            self.eng.wait_ge(self.handles[sem], val)

    def _wait_map(self, engine, m):
        for sem, val in m.items():
            self._wait(engine, sem, val)

    def _pre(self, engine, reads, writes, skip_pfx=None):
        def wm_of(res, m):
            if skip_pfx is not None and res.sbuf:
                return {k: v for k, v in m.items() if not k.startswith(skip_pfx)}
            return m
        for tvs in reads:
            if tvs is not None:
                self._wait_map(engine, wm_of(tvs.res, tvs.res.w))
        for tvs in writes:
            if tvs is not None:
                self._wait_map(engine, wm_of(tvs.res, tvs.res.w))
                self._wait_map(engine, wm_of(tvs.res, tvs.res.r))

    def _post(self, sem, reads, writes, label=None):
        inc = SEMS[sem][1]
        self.counts[sem] += inc
        v = self.counts[sem]
        if label is not None:
            if self.dry:
                if label in self.labels and self.labels[label] != (sem, v):
                    raise RuntimeError(f"label {label} inconsistent")
                self.labels[label] = (sem, v)
        for tvs in reads:
            if tvs is not None:
                if tvs.res.r.get(sem, 0) < v:
                    tvs.res.r[sem] = v
        for tvs in writes:
            if tvs is not None:
                if tvs.res.w.get(sem, 0) < v:
                    tvs.res.w[sem] = v
        return v

    def op(self, engine, fn, outs, ins, *a, label=None, dma_sem=None, **kw):
        sem = dma_sem if dma_sem else ENG_COMPUTE_SEM[engine]
        self._pre(engine, ins, outs,
                  skip_pfx=(DMA_PREFIX[engine] if dma_sem else None))
        if not self.dry and self.cur == engine:
            args = [x.ap if isinstance(x, TV) else x for x in a]
            inst = getattr(self.eng, fn)(*args, **{
                k: (v.ap if isinstance(v, TV) else v) for k, v in kw.items()})
            inst.then_inc(self.handles[sem], SEMS[sem][1])
        self._post(sem, ins, outs, label)

    def wait_label(self, engine, name):
        if self.dry:
            assert name in self.labels, f"undefined label {name}"
            return
        sem, val = self.labels[name]
        self._wait(engine, sem, val)

    def mm_group(self, out, ops, label=None, already_open=False):
        n = len(ops)
        self._pre("tensor", [x for p in ops for x in p], [out])
        for i, (l_, r_) in enumerate(ops):
            if not self.dry and self.cur == "tensor":
                inst = self.eng.matmul(out.ap, l_.ap, r_.ap,
                                       start=(i == 0 and not already_open),
                                       stop=(i == n - 1))
                if i == n - 1:
                    inst.then_inc(self.handles["T"], 1)
        self._post("T", [x for p in ops for x in p], [out], label)

    def mm_acc(self, out, lhsT, rhs, start, stop, label=None):
        self._pre("tensor", [lhsT, rhs], [out])
        if not self.dry and self.cur == "tensor":
            inst = self.eng.matmul(out.ap, lhsT.ap, rhs.ap, start=start, stop=stop)
            inst.then_inc(self.handles["T"], 1)
        self._post("T", [lhsT, rhs], [out], label)

    def tr(self, out, in_, ident, label=None):
        self.op("tensor", "transpose", [out], [in_, ident], out, in_, ident,
                label=label)

    def act(self, out, in_, func, bias=0.0, scale=1.0, label=None):
        br = [bias] if isinstance(bias, TV) else []
        sr = [scale] if isinstance(scale, TV) else []
        self.op("scalar", "activation", [out], [in_] + br + sr, out, in_, func,
                bias=bias, scale=scale, label=label)

    def s_copy(self, out, in_, label=None):
        self.op("scalar", "copy", [out], [in_], out, in_, label=label)

    def vcopy(self, out, in_, label=None):
        self.op("vector", "tensor_copy", [out], [in_], out, in_, label=label)

    def vtt(self, fn, out, a_, b_, label=None):
        self.op("vector", fn, [out], [a_, b_], out, a_, b_, label=label)

    def vts(self, fn, out, a_, s_, label=None):
        ins = [a_] + ([s_] if isinstance(s_, TV) else [])
        self.op("vector", fn, [out], ins, out, a_, s_, label=label)

    def vstt(self, out, in0, scal, in1, op0, op1, label=None):
        ins = [in0, in1] + ([scal] if isinstance(scal, TV) else [])
        self.op("vector", "scalar_tensor_tensor", [out], ins,
                out, in0, scal, in1, op0, op1, label=label)

    def vreduce(self, fn, out, in_, label=None):
        self.op("vector", fn, [out], [in_], out, in_, axis=AX.X, label=label)

    def vrecip(self, out, in_, label=None):
        self.op("vector", "reciprocal", [out], [in_], out, in_, label=label)

    def vmemset(self, out, val, label=None):
        self.op("vector", "memset", [out], [], out, val, label=label)

    def gmemset(self, out, val, label=None):
        self.op("gpsimd", "memset", [out], [], out, val, label=label)

    def gaffsel(self, out, in_, pattern, cmp, fill, base, label=None, chm=1):
        self.op("gpsimd", "affine_select", [out], [in_], label=label,
                out=out, in_=in_, pattern=pattern, compare_op=cmp, fill=fill,
                base=base, channel_multiplier=chm)

    def dma(self, q, out, in_, label=None, accum=False, slow=False):
        kw = {}
        if accum:
            kw["accum_op"] = ALU.add
        if slow:
            kw["allow_slow_non_contiguous"] = True
        qi = self.dma_idx[q]
        self.dma_idx[q] = qi + 1
        lane = f"{DMA_PREFIX[q]}{qi % NLANE}"
        if qi >= NLANE:
            self._wait(q, lane, 16 * (qi // NLANE))
        outs = [out]
        ins = [in_] + ([out] if accum else [])
        self.op(q, "dma_start", outs, ins, label=label, dma_sem=lane,
                out=out, in_=in_, **kw)

    def cc(self, kind, in_tv, out_tv, label=None):
        self._pre("gpsimd", [in_tv], [out_tv])
        if not self.dry and self.cur == "gpsimd":
            inst = self.eng.collective_compute(
                kind, ALU.bypass, replica_groups=[list(range(NCORE))],
                ins=[in_tv.ap.opt()], outs=[out_tv.ap.opt()])
            inst.then_inc(self.handles["CC"], 1)
        self._post("CC", [in_tv], [out_tv], label)

    def seed_alias(self, dst_tv, src_tv):
        for sem, v in src_tv.res.w.items():
            if dst_tv.res.w.get(sem, 0) < v:
                dst_tv.res.w[sem] = v
        for sem, v in src_tv.res.r.items():
            if dst_tv.res.r.get(sem, 0) < v:
                dst_tv.res.r[sem] = v

    def fence(self):
        for e in ENGINES:
            for s_ in SEMS:
                self._wait(e, s_, self.counts[s_])

    def snapshot(self):
        return (dict(self.counts),
                [r.snap() for r in self.registry],
                {e: dict(m) for e, m in self.wm.items()},
                dict(self.dma_idx))

    def restore(self, s):
        self.counts = dict(s[0])
        for r, rs in zip(self.registry, s[1]):
            r.restore(rs)
        self.wm = {e: dict(m) for e, m in s[2].items()}
        self.dma_idx = dict(s[3])

    def core_switch(self, tag, body_fn):
        nreg = len(self.registry)
        entry = self.snapshot()
        if self.dry:
            exit_s = None
            d0 = None
            for i in range(NCORE):
                self.restore(entry)
                body_fn(i)
                if len(self.registry) != nreg:
                    raise RuntimeError(f"switch {tag}: body allocated resources")
                d = {k: self.counts[k] - entry[0][k] for k in SEMS}
                if d0 is None:
                    d0, exit_s = d, self.snapshot()
                elif d != d0:
                    diff = {k: (d0[k], d[k]) for k in SEMS if d[k] != d0[k]}
                    raise RuntimeError(f"switch {tag} body {i}: delta {diff}")
            self.restore(exit_s)
        else:
            if self.pid is None:
                self.pid = self.eng.partition_id()
            exit_s = None
            first = True
            for i in self.eng.Switch(self.pid, n=NCORE):
                self.restore(entry)
                body_fn(i)
                if first:
                    exit_s = self.snapshot()
                    first = False
            self.restore(exit_s)
        final = dict(self.counts)
        for r_, es in zip(self.registry, entry[1]):
            if r_.snap() != es:
                r_.w = {s_: final[s_] for s_ in SEMS if final[s_] > 0}
                r_.r = {}


def build_program():
    nc = bass.Bass("TRN2", target_bir_lowering=False, debug=False,
                   detect_race_conditions=STRICT)
    ctx = Ctx()

    d = {}

    def ein(name, shape):
        d[name] = TV(nc.dram_tensor(name, list(shape), F32,
                                    kind="ExternalInput").ap(), ctx.newres())

    def din(name, shape, dt=F32):
        d[name] = TV(nc.dram_tensor(name, list(shape), dt).ap(), ctx.newres())

    ein("x0", (DM, COLS))
    ein("wqkvT", (DM, 1536))
    ein("woutT", (512, DM))
    for n in ("b_out", "bfv", "biv", "bcv", "bov", "incv"):
        ein(n, (DM,))
    for n in ("wqT", "wkT", "wvT", "woT",
              "wfT", "ufT", "wiT", "uiT", "wcT", "ucT", "wobT", "uoT"):
        ein(n, (DM, DM))
    ein("normw", (128,))
    d["y"] = TV(nc.dram_tensor("y", [DM, COLS], F32, kind="ExternalOutput").ap(),
                ctx.newres())

    din("x1", (DM, COLS))
    din("x2", (DM, COLS))
    din("x3", (DM, COLS))
    din("ha", (DM, COLS), BF16)
    din("hb", (DM, COLS), BF16)
    din("ka_pub", (B * 513, 1024), BF16)
    din("va_pub", (COLS, 512), BF16)
    for _b in range(B):
        din(f"ka_gat{_b}", (NCORE * 513, 1024), BF16)
        din(f"va_gat{_b}", (NCORE * 1024, 512), BF16)
    din("qb", (DM, COLS), BF16)
    din("kb", (DM, COLS), BF16)
    din("kbt", (COLS, DM), BF16)
    din("vbt", (COLS, DM), BF16)
    din("bs_pub", (2 * B * 8 * 128, 129), BF16)
    din("bs_gat_a", (NCORE * B * 8 * 128, 129), BF16)
    din("bs_gat_b", (NCORE * B * 8 * 128, 129), BF16)
    for _m in range(12):
        din(f"wst{_m}", (8 * 128, DM), BF16)
    din("wst12", (8 * 128, 512), BF16)

    mgrs = []
    t = {}

    def sb(name, shape, dt=F32, grp=None):
        g = nc.sbuf_tensor(name, list(shape), dt)
        h = g.__enter__()
        if grp is None:
            mgrs.append(g)
        else:
            grp.append(g)
        t[name] = TV(h[:], ctx.newres(sbuf=True))
        return t[name]

    def free_grp(grp):
        for g in reversed(grp):
            g.__exit__(None, None, None)

    sb("ident", (128, 128), BF16)
    sb("tmask", (128, 896), BF16)
    sb("tmask32", (128, 128))
    sb("ones_b", (128, 1), BF16)
    sb("ones_r32", (1, 128))
    sb("bout", (128, NCH))
    sb("bfs", (128, NCH))
    sb("bis", (128, NCH))
    sb("bcs", (128, NCH))
    sb("bos", (128, NCH))
    sb("incs", (128, NCH))
    sb("normw_s", (128, 1))
    sb("rows", (1, 4096))
    sb("xold", (128, 2 * COLS))
    sb("xnew", (128, COLS))
    sb("tscr", (128, 2 * 512))
    sb("hprod", (128, 2 * 512), BF16)
    sb("sq", (128, 2 * 512), BF16)
    sb("xbf", (128, 2 * 512), BF16)
    sb("wring", (128, 12288), BF16)
    ga = []
    sb("htile", (128, 2 * COLS), BF16, ga)
    sb("q_a", (128, 4 * COLS), BF16, ga)
    sb("kblk", (128, 2 * 2048), BF16, ga)
    sb("vtile", (128, 3 * 512), BF16, ga)
    sb("etile", (128, 2 * 512), BF16, ga)
    sb("kn", (1, 2 * 1024), BF16, ga)
    sb("qmaxg", (1, CT), F32, ga)
    sb("kmaxb", (1, B * 16), F32, ga)
    sb("negb", (128, 2), F32, ga)
    sb("rscol", (128, 4), F32, ga)
    sb("ao_tm", (128, 4 * 512), BF16, ga)
    sb("ao_fm", (128, 2 * 4 * 512), BF16, ga)
    free_grp(ga)
    gf = []
    sb("c8", (128, NCH * 1024), F32, gf)
    sb("cbf", (128, NCH * 1024), BF16, gf)
    sb("htf", (128, 2 * NCH * 1024), BF16, gf)
    sb("gates", (128, 12 * 512), F32, gf)
    free_grp(gf)
    gb = []
    sb("sT_dummy", (1, 1), F32, gb)
    sb("kcs", (128, 256), F32, gb)
    sb("zcx", (128, 257), BF16, gb)
    sb("kpfb", (128, 64), BF16, gb)
    for _b in range(B):
        sb(f"kpfx{_b}", (128, 16), F32, gb)
    sb("bsring", (128, 2 * 1032), F32, gb)
    sb("pfxm", (128, 1032), F32, gb)
    sb("zcol", (128, 4), BF16, gb)
    sb("zrun", (128, 1), F32, gb)
    sb("recd", (1, 8 * 1024), BF16, gb)
    sb("stq", (128, 2 * 1024), BF16, gb)
    sb("stk", (128, 2 * 1024), BF16, gb)
    sb("stkt", (128, 2 * 1024), BF16, gb)
    sb("stvt", (128, 2 * 1024), BF16, gb)
    sb("sop", (128, 2 * 128), BF16, gb)
    sb("stmf", (128, 2 * 128), F32, gb)
    sb("osb", (128, 2 * 1024), F32, gb)
    sb("onorm", (128, 8 * 1024), BF16, gb)
    gp = []
    sb("htileb", (128, 2 * COLS), BF16, gp)
    sb("ptile", (128, 8 * 1024), BF16, gp)
    free_grp(gp)
    gs = []
    sb("sT", (128, 8 * 8 * 128), BF16, gs)
    for _b in range(B):
        sb(f"spfx{_b}", (128, 16 * 128), BF16, gs)
    free_grp(gs)
    free_grp(gb)

    ps = []
    for i in range(7):
        g = nc.psum_tensor(f"ps{i}", [128, 512], F32)
        h = g.__enter__()
        mgrs.append(g)
        ps.append(TV(h[:], ctx.newres(sbuf=True)))
    g = nc.psum_tensor("psbf", [128, 1024], BF16)
    h = g.__enter__()
    mgrs.append(g)
    psbf = TV(h[:], ctx.newres(sbuf=True))

    for s_ in SEMS:
        g = nc.semaphore(s_)
        ctx.handles[s_] = g.__enter__()
        mgrs.append(g)

    def seg(name, a, b_):
        return t[name][:, a:b_]

    def ldtile(q, dst, src2d, c=8):
        k = src2d.ap.shape[1]
        ctx.dma(q, dst.rearrange("p (c k) -> p c k", c=c),
                src2d.rearrange("(c p) k -> p c k", p=128))

    def sttile(q, src, dst2d, c=8):
        ctx.dma(q, dst2d.rearrange("(c p) k -> p c k", p=128),
                src.rearrange("p (c k) -> p c k", c=c))

    def wslot(off, width):
        return t["wring"][:, off:off + width]

    def load_wcast(q, wt, ocol0, width, off, label=None):
        src = wt[:, ocol0:ocol0 + width].rearrange("(c p) o -> p c o", p=128)
        dst = wslot(off, NCH * width).rearrange("p (c o) -> p c o", c=NCH)
        ctx.dma(q, dst, src, label=label)

    def wst_fm(m, oc):
        return d[f"wst{m}"][oc * 128:(oc + 1) * 128, :]

    def ln_stats_chunk(xn_tv, dc, uid):
        xb = seg("xbf", (dc % 2) * 512, (dc % 2) * 512 + 512)
        sqs = seg("sq", (dc % 2) * 512, (dc % 2) * 512 + 512)
        ctx.vcopy(xb, xn_tv)
        ctx.act(sqs, xb, AF.Square)
        ctx.mm_acc(ps[2][0:1, :], t["ones_b"], xb, start=(dc == 0),
                   stop=(dc == NCH - 1))
        ctx.mm_acc(ps[3][0:1, :], t["ones_b"], sqs, start=(dc == 0),
                   stop=(dc == NCH - 1))

    def ln_finalize():
        mu = t["rows"][0:1, 0:512]
        m2 = t["rows"][0:1, 512:1024]
        rstd = t["rows"][0:1, 1024:1536]
        ctx.vts("tensor_scalar_mul", mu, ps[2][0:1, 0:512], 1.0 / DM)
        ctx.vts("tensor_scalar_mul", m2, ps[3][0:1, 0:512], 1.0 / DM)
        ctx.vtt("tensor_mul", rstd, mu, mu)
        ctx.vtt("tensor_sub", rstd, m2, rstd)
        ctx.vts("tensor_scalar_add", rstd, rstd, EPS)
        ctx.act(rstd, rstd, AF.Ln)
        ctx.act(rstd, rstd, AF.Exp, scale=-0.5)
        ctx.mm_group(ps[4], [(t["ones_r32"], mu)])
        ctx.mm_group(ps[5], [(t["ones_r32"], rstd)])

    def ln_apply_chunk(x_tv, h_tv, par=0):
        sl = t["tscr"][:, (par % 2) * 512:(par % 2 + 1) * 512]
        ctx.vtt("tensor_sub", sl, x_tv, ps[4])
        ctx.vtt("tensor_mul", h_tv, sl, ps[5])

    epi_seq = [0]

    def emit_epilogue_tile(colbase, y_provider, x_old, x_new_dram, h_dram):
        eid = epi_seq[0]
        epi_seq[0] += 1
        xo = t["xold"][:, (eid % 2) * COLS:(eid % 2) * COLS + COLS]
        ldtile("sync", xo, x_old[:, colbase:colbase + 512])
        def stats_emit(dc):
            xn = t["xnew"][:, dc * 512:(dc + 1) * 512]
            ln_stats_chunk(xn, dc, eid)
        for dc in range(NCH):
            ysrc, bias = y_provider(dc)
            xn = t["xnew"][:, dc * 512:(dc + 1) * 512]
            if bias is not None:
                ctx.vstt(xn, ysrc, bias, xo[:, dc * 512:(dc + 1) * 512],
                         ALU.add, ALU.add)
            else:
                ctx.vtt("tensor_add", xn, ysrc, xo[:, dc * 512:(dc + 1) * 512])
            ctx.dma("scalar", x_new_dram[dc * 128:(dc + 1) * 128,
                                         colbase:colbase + 512], xn)
            if h_dram is not None and dc >= 1:
                stats_emit(dc - 1)
        if h_dram is not None:
            stats_emit(NCH - 1)
        if h_dram is None:
            return
        ln_finalize()
        for dc in range(NCH):
            hp = t["hprod"][:, (dc % 2) * 512:(dc % 2) * 512 + 512]
            xn = t["xnew"][:, dc * 512:(dc + 1) * 512]
            ln_apply_chunk(xn, hp, par=dc)
            ctx.dma("scalar", h_dram[dc * 128:(dc + 1) * 128,
                                     colbase:colbase + 512], hp)

    def emit_consts():
        ctx.gmemset(t["ones_b"], 1.0)
        ctx.gmemset(t["ones_r32"], 1.0)
        ctx.gmemset(t["tmask"], 1.0)
        ctx.gmemset(t["tmask32"], 1.0)
        ctx.gmemset(t["ident"], 1.0)
        ctx.gaffsel(t["ident"], t["ident"], [[-1, 128]], ALU.is_equal, 0.0, 0)
        ctx.gaffsel(t["tmask"], t["tmask"], [[1, 896]], ALU.is_ge, 0.0, -384,
                    chm=-1)
        ctx.gaffsel(t["tmask32"], t["tmask32"], [[1, 128]], ALU.is_ge, 0.0, 0,
                    chm=-1)
        for name, dst in (("b_out", "bout"), ("bfv", "bfs"), ("biv", "bis"),
                          ("bcv", "bcs"), ("bov", "bos"), ("incv", "incs")):
            ctx.dma("sync", t[dst], d[name].rearrange("(c p) -> p c", p=128),
                    slow=True)
        ctx.dma("sync", t["normw_s"], d["normw"][:, None])
        for oc in range(NCH):
            src = d["woutT"][:, oc * 128:(oc + 1) * 128].rearrange(
                "(dc p) o -> p dc o", p=128)
            dst = d["wst12"][oc * 128:(oc + 1) * 128, :].rearrange(
                "p (dc o) -> p dc o", dc=4)
            ctx.dma("gpsimd", dst, src)


    def emit_wstage(ms):
        for m in ms:
            w = ("wfT", "ufT", "wiT", "uiT", "wcT", "ucT",
                 "wobT", "uoT", "wqT", "wkT", "wvT", "woT")[m]
            for oc in range(8):
                src = d[w][:, oc * 128:(oc + 1) * 128].rearrange(
                    "(dc p) o -> p dc o", p=128)
                dst = wst_fm(m, oc).rearrange("p (dc o) -> p dc o", dc=8)
                ctx.dma("gpsimd", dst, src)

    def emit_attn_a():
        for ct in range(CT):
            b, g_ = ct // 2, ct % 2
            colbase = ct * 512
            xo = t["xold"][:, (ct % 2) * COLS:(ct % 2) * COLS + COLS]
            ldtile("sync", xo, d["x0"][:, colbase:colbase + 512])
            for dc in range(NCH):
                ln_stats_chunk(xo[:, dc * 512:(dc + 1) * 512], dc, ct)
            ln_finalize()
            ht = t["htile"][:, (ct % 2) * COLS:(ct % 2) * COLS + COLS]
            for dc in range(NCH):
                ln_apply_chunk(xo[:, dc * 512:(dc + 1) * 512],
                               ht[:, dc * 512:(dc + 1) * 512], par=dc)
            for oc in range(8):
                off = (oc % 2) * 1024
                load_wcast("gpsimd", d["wqkvT"], oc * 128, 128, off)
                bank = ps[oc % 2]
                ctx.mm_group(bank, [
                    (wslot(off + dc * 128, 128), ht[:, dc * 512:(dc + 1) * 512])
                    for dc in range(NCH)])
                if oc < 4:
                    qd = t["q_a"][:, oc * COLS + colbase: oc * COLS + colbase + 512]
                    ctx.vcopy(qd, bank)
                    sqs = seg("sq", (oc % 2) * 512, (oc % 2) * 512 + 512)
                    ctx.act(sqs, qd, AF.Square)
                    ctx.mm_acc(ps[6][0:1, :], t["ones_b"], sqs, start=(oc == 0),
                               stop=(oc == 3))
                    if oc == 3:
                        ctx.vreduce("reduce_max",
                                    t["qmaxg"][0:1, ct:ct + 1], ps[6][0:1, :])
                else:
                    ko = oc - 4
                    kp = t["hprod"][:, (oc % 2) * 512:(oc % 2) * 512 + 512]
                    ctx.vcopy(kp, bank)
                    ctx.dma("scalar",
                            d["ka_pub"][b * 513 + ko * 128:b * 513 + (ko + 1) * 128,
                                        g_ * 512:g_ * 512 + 512], kp)
                    sqs = seg("sq", (oc % 2) * 512, (oc % 2) * 512 + 512)
                    ctx.act(sqs, kp, AF.Square)
                    ctx.mm_acc(ps[6][0:1, :], t["ones_b"], sqs, start=(ko == 0),
                               stop=(ko == 3))
                    if ko == 3:
                        kr = t["rows"][0:1, 1536:2048]
                        ctx.vcopy(kr, ps[6][0:1, :])
                        ctx.s_copy(t["kn"][0:1, 0:512], kr)
                        ctx.dma("scalar",
                                d["ka_pub"][b * 513 + 512:b * 513 + 513,
                                            g_ * 512:g_ * 512 + 512],
                                t["kn"][0:1, 0:512])
            load_wcast("gpsimd", d["wqkvT"], 1024, 256, 2048)
            load_wcast("gpsimd", d["wqkvT"], 1280, 256, 4096)
            for pc in range(4):
                bank = ps[pc % 2]
                for half in range(2):
                    ctx.mm_group(bank[:, half * 256:(half + 1) * 256], [
                        (ht[:, dc * 512 + pc * 128: dc * 512 + (pc + 1) * 128],
                         wslot(2048 + half * 2048 + dc * 256, 256))
                        for dc in range(NCH)])
                vp = t["hprod"][:, (pc % 2) * 512:(pc % 2) * 512 + 512]
                ctx.vcopy(vp, bank)
                ctx.dma("scalar", d["va_pub"][colbase + pc * 128:
                                              colbase + (pc + 1) * 128, :], vp)
            if g_ == 1:
                ctx.cc("AllGather", d["ka_pub"][b * 513:(b + 1) * 513, :],
                       d[f"ka_gat{b}"])
                ctx.cc("AllGather", d["va_pub"][b * 1024:(b + 1) * 1024, :],
                       d[f"va_gat{b}"])
        emit_wstage(range(8))
        for b in range(B):
            for r_ in range(NCORE):
                sl = t["kn"][0:1, (r_ % 2) * 1024:(r_ % 2) * 1024 + 1024]
                ctx.dma("sync", sl,
                        d[f"ka_gat{b}"][r_ * 513 + 512:r_ * 513 + 513, :])
                for half in range(2):
                    j = r_ if half == 0 else 15 - r_
                    ctx.vreduce("reduce_max",
                                t["kmaxb"][0:1, b * 16 + j:b * 16 + j + 1],
                                sl[0:1, half * 512:(half + 1) * 512])
        if TAP == "dbg":
            ctx.vmemset(t["xnew"], 0.0)
        for b in range(B):
            ctx.core_switch(f"attn_a_b{b}",
                            lambda cid, b=b: attn_a_body(cid, b))
            if TAP == "dbg":
                return
            for g in range(2):
                emit_attn_a_epilogue(b, g)

    def attn_a_body(cid, b):
        for g, bl in enumerate(core_blocks(cid)):
            attn_a_group(cid, b, g, bl)

    def attn_a_group(cid, b, g, bl):
        colbase = b * 1024 + g * 512
        nkc = (bl + 1) * 4
        qmax = t["rows"][0:1, 0:1]
        kmax = t["rows"][0:1, 1:2]
        ctx.vreduce("reduce_max", kmax, t["kmaxb"][0:1, b * 16:b * 16 + bl + 1])
        ctx.vtt("tensor_mul", qmax, t["qmaxg"][0:1, 2 * b + g:2 * b + g + 1],
                kmax)
        ctx.act(qmax, qmax, AF.Ln)
        ctx.act(qmax, qmax, AF.Exp, scale=0.5)
        ctx.op("scalar", "mul", [qmax], [qmax], qmax, qmax, -ATT_SCALE)
        ctx.mm_group(ps[2][:, 0:1], [(t["ones_r32"], qmax)])
        nb = t["negb"][:, g:g + 1]
        ctx.vcopy(nb, ps[2][:, 0:1])

        def kblk_slot(j):
            return t["kblk"][:, (j % 2) * 2048:(j % 2) * 2048 + 2048]

        def kblk_load(j):
            r_ = j if j < 8 else 15 - j
            off = 0 if j < 8 else 512
            ldtile("sync", kblk_slot(j),
                   d[f"ka_gat{b}"][r_ * 513:r_ * 513 + 512, off:off + 512], c=4)

        def vt_slot(i):
            return t["vtile"][:, (i % 3) * 512:(i % 3) * 512 + 512]

        def vt_load(kc):
            j = kc // 4
            r_ = j if j < 8 else 15 - j
            row = r_ * 1024 + (0 if j < 8 else 512) + (kc % 4) * 128
            ctx.dma("sync", vt_slot(kc), d[f"va_gat{b}"][row:row + 128, :])

        kblk_load(0)
        if nkc > 4:
            kblk_load(1)
        for i in range(min(3, nkc)):
            vt_load(i)

        for kc in range(nkc):
            j = kc // 4
            sbank = ps[kc % 2]
            et = t["etile"][:, (kc % 2) * 512:(kc % 2) * 512 + 512]
            ctx.mm_group(sbank, [
                (kblk_slot(j)[:, dc * 512 + (kc % 4) * 128:
                              dc * 512 + (kc % 4) * 128 + 128],
                 t["q_a"][:, dc * COLS + colbase: dc * COLS + colbase + 512])
                for dc in range(4)])
            ctx.act(et, sbank, AF.Exp, bias=nb, scale=ATT_SCALE)
            if kc >= nkc - 4:
                o_ = (kc - (nkc - 4)) * 128
                msl = t["tmask"][:, 384 - o_:384 - o_ + 512]
                ctx.vtt("tensor_mul", et, et, msl)
            if TAP == "dbg" and DBG_DUMP and b == 0 and kc == 0:
                xn = t["xnew"]
                ctx.vcopy(xn[:, g * 2048 + 0:g * 2048 + 512], sbank)
                ctx.vcopy(xn[:, g * 2048 + 512:g * 2048 + 1024], et)
                ctx.vcopy(xn[:, g * 2048 + 1024:g * 2048 + 1152],
                          kblk_slot(0)[:, 0:128])
                ctx.vcopy(xn[:, g * 2048 + 1152:g * 2048 + 1280],
                          t["q_a"][:, colbase:colbase + 128])
                ctx.vcopy(xn[:, g * 2048 + 1280:g * 2048 + 1281], nb)
                ctx.vcopy(xn[0:1, g * 2048 + 1300:g * 2048 + 1308],
                          t["qmaxg"][0:1, :])
                ctx.vcopy(xn[0:1, g * 2048 + 1308:g * 2048 + 1372],
                          t["kmaxb"][0:1, :])
                ctx.vcopy(xn[:, g * 2048 + 1408:g * 2048 + 1920],
                          vt_slot(0))
            def pv_emit(kcp):
                etp = t["etile"][:, (kcp % 2) * 512:(kcp % 2) * 512 + 512]
                ctx.mm_acc(ps[2][0:1, :], t["ones_b"], etp, start=(kcp == 0),
                           stop=(kcp == nkc - 1))
                for qt in range(4):
                    ctx.mm_acc(ps[3 + qt], etp[:, qt * 128:(qt + 1) * 128],
                               vt_slot(kcp), start=(kcp == 0),
                               stop=(kcp == nkc - 1))
            if kc >= 1:
                pv_emit(kc - 1)
                if kc + 2 < nkc:
                    vt_load(kc + 2)
            if kc == nkc - 1:
                pv_emit(kc)
            if kc % 4 == 3 and j + 2 <= bl:
                kblk_load(j + 2)
        se = t["rows"][0:1, 0:512]
        rs = t["rows"][0:1, 512:1024]
        ctx.vcopy(se, ps[2][0:1, :])
        ctx.vts("tensor_scalar_add", se, se, 1e-30)
        ctx.vrecip(rs, se)
        for qt in range(4):
            ctx.tr(ps[2][:, qt:qt + 1], rs[0:1, qt * 128:(qt + 1) * 128],
                   t["ones_r32"][0:1, 0:1])
            ctx.vcopy(t["rscol"][:, qt:qt + 1], ps[2][:, qt:qt + 1])
            ctx.vts("tensor_scalar_mul", t["ao_tm"][:, qt * 512:(qt + 1) * 512],
                    ps[3 + qt], t["rscol"][:, qt:qt + 1])
        for qt in range(4):
            for vc in range(4):
                pslot = psbf[:, (qt % 2) * 512 + vc * 128:
                             (qt % 2) * 512 + (vc + 1) * 128]
                ctx.tr(pslot, t["ao_tm"][:, qt * 512 + vc * 128:
                                         qt * 512 + (vc + 1) * 128], t["ident"])
                ctx.vcopy(t["ao_fm"][:, g * 2048 + vc * 512 + qt * 128:
                                     g * 2048 + vc * 512 + (qt + 1) * 128], pslot)

    def emit_attn_a_epilogue(b, g):
        colbase = b * 1024 + g * 512

        def y_provider(dc):
            off = 6144 + (dc % 2) * 512
            ctx.dma("sync", wslot(off, 512),
                    d["wst12"][dc * 128:(dc + 1) * 128, :])
            bank = ps[dc % 2]
            ctx.mm_group(bank, [
                (wslot(off + vc * 128, 128),
                 t["ao_fm"][:, g * 2048 + vc * 512:g * 2048 + (vc + 1) * 512])
                for vc in range(4)])
            return bank, t["bout"][:, dc:dc + 1]

        emit_epilogue_tile(colbase, y_provider, d["x0"], d["x1"],
                           d["ha"] if TAP != "x1" else None)

    def emit_ffn(call):
        ctx.fence()
        if call == 0:
            emit_wstage((8, 9, 10, 11))
        x_in = d["x1"] if call == 0 else d["x3"]
        x_out = d["x2"] if call == 0 else d["y"]
        h_in = d["ha"]
        h_out = d["hb"] if (call == 0 and TAP != "x2") else None
        cc8 = [t["c8"][:, i * 1024:(i + 1) * 1024] for i in range(NCH)]
        cbf = [t["cbf"][:, i * 1024:(i + 1) * 1024] for i in range(NCH)]

        def htslot(i):
            return t["htf"][:, (i % 2) * 8192:(i % 2) * 8192 + 8192]

        def ht_load(i, tstep):
            ldtile("sync", htslot(i), h_in[:, tstep * 1024:(tstep + 1) * 1024])

        for dc in range(NCH):
            ctx.vmemset(cc8[dc], 0.0)
            ctx.op("scalar", "add", [cc8[dc]], [cc8[dc], t["incs"]],
                   cc8[dc], cc8[dc], t["incs"][:, dc:dc + 1])
        ht_load(0, 0)
        GW = {0: ("bfs", AF.Sigmoid, 0), 1: ("bis", AF.Sigmoid, 2),
              2: ("bcs", AF.Tanh, 4)}
        for tstep in range(4):
            if tstep < 3:
                ht_load(tstep + 1, tstep + 1)
            ht = htslot(tstep)
            for dc in range(NCH):
                ctx.vcopy(cbf[dc], cc8[dc])
            for oc in range(NCH):
                wb = (oc % 2) * 6144
                gb_ = (oc % 2) * 3072
                if tstep == 0 or True:
                    ld_oc = oc + 1
                    if oc == 0:
                        for g3 in range(3):
                            ctx.dma("sync", wslot(wb + g3 * 2048, 1024),
                                    wst_fm(2 * g3, 0))
                            ctx.dma("sync", wslot(wb + g3 * 2048 + 1024, 1024),
                                    wst_fm(2 * g3 + 1, 0))
                    if ld_oc < NCH:
                        wbn = (ld_oc % 2) * 6144
                        for g3 in range(3):
                            ctx.dma("sync", wslot(wbn + g3 * 2048, 1024),
                                    wst_fm(2 * g3, ld_oc))
                            ctx.dma("sync", wslot(wbn + g3 * 2048 + 1024, 1024),
                                    wst_fm(2 * g3 + 1, ld_oc))
                for g3 in range(3):
                    bname, afn, bk = GW[g3]
                    for ph in range(2):
                        bank = ps[bk + ph]
                        ctx.mm_group(bank, [
                            (wslot(wb + g3 * 2048 + dc * 128, 128),
                             ht[:, dc * 1024 + ph * 512: dc * 1024 + ph * 512 + 512])
                            for dc in range(NCH)] + [
                            (wslot(wb + g3 * 2048 + 1024 + dc * 128, 128),
                             cbf[dc][:, ph * 512:(ph + 1) * 512])
                            for dc in range(NCH)])
                        gt = t["gates"][:, gb_ + (g3 * 2 + ph) * 512:
                                        gb_ + (g3 * 2 + ph + 1) * 512]
                        ctx.act(gt, bank, afn, bias=t[bname][:, oc:oc + 1])
                for ph in range(2):
                    fg = t["gates"][:, gb_ + ph * 512:gb_ + (ph + 1) * 512]
                    ig = t["gates"][:, gb_ + (2 + ph) * 512:gb_ + (3 + ph) * 512]
                    gg = t["gates"][:, gb_ + (4 + ph) * 512:gb_ + (5 + ph) * 512]
                    cs = cc8[oc][:, ph * 512:(ph + 1) * 512]
                    scr = t["tscr"][:, ph * 512:(ph + 1) * 512]
                    ctx.vtt("tensor_mul", scr, ig, gg)
                    ctx.vtt("tensor_mul", cs, fg, cs)
                    ctx.vtt("tensor_add", cs, cs, scr)
        for dc in range(NCH):
            ctx.vcopy(cbf[dc], cc8[dc])
        for oc in range(NCH):
            ub = (oc % 2) * 1024
            if oc == 0:
                ctx.dma("sync", wslot(0, 1024), wst_fm(7, 0))
            if oc + 1 < NCH:
                ctx.dma("sync", wslot(((oc + 1) % 2) * 1024, 1024),
                        wst_fm(7, oc + 1))
            for ph in range(2):
                bank = ps[oc % 2]
                ctx.mm_group(bank, [
                    (wslot(ub + dc * 128, 128),
                     cbf[dc][:, ph * 512:(ph + 1) * 512])
                    for dc in range(NCH)])
                ctx.vcopy(cc8[oc][:, ph * 512:(ph + 1) * 512], bank)
        for tstep in range(4):
            ht_load(tstep, tstep)
            ht = htslot(tstep)
            for ph in range(2):
                def y_provider(dc, tstep=tstep, ph=ph, ht=ht):
                    if dc == 0:
                        ctx.dma("sync", wslot(2048, 1024), wst_fm(6, 0))
                    if dc + 1 < NCH:
                        ctx.dma("sync", wslot(2048 + ((dc + 1) % 2) * 1024, 1024),
                                wst_fm(6, dc + 1))
                    bank = ps[dc % 2]
                    ctx.mm_group(bank, [
                        (wslot(2048 + (dc % 2) * 1024 + k_ * 128, 128),
                         ht[:, k_ * 1024 + ph * 512: k_ * 1024 + ph * 512 + 512])
                        for k_ in range(NCH)])
                    ys = t["gates"][:, (dc % 2) * 512:(dc % 2) * 512 + 512]
                    ctx.vstt(ys, bank, t["bos"][:, dc:dc + 1],
                             cc8[dc][:, ph * 512:(ph + 1) * 512], ALU.add, ALU.add)
                    ctx.act(ys, ys, AF.Tanh)
                    return ys, None
                emit_epilogue_tile(tstep * 1024 + ph * 512, y_provider,
                                   x_in, x_out, h_out)

    def emit_attn_b():
        ctx.fence()
        ctx.vmemset(t["zcx"][:, 256:257], 0.0)
        h_in = d["hb"]
        for g in range(2):
            if g == 1:
                ctx.cc("AllGather", d["bs_pub"][0:4096, :], d["bs_gat_a"])
            for b in range(B):
                colbase = b * 1024 + g * 512
                cti = g * 4 + b
                ht = t["htileb"][:, (cti % 2) * COLS:(cti % 2) * COLS + COLS]
                ldtile("sync", ht, h_in[:, colbase:colbase + 512])
                PB = (ps[0], ps[1], ps[3], ps[4])
                for mi, (m, dstd) in enumerate(((8, d["qb"]), (9, d["kb"]))):
                    for oc in range(NCH):
                        woff = mi * 2048 + (oc % 2) * 1024
                        if oc == 0:
                            ctx.dma("sync", wslot(woff, 1024), wst_fm(m, 0))
                        if oc + 1 < NCH:
                            wof2 = mi * 2048 + ((oc + 1) % 2) * 1024
                            ctx.dma("sync", wslot(wof2, 1024),
                                    wst_fm(m, oc + 1))
                        bank = PB[oc % 4]
                        ctx.mm_group(bank, [
                            (wslot(woff + dc * 128, 128),
                             ht[:, dc * 512:(dc + 1) * 512])
                            for dc in range(NCH)])
                        pp = t["hprod"][:, (oc % 2) * 512:(oc % 2) * 512 + 512]
                        ctx.vcopy(pp, bank)
                        ctx.dma("scalar", dstd[oc * 128:(oc + 1) * 128,
                                               colbase:colbase + 512], pp)
                        if m == 9:
                            base = (cti * 8 + oc) * 4
                            for pc in range(4):
                                ctx.vreduce(
                                    "reduce_sum",
                                    t["kcs"][:, base + pc:base + pc + 1],
                                    pp[:, pc * 128:(pc + 1) * 128])
                            ctx.vcopy(t["zcx"][:, base + 1:base + 2],
                                      t["kcs"][:, base:base + 1])
                            ctx.vtt("tensor_add", t["zrun"],
                                    t["kcs"][:, base:base + 1],
                                    t["kcs"][:, base + 1:base + 2])
                            ctx.vcopy(t["zcx"][:, base + 2:base + 3], t["zrun"])
                            ctx.vtt("tensor_add", t["zrun"], t["zrun"],
                                    t["kcs"][:, base + 2:base + 3])
                            ctx.vcopy(t["zcx"][:, base + 3:base + 4], t["zrun"])
                for mi, (m, dstd) in enumerate(((9, d["kbt"]), (10, d["vbt"]))):
                    for half in range(2):
                        wsl = wslot(4096 + half * 4096, 4096)
                        rows = d[f"wst{m}"][(half * 4) * 128:
                                            (half * 4 + 4) * 128, :]
                        ctx.dma("sync",
                                wsl.rearrange("p (dc oc o) -> p dc oc o",
                                              dc=8, oc=4),
                                rows.rearrange("(oc p) (dc o) -> p dc oc o",
                                               p=128, dc=8))
                        for pc in range(4):
                            bank = ps[pc % 2]
                            ctx.mm_group(bank, [
                                (ht[:, dc * 512 + pc * 128:
                                    dc * 512 + (pc + 1) * 128],
                                 wsl[:, dc * 512:(dc + 1) * 512])
                                for dc in range(NCH)])
                            pdst = t["ptile"][:, mi * 4096 + pc * 1024 +
                                              half * 512: mi * 4096 + pc * 1024 +
                                              half * 512 + 512]
                            ctx.vcopy(pdst, bank)
                    for pc in range(4):
                        ctx.dma("scalar",
                                dstd[colbase + pc * 128:colbase + (pc + 1) * 128, :],
                                t["ptile"][:, mi * 4096 + pc * 1024:
                                           mi * 4096 + (pc + 1) * 1024])
                for h in range(8):
                    ctx.mm_group(ps[6][:, 0:128], [
                        (t["ptile"][:, pc * 1024 + h * 128:
                                    pc * 1024 + (h + 1) * 128],
                         t["ptile"][:, 4096 + pc * 1024 + h * 128:
                                    4096 + pc * 1024 + (h + 1) * 128])
                        for pc in range(4)], )
                    bst = t["tscr"]
                    ctx.vcopy(bst[:, 0:128], ps[6][:, 0:128])
                    ctx.vreduce("reduce_sum", bst[:, 128:129],
                                t["kcs"][:, (cti * 8 + h) * 4:(cti * 8 + h) * 4 + 4])
                    rowb = ((g * 4 + b) * 8 + h) * 128
                    ctx.dma("gpsimd", d["bs_pub"][rowb:rowb + 128, 0:129],
                            bst[:, 0:129])
        ctx.cc("AllGather", d["bs_pub"][4096:8192, :], d["bs_gat_b"])
        for _tv in [t["sT"]] + [t[f"spfx{_b}"] for _b in range(B)]:
            ctx.seed_alias(_tv, t["ptile"])
            ctx.seed_alias(_tv, t["htileb"])

        def pfx(b):
            ctx.core_switch(f"attn_b_pfx{b}",
                            lambda cid, b=b: attn_b_pfx_body(cid, b))
            ctx.op("gpsimd", "tensor_copy",
                   [t["kpfb"][:, b * 16:(b + 1) * 16]], [t[f"kpfx{b}"]],
                   t["kpfb"][:, b * 16:(b + 1) * 16], t[f"kpfx{b}"])
        pfx(0)
        pfx(1)
        for b in range(B):
            for h in range(8):
                attn_b_B1(b, h)
            if b + 2 < B:
                pfx(b + 2)
            for hp in range(4):
                attn_b_B2(b, hp)
            emit_attn_b_project(b)

    def attn_b_pfx_body(cid, b):
        bl0, bl1 = core_blocks(cid)
        mst = t["pfxm"]
        ctx.vmemset(mst, 0.0)
        for j in range(16):
            for gi, bl in enumerate((bl0, bl1)):
                if j == bl:
                    ctx.op("vector", "tensor_copy",
                           [t[f"spfx{b}"]], [mst],
                           t[f"spfx{b}"].rearrange("p (h g c) -> p h g c",
                                                   h=8, g=2)[:, :, gi, :],
                           mst.rearrange("p (h c) -> p h c", h=8)[:, :, 0:128])
                    ctx.op("vector", "tensor_copy",
                           [t[f"kpfx{b}"]], [mst],
                           t[f"kpfx{b}"].rearrange("p (h g) -> p h g",
                                                   h=8)[:, :, gi:gi + 1],
                           mst.rearrange("p (h c) -> p h c", h=8)[:, :, 128:129])
            if j < 15:
                r_ = j if j < 8 else 15 - j
                gat = d["bs_gat_a"] if j < 8 else d["bs_gat_b"]
                rowb = r_ * 4096 + b * 1024
                sl = t["bsring"][:, (j % 2) * 1032:(j % 2) * 1032 + 1032]
                ctx.dma("gpsimd", sl.rearrange("p (h c) -> p h c", h=8),
                        gat[rowb:rowb + 1024, 0:129].rearrange(
                            "(h p) c -> p h c", p=128))
                ctx.vtt("tensor_add", mst, mst, sl)

    def attn_b_B1(b, h):
        qs = t["stq"][:, (h % 2) * 1024:(h % 2) * 1024 + 1024]
        ks = t["stk"][:, (h % 2) * 1024:(h % 2) * 1024 + 1024]
        ctx.dma("sync", qs, d["qb"][h * 128:(h + 1) * 128,
                                    b * 1024:(b + 1) * 1024])
        ctx.dma("sync", ks, d["kb"][h * 128:(h + 1) * 128,
                                    b * 1024:(b + 1) * 1024])
        SB_ = (ps[0], ps[1], ps[3], ps[4])
        for cc8 in range(8):
            sbank = SB_[cc8 % 4]
            ctx.mm_group(sbank[:, 0:128], [
                (ks[:, cc8 * 128:(cc8 + 1) * 128],
                 qs[:, cc8 * 128:(cc8 + 1) * 128])])
            stile = t["sT"][:, (h * 8 + cc8) * 128:(h * 8 + cc8 + 1) * 128]
            ctx.vtt("tensor_mul", stile, sbank[:, 0:128], t["tmask32"])
        for cc8 in range(8):
            g = cc8 // 4
            cti = g * 4 + b
            stile = t["sT"][:, (h * 8 + cc8) * 128:(h * 8 + cc8 + 1) * 128]
            kcol = t["kpfb"][:, b * 16 + h * 2 + g:b * 16 + h * 2 + g + 1]
            pc = cc8 % 4
            zcol = (t["zcx"][:, 256:257] if pc == 0 else
                    t["zcx"][:, (cti * 8 + h) * 4 + pc:
                             (cti * 8 + h) * 4 + pc + 1])
            dbank = ps[2] if cc8 < 4 else ps[6]
            ctx.mm_group(dbank[0:1, (cc8 % 4) * 128:(cc8 % 4 + 1) * 128], [
                (kcol, qs[:, cc8 * 128:(cc8 + 1) * 128]),
                (zcol, qs[:, cc8 * 128:(cc8 + 1) * 128]),
                (t["ones_b"], stile)])
        for half, bank in ((0, ps[2]), (1, ps[6])):
            tmp = t["rows"][0:1, 0:512]
            ctx.act(tmp, bank[0:1, 0:512], AF.Abs)
            ctx.vts("tensor_scalar_max",
                    t["recd"][0:1, h * 1024 + half * 512:
                              h * 1024 + half * 512 + 512], tmp, 1.0)

    O_BANK = {0: 5, 1: 0}
    MS_BANK = {0: 4, 1: 2}
    ST_BANK = {0: 3, 1: 6}

    def attn_b_B2(b, hp):
        hs = (hp * 2, hp * 2 + 1)
        qsl, ktl, vtl, sol, osl = {}, {}, {}, {}, {}
        for s_, h in enumerate(hs):
            qsl[s_] = t["stq"][:, s_ * 1024:(s_ + 1) * 1024]
            ctx.dma("sync", qsl[s_], d["qb"][h * 128:(h + 1) * 128,
                                             b * 1024:(b + 1) * 1024])
            ktl[s_] = t["stkt"][:, s_ * 1024:(s_ + 1) * 1024]
            ctx.dma("scalar", ktl[s_].rearrange("p (pc e) -> p pc e", pc=8),
                    d["kbt"][b * 1024:(b + 1) * 1024,
                             h * 128:(h + 1) * 128].rearrange(
                        "(pc p) e -> p pc e", p=128))
            vtl[s_] = t["stvt"][:, s_ * 1024:(s_ + 1) * 1024]
            ctx.dma("scalar", vtl[s_].rearrange("p (pc e) -> p pc e", pc=8),
                    d["vbt"][b * 1024:(b + 1) * 1024,
                             h * 128:(h + 1) * 128].rearrange(
                        "(pc p) e -> p pc e", p=128))
            sol[s_] = t["sop"][:, s_ * 128:(s_ + 1) * 128]
            osl[s_] = t["osb"][:, s_ * 1024:(s_ + 1) * 1024]
        stm = {0: t["stmf"][:, 0:128], 1: t["stmf"][:, 128:256]}
        for cc8 in range(8):
            for s_, h in enumerate(hs):
                stb = ps[ST_BANK[s_]]
                ob = ps[O_BANK[s_]]
                msb = ps[MS_BANK[s_]]
                if cc8 % 4 == 0:
                    g = cc8 // 4
                    idx = h * 2 + g
                    ctx.vcopy(stm[s_],
                              t[f"spfx{b}"][:, idx * 128:(idx + 1) * 128])
                ctx.vcopy(sol[s_], stm[s_])
                ctx.mm_group(ob[:, 0:128], [
                    (sol[s_], qsl[s_][:, cc8 * 128:(cc8 + 1) * 128]),
                    (vtl[s_][:, cc8 * 128:(cc8 + 1) * 128],
                     t["sT"][:, (h * 8 + cc8) * 128:(h * 8 + cc8 + 1) * 128])])
                ctx.vcopy(osl[s_][:, cc8 * 128:(cc8 + 1) * 128], ob[:, 0:128])
                if cc8 % 4 != 3:
                    ctx.mm_group(stb[:, 0:128],
                                 [(ktl[s_][:, cc8 * 128:(cc8 + 1) * 128],
                                   vtl[s_][:, cc8 * 128:(cc8 + 1) * 128])])
                    ctx.vtt("tensor_add", stm[s_], stm[s_], stb[:, 0:128])

        for s_, h in enumerate(hs):
            msb = ps[MS_BANK[s_]]
            for g in range(2):
                sqs = t["sq"][:, ((s_ * 2 + g) % 2) * 512:
                              ((s_ * 2 + g) % 2) * 512 + 512]
                ctx.act(sqs, osl[s_][:, g * 512:(g + 1) * 512], AF.Square)
                ctx.mm_group(msb[0:1, 0:512], [(t["ones_b"], sqs)])
                ctx.vcopy(t["rows"][0:1, 2048 + s_ * 1024 + g * 512:
                                    2048 + s_ * 1024 + (g + 1) * 512],
                          msb[0:1, 0:512])
        for s_, h in enumerate(hs):
            dd = t["rows"][0:1, 0:1024]
            rd = t["recd"][0:1, h * 1024:(h + 1) * 1024]
            ctx.vtt("tensor_mul", dd, rd, rd)
            ctx.vts("tensor_scalar_mul", dd, dd, EPS)
            ctx.vstt(t["rows"][0:1, 1024:2048],
                     t["rows"][0:1, 2048 + s_ * 1024:2048 + (s_ + 1) * 1024],
                     1.0 / 128.0, dd, ALU.mult, ALU.add)
            al = t["rows"][0:1, 1024:2048]
            ctx.act(al, al, AF.Ln)
            ctx.act(al, al, AF.Exp, scale=-0.5)
            for g in range(2):
                bcb = ps[1] if g == 0 else ps[O_BANK[0]]
                ctx.mm_group(bcb, [(t["ones_r32"],
                                    al[0:1, g * 512:(g + 1) * 512])])
                ctx.vstt(t["onorm"][:, h * 1024 + g * 512:
                                    h * 1024 + (g + 1) * 512],
                         osl[s_][:, g * 512:(g + 1) * 512], t["normw_s"],
                         bcb, ALU.mult, ALU.mult)

    def emit_attn_b_project(b):
        for g in range(2):
            colbase = b * 1024 + g * 512

            def y_provider(dc, g=g):
                if dc == 0:
                    ctx.dma("sync", wslot(0, 1024), wst_fm(11, 0))
                if dc + 1 < NCH:
                    ctx.dma("sync", wslot(((dc + 1) % 2) * 1024, 1024),
                            wst_fm(11, dc + 1))
                bank = ps[dc % 2]
                ctx.mm_group(bank, [
                    (wslot((dc % 2) * 1024 + k_ * 128, 128),
                     t["onorm"][:, k_ * 1024 + g * 512: k_ * 1024 + g * 512 + 512])
                    for k_ in range(NCH)])
                return bank, None

            emit_epilogue_tile(colbase, y_provider, d["x2"], d["x3"],
                               d["ha"] if TAP != "x3" else None)

    def emit_tap_x(src_dram):
        for ct in range(CT):
            xo = t["xold"][:, (ct % 2) * COLS:(ct % 2) * COLS + COLS]
            ldtile("sync", xo, src_dram[:, ct * 512:(ct + 1) * 512])
            sttile("gpsimd", xo, d["y"][:, ct * 512:(ct + 1) * 512])

    def emit_final_wait():
        for sem, val in sorted(d["y"].res.w.items()):
            if not ctx.dry and ctx.cur == "gpsimd":
                ctx.eng.wait_ge(ctx.handles[sem], val)

    def emit_program():
        epi_seq[0] = 0
        emit_consts()
        emit_attn_a()
        if TAP == "dbg":
            sttile("gpsimd", t["xnew"], d["y"][:, 0:512])
            emit_final_wait()
            return
        if TAP == "x1":
            emit_tap_x(d["x1"])
            emit_final_wait()
            return
        emit_ffn(0)
        if TAP == "x2":
            emit_tap_x(d["x2"])
            emit_final_wait()
            return
        emit_attn_b()
        if TAP == "x3":
            if TAPD is not None:
                src = d[TAPD]
                rows, cols = src.ap.shape
                if (rows, cols) == (DM, COLS):
                    ydst = d["y"]
                else:
                    ydst = d["y"].rearrange("a (b c) -> (a b) c", b=4)
                ctx.dma("gpsimd", ydst, src)
                emit_final_wait()
                return
            emit_tap_x(d["x3"])
            emit_final_wait()
            return
        emit_ffn(1)
        emit_final_wait()

    ctx.dry = True
    emit_program()
    ctx.dry = False
    with nc.Block() as block:
        def run_for(name):
            def _(eng):
                ctx.cur = name
                ctx.eng = eng
                ctx.pid = None
                ctx.counts = {k: 0 for k in SEMS}
                ctx.wm = {e: {s: 0 for s in SEMS} for e in ENGINES}
                ctx.dma_idx = {q: 0 for q in DMA_PREFIX}
                for r_ in ctx.registry:
                    r_.w, r_.r = {}, {}
                emit_program()
            return _
        block.tensor(run_for("tensor"))
        block.vector(run_for("vector"))
        block.scalar(run_for("scalar"))
        block.sync(run_for("sync"))
        block.gpsimd(run_for("gpsimd"))

    for g in reversed(mgrs):
        g.__exit__(None, None, None)
    return nc


def stage_inputs(inp):
    x0 = np.asarray(inp["L0_attn_a_x"])
    xT = np.transpose(x0, (2, 0, 1))
    tr = lambda k: np.ascontiguousarray(np.asarray(inp[k]).T)
    ar = lambda k: np.ascontiguousarray(np.asarray(inp[k]))
    shared = {
        "wqkvT": tr("L0_attn_a_w_qkv"), "woutT": tr("L0_attn_a_w_out"),
        "b_out": ar("L0_attn_a_b_out"),
        "wqT": tr("L0_attn_b_Wq"), "wkT": tr("L0_attn_b_Wk"),
        "wvT": tr("L0_attn_b_Wv"), "woT": tr("L0_attn_b_Wo"),
        "normw": ar("L0_attn_b_norm_w"),
        "wfT": tr("L0_ffn_wf"), "ufT": tr("L0_ffn_uf"),
        "wiT": tr("L0_ffn_wi"), "uiT": tr("L0_ffn_ui"),
        "wcT": tr("L0_ffn_wc"), "ucT": tr("L0_ffn_uc"),
        "wobT": tr("L0_ffn_wo"), "uoT": tr("L0_ffn_uo"),
        "bfv": ar("L0_ffn_bf"), "biv": ar("L0_ffn_bi"), "bcv": ar("L0_ffn_bc"),
        "bov": ar("L0_ffn_bo"), "incv": ar("L0_ffn_in_c"),
    }
    in_maps = []
    for c in range(NCORE):
        b0, b1 = core_blocks(c)
        xc = np.concatenate([xT[:, :, b0 * 512:(b0 + 1) * 512],
                             xT[:, :, b1 * 512:(b1 + 1) * 512]], axis=2)
        m = dict(shared)
        m["x0"] = np.ascontiguousarray(xc.reshape(DM, COLS))
        in_maps.append(m)
    return in_maps


def unstage_output(results):
    out = np.empty((DM, B, S), np.float32)
    for c in range(NCORE):
        b0, b1 = core_blocks(c)
        yc = results[c]["y"].reshape(DM, B, 1024)
        out[:, :, b0 * 512:(b0 + 1) * 512] = yc[:, :, 0:512]
        out[:, :, b1 * 512:(b1 + 1) * 512] = yc[:, :, 512:1024]
    return np.ascontiguousarray(np.transpose(out, (1, 2, 0)))


def kernel(**inp):
    in_maps = stage_inputs(inp)
    nc = build_program()
    res = run_bass_kernel_spmd(nc, in_maps, core_ids=list(range(NCORE)))
    return unstage_output(res.results)
